# version 01
# baseline (speedup 1.0000x reference)
import numpy as np
import contextlib
import concourse.bass as bass
import concourse.mybir as mybir
from concourse.alu_op_type import AluOpType
from concourse.bass_utils import run_bass_kernel_spmd

F32 = mybir.dt.float32
BF16 = mybir.dt.bfloat16
I32 = mybir.dt.int32
AF = mybir.ActivationFunctionType
AX = mybir.AxisListType
OP = AluOpType

NCORES = 8
B = 2
S = 8192
D = 1024
TPC = 2048
NQT = 16
DC = D // 128

def core_qtiles(core):
    cc = core % 4
    return [4 * k + cc for k in range(8)] + [63 - 4 * k - cc for k in range(8)]


class Em:
    ENG = ("tensor", "vector", "scalar", "gpsimd", "sync")

    def __init__(self, name):
        self.nc = bass.Bass("TRN2", target_bir_lowering=False)
        self.stack = contextlib.ExitStack()
        self.ops = []
        self.counts = {}
        self.last = None
        self.n_sb = 0

    def inp(self, name, shape, dtype):
        return self.nc.dram_tensor(name, list(shape), dtype, kind="ExternalInput").ap()

    def out(self, name, shape, dtype):
        return self.nc.dram_tensor(name, list(shape), dtype, kind="ExternalOutput").ap()

    def dram(self, name, shape, dtype):
        return self.nc.dram_tensor(name, list(shape), dtype, kind="Internal").ap()

    def sb(self, shape, dtype, name=None):
        self.n_sb += 1
        return self.stack.enter_context(
            self.nc.sbuf_tensor(name or f"sb{self.n_sb}", list(shape), dtype))

    def ps(self, shape, dtype=F32, name=None):
        self.n_sb += 1
        return self.stack.enter_context(
            self.nc.psum_tensor(name or f"ps{self.n_sb}", list(shape), dtype))

    def sbc(self, name, shape, dtype):
        if not hasattr(self, "_cache"):
            self._cache = {}
        if name not in self._cache:
            self._cache[name] = self.sb(shape, dtype, name=name)
        return self._cache[name]

    def _rec(self, engine, kind, fn, deps):
        key = engine + ("_dma" if kind == "dma" else "")
        inc = 16 if kind == "dma" else 1
        self.counts[key] = self.counts.get(key, 0) + inc
        if deps is None:
            deps = [self.last] if self.last is not None else []
        deps = [d for d in deps if d is not None]
        op = dict(engine=engine, kind=kind, fn=fn, key=key,
                  cum=self.counts[key], deps=deps)
        self.ops.append(op)
        self.last = op
        return op

    def dma(self, dst, src, deps=None, eng="sync"):
        return self._rec(eng, "dma", lambda e: e.dma_start(out=dst, in_=src), deps)

    def mm(self, out, lhsT, rhs, start=True, stop=True, deps=None):
        return self._rec("tensor", "op",
                         lambda e: e.matmul(out=out, lhsT=lhsT, rhs=rhs, start=start,
                                            stop=stop, skip_group_check=True), deps)

    def transpose(self, out, in_, ident, deps=None):
        return self._rec("tensor", "op",
                         lambda e: e.transpose(out, in_, ident), deps)

    def act(self, out, in_, func, scale=1.0, bias=None, deps=None, accum_out=None):
        kw = {}
        if accum_out is not None:
            kw["accum_out"] = accum_out
        if bias is not None:
            assert not isinstance(bias, float), "float act bias needs const AP; use an AP"
            kw["bias"] = bias
        return self._rec("scalar", "op",
                         lambda e: e.activation(out=out, in_=in_, func=func,
                                                scale=scale, **kw), deps)

    def tt(self, out, a, b, op, deps=None, eng="vector"):
        return self._rec(eng, "op",
                         lambda e: e.tensor_tensor(out=out, in0=a, in1=b, op=op), deps)

    def ts(self, out, in0, s1, op0, s2=None, op1=None, deps=None, eng="vector"):
        if op1 is None:
            return self._rec(eng, "op",
                             lambda e: e.tensor_scalar(out=out, in0=in0, scalar1=s1,
                                                       scalar2=None, op0=op0), deps)
        return self._rec(eng, "op",
                         lambda e: e.tensor_scalar(out=out, in0=in0, scalar1=s1,
                                                   scalar2=s2, op0=op0, op1=op1), deps)

    def stt(self, out, in0, scalar, in1, op0, op1, deps=None):
        return self._rec("vector", "op",
                         lambda e: e.scalar_tensor_tensor(out=out, in0=in0, scalar=scalar,
                                                          in1=in1, op0=op0, op1=op1), deps)

    def red(self, out, in_, op, deps=None, negate=None):
        return self._rec("vector", "op",
                         lambda e: e.tensor_reduce(out=out, in_=in_, axis=AX.X, op=op,
                                                   negate=negate), deps)

    def recip(self, out, in_, deps=None):
        return self._rec("vector", "op", lambda e: e.reciprocal(out=out, in_=in_), deps)

    def copy(self, out, in_, deps=None, eng="vector"):
        return self._rec(eng, "op", lambda e: e.tensor_copy(out=out, in_=in_), deps)

    def memset(self, ap, v, deps=None, eng="vector"):
        return self._rec(eng, "op", lambda e: e.memset(ap, float(v)), deps)

    def coll(self, kind, op, ins, outs, replica_groups, deps=None):
        return self._rec("gpsimd", "dma",
                         lambda e: e.collective_compute(kind, op,
                                                        replica_groups=replica_groups,
                                                        ins=ins, outs=outs), deps)

    def emit(self):
        nc = self.nc
        sems = {}
        for k in self.counts:
            sems[k] = self.stack.enter_context(nc.semaphore(f"sem_{k}"))
        byeng = {e: [o for o in self.ops if o["engine"] == e] for e in self.ENG}
        finals = dict(self.counts)

        def run_stream(eng_obj, eng_name):
            water = {}
            for op in byeng[eng_name]:
                for d in op["deps"]:
                    if (d["key"] == op["key"] and eng_name == "tensor"
                            and d["kind"] == "op" and op["kind"] == "op"):
                        continue
                    if water.get(d["key"], -1) >= d["cum"]:
                        continue
                    eng_obj.wait_ge(sems[d["key"]], d["cum"])
                    water[d["key"]] = d["cum"]
                inst = op["fn"](eng_obj)
                inst.then_inc(sems[op["key"]], 16 if op["kind"] == "dma" else 1)
            dk = eng_name + "_dma"
            if dk in finals:
                eng_obj.wait_ge(sems[dk], finals[dk])

        with nc.Block() as block:
            @block.tensor
            def _(e):
                run_stream(e, "tensor")

            @block.vector
            def _(e):
                run_stream(e, "vector")

            @block.scalar
            def _(e):
                run_stream(e, "scalar")

            @block.gpsimd
            def _(e):
                run_stream(e, "gpsimd")

            @block.sync
            def _(e):
                run_stream(e, "sync")
        self.stack.close()
        return nc



def ln_rows(em, xcol, nT, ones_f, scratch_sq, psum, psq, eps=1e-5):
    for c in range(DC):
        em.mm(psum[:1, :], ones_f[:, :1], xcol[:, c, :], start=(c == 0), stop=(c == DC - 1))
    for c in range(DC):
        em.act(scratch_sq[:, c, :], xcol[:, c, :], AF.Square)
    for c in range(DC):
        em.mm(psq[:1, :], ones_f[:, :1], scratch_sq[:, c, :], start=(c == 0), stop=(c == DC - 1))
    m = em.sbc("lnr_m", [1, nT], F32)
    var = em.sbc("lnr_v", [1, nT], F32)
    rstd = em.sbc("lnr_r", [1, nT], F32)
    negmr = em.sbc("lnr_n", [1, nT], F32)
    em.ts(m[:1, :], psum[:1, :], 1.0 / D, OP.mult)
    m2 = em.sbc("lnr_m2", [1, nT], F32)
    em.tt(m2[:1, :], m[:1, :], m[:1, :], OP.mult)
    em.ts(m2[:1, :], m2[:1, :], float(eps), OP.subtract)
    em.stt(var[:1, :], psq[:1, :], 1.0 / D, m2[:1, :], OP.mult, OP.subtract)
    sq = em.sbc("lnr_sq", [1, nT], F32)
    em.act(sq[:1, :], var[:1, :], AF.Sqrt)
    em.recip(rstd[:1, :], sq[:1, :])
    em.tt(negmr[:1, :], m[:1, :], rstd[:1, :], OP.mult)
    em.ts(negmr[:1, :], negmr[:1, :], -1.0, OP.mult)
    return rstd, negmr


def bcast_row(em, row, nP, nT, ones_f, out_sb, pbank):
    em.mm(pbank[:nP, :nT], ones_f[:1, :nP], row[:1, :nT])
    em.copy(out_sb[:nP, :nT], pbank[:nP, :nT])


def apply_ln(em, xcol, out_bf, rstd_b, negmr_b, nT, scratch, scale_ap=None, bias_ap=None):
    for c in range(DC):
        em.tt(scratch[:, :nT], xcol[:, c, :], rstd_b[:, :nT], OP.mult)
        if scale_ap is None:
            em.tt(out_bf[:, c, :], scratch[:, :nT], negmr_b[:, :nT], OP.add)
        else:
            em.tt(scratch[:, :nT], scratch[:, :nT], negmr_b[:, :nT], OP.add)
            em.ts(out_bf[:, c, :], scratch[:, :nT], scale_ap(c), OP.mult,
                  bias_ap(c), OP.add)


H_FFN = 3072
NHT = H_FFN // 128


def build_ffn():
    em = Em("ffn")
    xs = em.inp("xs", [D, TPC], F32)
    wgT = em.inp("wgT", [D, H_FFN], I32)
    wuT = em.inp("wuT", [D, H_FFN], I32)
    wdT = em.inp("wdT", [H_FFN, D], I32)
    yT = em.out("yT", [D, TPC], F32)

    xs_r = xs.rearrange("(c p) t -> c p t", p=128)
    yT_r = yT.rearrange("(c p) t -> c p t", p=128)
    wgT_r = wgT.rearrange("(c p) h -> c p h", p=128)
    wuT_r = wuT.rearrange("(c p) h -> c p h", p=128)
    wdT_r = wdT.rearrange("(c p) o -> c p o", p=128)

    ones_f = em.sb([128, 128], F32)
    em.memset(ones_f[:, :], 1.0)

    nT = 512
    xcol = em.sb([128, DC, nT], F32)
    xsq = em.sb([128, DC, nT], F32)
    xln = em.sb([128, DC, nT], BF16)
    rb = em.sb([128, nT], F32)
    nb = em.sb([128, nT], F32)
    wg_i = em.sb([128, DC, 128], I32)
    wg_b = em.sb([128, DC, 128], BF16)
    wu_i = em.sb([128, DC, 128], I32)
    wu_b = em.sb([128, DC, 128], BF16)
    wd_i = em.sb([128, NHT, 128], I32)
    wd_b = em.sb([128, NHT, 128], BF16)
    sg = em.sb([128, nT], BF16)
    su = em.sb([128, nT], F32)
    hsb = em.sb([128, NHT, nT], BF16)
    xres = em.sb([128, nT], F32)
    yout = em.sb([128, nT], F32)

    p_s1 = em.ps([1, nT])
    p_s2 = em.ps([1, nT])
    pb = em.ps([128, nT])
    pg = em.ps([128, nT])
    pu = em.ps([128, nT])
    pd = em.ps([128, nT])

    for ti in range(TPC // nT):
        tsl = slice(ti * nT, (ti + 1) * nT)
        for c in range(DC):
            em.dma(xcol[:, c, :], xs_r[c, :, tsl])
        rstd, negmr = ln_rows(em, xcol, nT, ones_f, xsq, p_s1, p_s2)
        bcast_row(em, rstd, 128, nT, ones_f, rb, pb)
        bcast_row(em, negmr, 128, nT, ones_f, nb, pb)
        apply_ln(em, xcol, xln, rb, nb, nT, xres)
        for ht in range(NHT):
            hsl = slice(ht * 128, (ht + 1) * 128)
            for c in range(DC):
                em.dma(wg_i[:, c, :], wgT_r[c, :, hsl])
                em.dma(wu_i[:, c, :], wuT_r[c, :, hsl])
            em.copy(wg_b[:, :, :], wg_i[:, :, :], eng="gpsimd")
            em.copy(wu_b[:, :, :], wu_i[:, :, :], eng="gpsimd")
            for c in range(DC):
                em.mm(pg[:, :], wg_b[:, c, :], xln[:, c, :], start=(c == 0), stop=(c == DC - 1))
            for c in range(DC):
                em.mm(pu[:, :], wu_b[:, c, :], xln[:, c, :], start=(c == 0), stop=(c == DC - 1))
            em.act(sg[:, :], pg[:, :], AF.Silu, scale=1.0 / 127.0)
            em.ts(su[:, :], pu[:, :], 1.0 / 127.0, OP.mult)
            em.tt(hsb[:, ht, :], sg[:, :], su[:, :], OP.mult)
        for oc in range(DC):
            osl = slice(oc * 128, (oc + 1) * 128)
            for ht in range(NHT):
                em.dma(wd_i[:, ht, :], wdT_r[ht, :, osl])
            em.copy(wd_b[:, :, :], wd_i[:, :, :], eng="gpsimd")
            for ht in range(NHT):
                em.mm(pd[:, :], wd_b[:, ht, :], hsb[:, ht, :], start=(ht == 0), stop=(ht == NHT - 1))
            em.dma(xres[:, :], xs_r[oc, :, tsl])
            em.stt(yout[:, :], pd[:, :], 1.0 / 127.0, xres[:, :], OP.mult, OP.add)
            em.dma(yT_r[oc, :, tsl], yout[:, :])
    return em.emit()


NE = 8
NH = 16
HD = 64
LAT = 256


def build_moe():
    em = Em("moe")
    xs = em.inp("xs", [D, TPC], F32)
    rw = em.inp("rw", [D, NE], F32)
    win = em.inp("win", [NE, D, D], F32)
    wout = em.inp("wout", [NE, D, D], F32)
    binP = em.inp("binP", [128, DC, NE], F32)
    boutT = em.inp("boutT", [1, NE, D], F32)
    ident = em.inp("ident", [128, 128], F32)
    yT = em.out("yT", [D, TPC], F32)

    xs_r = xs.rearrange("(c p) t -> c p t", p=128)
    yT_r = yT.rearrange("(c p) t -> c p t", p=128)
    rw_r = rw.rearrange("(c p) e -> c p e", p=128)
    win_r = win.rearrange("e (c p) h -> e c p h", p=128)
    wout_r = wout.rearrange("e (c p) o -> e c p o", p=128)

    ones_f = em.sb([128, 128], F32)
    em.memset(ones_f[:, :], 1.0)
    id_sb = em.sb([128, 128], F32)
    em.dma(id_sb[:, :], ident)
    rw_f = em.sb([128, DC, NE], F32)
    for c in range(DC):
        em.dma(rw_f[:, c, :], rw_r[c])
    rw_sb = em.sb([128, DC, NE], BF16)
    em.copy(rw_sb[:, :, :], rw_f[:, :, :])
    binP_sb = em.sb([128, DC, NE], F32)
    em.dma(binP_sb[:, :, :], binP)
    boutT_sb = em.sb([1, NE, D], F32)
    em.dma(boutT_sb[:, :, :], boutT)

    nT = 512
    xcol = em.sb([128, DC, nT], F32)
    xsq = em.sb([128, DC, nT], F32)
    xln = em.sb([128, DC, nT], BF16)
    rb = em.sb([128, nT], F32)
    nb = em.sb([128, nT], F32)
    scr = em.sb([128, nT], F32)
    lg_sb = em.sb([8, nT], F32)
    ltok = em.sb([128, 4, 8], F32)
    ptok = em.sb([128, 4, 8], F32)
    wtok = em.sb([128, 4, 8], F32)
    tmp8 = em.sb([128, 8], F32)
    col1 = em.sb([128, 1], F32)
    wT_sb = em.sb([8, nT], F32)
    we_row = em.sb([1, nT], F32)
    web = em.sb([128, nT], F32)
    wf32 = em.sb([128, DC, D], F32)
    win_b = em.sb([128, DC, D], BF16)
    wout_b = em.sb([128, DC, D], BF16)
    hg = em.sb([128, nT], F32)
    hs = em.sb([128, DC, nT], BF16)
    yacc = em.sb([128, DC, nT], F32)
    xres = em.sb([128, nT], F32)
    yout = em.sb([128, nT], F32)

    p_s1 = em.ps([1, nT])
    p_s2 = em.ps([1, nT])
    pb = em.ps([128, nT])
    pr = em.ps([8, nT])
    ph = em.ps([128, nT])
    py = em.ps([128, nT])

    for ti in range(TPC // nT):
        tsl = slice(ti * nT, (ti + 1) * nT)
        for c in range(DC):
            em.dma(xcol[:, c, :], xs_r[c, :, tsl])
        rstd, negmr = ln_rows(em, xcol, nT, ones_f, xsq, p_s1, p_s2)
        bcast_row(em, rstd, 128, nT, ones_f, rb, pb)
        bcast_row(em, negmr, 128, nT, ones_f, nb, pb)
        apply_ln(em, xcol, xln, rb, nb, nT, scr)
        for c in range(DC):
            em.mm(pr[:8, :], rw_sb[:, c, :], xln[:, c, :], start=(c == 0), stop=(c == DC - 1))
        em.copy(lg_sb[:8, :], pr[:8, :])
        for q in range(nT // 128):
            qsl = slice(q * 128, (q + 1) * 128)
            em.transpose(pb[:, 0:8], lg_sb[0:8, qsl], id_sb[0:8, 0:8])
            em.copy(ltok[:, q, :], pb[:, 0:8])
            em.red(col1[:, :], ltok[:, q, :], OP.max)
            em.ts(tmp8[:, :], ltok[:, q, :], col1[:, 0:1], OP.subtract)
            em.act(tmp8[:, :], tmp8[:, :], AF.Exp)
            em.red(col1[:, :], tmp8[:, :], OP.add)
            em.recip(col1[:, :], col1[:, :])
            em.ts(ptok[:, q, :], tmp8[:, :], col1[:, 0:1], OP.mult)
            em.red(col1[:, :], ptok[:, q, :], OP.max)
            em.ts(tmp8[:, :], ptok[:, q, :], col1[:, 0:1], OP.is_ge)
            em.stt(tmp8[:, :], tmp8[:, :], -1e30, ptok[:, q, :], OP.mult, OP.add)
            em.red(col1[:, :], tmp8[:, :], OP.max)
            em.ts(tmp8[:, :], ptok[:, q, :], col1[:, 0:1], OP.is_ge)
            em.tt(wtok[:, q, :], ptok[:, q, :], tmp8[:, :], OP.mult)
            em.transpose(pr[:8, qsl], wtok[:, q, :], id_sb[0:128, 0:128])
        em.copy(wT_sb[:8, :], pr[:8, :])
        for e in range(NE):
            em.mm(p_s1[:1, :], id_sb[0:8, e:e + 1], wT_sb[0:8, :])
            em.copy(we_row[:1, :], p_s1[:1, :])
            em.mm(pb[:, :], ones_f[0:1, 0:128], we_row[:1, :])
            em.copy(web[:, :], pb[:, :])
            for c in range(DC):
                em.dma(wf32[:, c, :], win_r[e, c])
            em.copy(win_b[:, :, :], wf32[:, :, :], eng="gpsimd")
            for hc in range(DC):
                hsl = slice(hc * 128, (hc + 1) * 128)
                for c in range(DC):
                    em.mm(ph[:, :], win_b[:, c, hsl], xln[:, c, :], start=(c == 0), stop=(c == DC - 1))
                em.act(hg[:, :], ph[:, :], AF.Gelu_apprx_tanh, bias=binP_sb[:, hc, e:e + 1])
                em.tt(hs[:, hc, :], hg[:, :], web[:, :], OP.mult)
            for c in range(DC):
                em.dma(wf32[:, c, :], wout_r[e, c])
            em.copy(wout_b[:, :, :], wf32[:, :, :], eng="gpsimd")
            for oc in range(DC):
                osl = slice(oc * 128, (oc + 1) * 128)
                for hc in range(DC):
                    em.mm(py[:, :], wout_b[:, hc, osl], hs[:, hc, :], start=(hc == 0), stop=False)
                em.mm(py[:, :], boutT_sb[0:1, e, osl], we_row[:1, :], start=False, stop=True)
                if e == 0:
                    em.copy(yacc[:, oc, :], py[:, :])
                else:
                    em.tt(yacc[:, oc, :], yacc[:, oc, :], py[:, :], OP.add)
        for oc in range(DC):
            em.dma(xres[:, :], xs_r[oc, :, tsl])
            em.tt(yout[:, :], yacc[:, oc, :], xres[:, :], OP.add)
            em.dma(yT_r[oc, :, tsl], yout[:, :])
    return em.emit()


NBT = 64


def build_bank():
    em = Em("bank")
    xs = em.inp("xs", [D, TPC], F32)
    kT = em.inp("kT", [D, S], F32)
    v = em.inp("v", [S, D], F32)
    yT = em.out("yT", [D, TPC], F32)

    xs_r = xs.rearrange("(c p) t -> c p t", p=128)
    yT_r = yT.rearrange("(c p) t -> c p t", p=128)
    kT_r = kT.rearrange("(c p) s -> c p s", p=128)
    v_r = v.rearrange("(st p) o -> st p o", p=128)

    kTb2 = em.dram("kTb2", [NBT, DC, 128, 128], BF16)
    vb = em.dram("vb", [NBT, 128, D], BF16)
    kTb2_rd = kTb2.rearrange("n c p m -> n p c m")

    ones_f = em.sb([128, 128], F32)
    em.memset(ones_f[:, :], 1.0)
    ones_b = em.sb([128, 1], BF16)
    em.memset(ones_b[:, :], 1.0)

    cast_f = em.sb([128, 2048], F32)
    cast_b = em.sb([128, 2048], BF16)
    for c in range(DC):
        for j in range(4):
            ssl = slice(j * 2048, (j + 1) * 2048)
            em.dma(cast_f[:, :], kT_r[c][:, ssl])
            em.copy(cast_b[:, :], cast_f[:, :])
            for q in range(16):
                nt = j * 16 + q
                em.dma(kTb2[nt, c], cast_b[:, q * 128:(q + 1) * 128])
    for st in range(NBT):
        em.dma(cast_f[:, :1024], v_r[st])
        em.copy(cast_b[:, :1024], cast_f[:, :1024])
        em.dma(vb[st], cast_b[:, :1024])

    nT = 512
    xcol = em.sb([128, DC, nT], F32)
    xsq = em.sb([128, DC, nT], F32)
    rb = em.sb([128, nT], F32)
    nb = em.sb([128, nT], F32)
    scr = em.sb([128, nT], F32)
    xlnb = em.sb([128, DC, TPC], BF16)
    p_s1 = em.ps([1, nT])
    p_s2 = em.ps([1, nT])
    pbB = em.ps([128, nT])
    for ti in range(TPC // nT):
        tsl = slice(ti * nT, (ti + 1) * nT)
        for c in range(DC):
            em.dma(xcol[:, c, :], xs_r[c, :, tsl])
        rstd, negmr = ln_rows(em, xcol, nT, ones_f, xsq, p_s1, p_s2)
        bcast_row(em, rstd, 128, nT, ones_f, rb, pbB)
        bcast_row(em, negmr, 128, nT, ones_f, nb, pbB)
        apply_ln(em, xcol, xlnb[:, :, tsl], rb, nb, nT, scr)

    nG = 256
    kt_sb = em.sb([128, DC, 128], BF16)
    sc_sb = em.sb([128, NBT, nG], F32)
    sqt = em.sb([128, nG], BF16)
    gt = em.sb([128, nG], BF16)
    vt_sb = em.sb([128, D], BF16)
    srow = em.sb([1, nG], F32)
    cb = em.sb([128, nG], F32)
    g1 = em.sb([128, nG], F32)
    yout = em.sb([128, nG], F32)
    xres = em.sb([128, nG], F32)
    psC = em.ps([128, nG])
    py4 = [em.ps([128, nG]) for _ in range(4)]

    SQRT_N = float(np.sqrt(S))
    for tg in range(TPC // nG):
        gsl = slice(tg * nG, (tg + 1) * nG)
        for nt in range(NBT):
            em.dma(kt_sb[:, :, :], kTb2_rd[nt])
            for c in range(DC):
                em.mm(psC[:, :], kt_sb[:, c, :], xlnb[:, c, gsl], start=(c == 0), stop=(c == DC - 1))
            em.copy(sc_sb[:, nt, :], psC[:, :])
            em.act(sqt[:, :], psC[:, :], AF.Square)
            em.mm(p_s1[:1, :nG], ones_b[:, 0:1], sqt[:, :], start=(nt == 0), stop=(nt == NBT - 1))
        em.copy(srow[:1, :], p_s1[:1, :nG])
        em.act(srow[:1, :], srow[:1, :], AF.Sqrt)
        em.recip(srow[:1, :], srow[:1, :])
        em.ts(srow[:1, :], srow[:1, :], SQRT_N, OP.mult)
        em.mm(pbB[:, :nG], ones_f[0:1, 0:128], srow[:1, :])
        em.copy(cb[:, :], pbB[:, :nG])
        for half in range(2):
            for nt in range(NBT):
                em.dma(vt_sb[:, :], vb[nt])
                em.tt(g1[:, :], sc_sb[:, nt, :], cb[:, :], OP.mult)
                em.act(gt[:, :], g1[:, :], AF.Gelu)
                for j in range(4):
                    oc = half * 4 + j
                    em.mm(py4[j][:, :], vt_sb[:, oc * 128:(oc + 1) * 128], gt[:, :],
                          start=(nt == 0), stop=(nt == NBT - 1))
            for j in range(4):
                oc = half * 4 + j
                em.dma(xres[:, :], xs_r[oc, :, gsl])
                em.tt(yout[:, :], py4[j][:, :], xres[:, :], OP.add)
                em.dma(yT_r[oc, :, gsl], yout[:, :])
    return em.emit()


F_LAB = 4096
NFT = F_LAB // 128


def build_lab(variant):
    em = Em("lab_" + variant)
    xc = em.inp("xc", [D, S], F32)
    xs = em.inp("xs", [D, TPC], F32)
    wqT = em.inp("wqT", [D, D], F32)
    wkT = em.inp("wkT", [D, D], F32)
    wvT = em.inp("wvT", [D, D], F32)
    wo = em.inp("wo", [D, D], F32)
    w1 = em.inp("w1", [D, F_LAB], F32)
    w2 = em.inp("w2", [F_LAB, D], F32)
    bqP = em.inp("bqP", [128, DC], F32)
    bkP = em.inp("bkP", [128, DC], F32)
    bvP = em.inp("bvP", [128, DC], F32)
    boP = em.inp("boP", [128, DC], F32)
    b1P = em.inp("b1P", [128, NFT], F32)
    b2P = em.inp("b2P", [128, DC], F32)
    ln1sP = em.inp("ln1sP", [128, DC], F32)
    ln1bP = em.inp("ln1bP", [128, DC], F32)
    ln2sP = em.inp("ln2sP", [128, DC], F32)
    ln2bP = em.inp("ln2bP", [128, DC], F32)
    if variant == "last":
        xtop = em.inp("xtop", [D, TPC], F32)
        xtop_r = xtop.rearrange("(c p) t -> c p t", p=128)
    yT = em.out("yT", [D, TPC], F32)

    xc_r = xc.rearrange("(c p) s -> c p s", p=128)
    xs_r = xs.rearrange("(c p) t -> c p t", p=128)
    yT_r = yT.rearrange("(c p) t -> c p t", p=128)
    wqT_r = wqT.rearrange("(c p) o -> c p o", p=128)
    wkT_r = wkT.rearrange("(c p) o -> c p o", p=128)
    wvT_r = wvT.rearrange("(c p) o -> c p o", p=128)
    wo_r = wo.rearrange("(c p) o -> c p o", p=128)
    w1_r = w1.rearrange("(c p) f -> c p f", p=128)
    w2_r = w2.rearrange("(c p) o -> c p o", p=128)

    ones_f = em.sb([128, 128], F32)
    em.memset(ones_f[:, :], 1.0)

    nT = 512
    xcol = em.sb([128, DC, nT], F32)
    qf = em.sb([128, DC, nT], F32)
    xb = em.sb([128, DC, nT], BF16)
    rbb = em.sb([128, nT], F32)
    nbb = em.sb([128, nT], F32)
    scr = em.sb([128, nT], F32)
    t1 = em.sb([128, nT], F32)
    e1 = em.sb([128, nT], F32)
    t3 = em.sb([128, nT], F32)
    kf = em.sb([128, nT], F32)
    vv = em.sb([128, nT], F32)
    red1 = em.sb([128, 1], F32)
    KVacc = em.sb([128, DC], F32)
    SKacc = em.sb([128, DC], F32)
    wA_b = em.sb([128, DC, D], BF16)
    wB_b = em.sb([128, DC, D], BF16)

    bq_sb = em.sb([128, DC], F32)
    em.dma(bq_sb[:, :], bqP)
    bk_sb = em.sb([128, DC], F32)
    em.dma(bk_sb[:, :], bkP)
    bv_sb = em.sb([128, DC], F32)
    em.dma(bv_sb[:, :], bvP)
    bo_sb = em.sb([128, DC], F32)
    em.dma(bo_sb[:, :], boP)
    b1_sb = em.sb([128, NFT], F32)
    em.dma(b1_sb[:, :], b1P)
    b2_sb = em.sb([128, DC], F32)
    em.dma(b2_sb[:, :], b2P)
    l1s = em.sb([128, DC], F32)
    em.dma(l1s[:, :], ln1sP)
    l1b = em.sb([128, DC], F32)
    em.dma(l1b[:, :], ln1bP)
    l2s = em.sb([128, DC], F32)
    em.dma(l2s[:, :], ln2sP)
    l2b = em.sb([128, DC], F32)
    em.dma(l2b[:, :], ln2bP)

    p_s1 = em.ps([1, nT])
    p_s2 = em.ps([1, nT])
    pbb = em.ps([128, nT])
    psK = em.ps([128, nT])
    psV = em.ps([128, nT])

    def load_w(dst_b, src_r):
        for c in range(DC):
            em.dma(xcol[:, c, :], src_r[c][:, 0:512])
        em.copy(dst_b[:, :, 0:512], xcol[:, :, :], eng="gpsimd")
        for c in range(DC):
            em.dma(xcol[:, c, :], src_r[c][:, 512:])
        em.copy(dst_b[:, :, 512:], xcol[:, :, :], eng="gpsimd")

    load_w(wA_b, wkT_r)
    load_w(wB_b, wvT_r)

    for si in range(S // nT):
        ssl = slice(si * nT, (si + 1) * nT)
        for c in range(DC):
            em.dma(xcol[:, c, :], xc_r[c, :, ssl])
        if variant == "first":
            rstd, negmr = ln_rows(em, xcol, nT, ones_f, qf, p_s1, p_s2)
            bcast_row(em, rstd, 128, nT, ones_f, rbb, pbb)
            bcast_row(em, negmr, 128, nT, ones_f, nbb, pbb)
            apply_ln(em, xcol, xb, rbb, nbb, nT, scr)
        else:
            em.copy(xb[:, :, :], xcol[:, :, :])
        for ct in range(DC):
            csl = slice(ct * 128, (ct + 1) * 128)
            for c in range(DC):
                em.mm(psK[:, :], wA_b[:, c, csl], xb[:, c, :], start=(c == 0), stop=(c == DC - 1))
            for c in range(DC):
                em.mm(psV[:, :], wB_b[:, c, csl], xb[:, c, :], start=(c == 0), stop=(c == DC - 1))
            em.ts(t1[:, :], psK[:, :], bk_sb[:, ct:ct + 1], OP.add, 0.0, OP.min)
            em.act(e1[:, :], t1[:, :], AF.Exp)
            em.ts(t3[:, :], psK[:, :], bk_sb[:, ct:ct + 1], OP.add, 0.0, OP.max)
            em.tt(kf[:, :], e1[:, :], t3[:, :], OP.add)
            em.ts(vv[:, :], psV[:, :], bv_sb[:, ct:ct + 1], OP.add)
            em.tt(e1[:, :], kf[:, :], vv[:, :], OP.mult)
            em.red(red1[:, :], e1[:, :], OP.add)
            if si == 0:
                em.copy(KVacc[:, ct:ct + 1], red1[:, :])
            else:
                em.tt(KVacc[:, ct:ct + 1], KVacc[:, ct:ct + 1], red1[:, :], OP.add)
            em.red(red1[:, :], kf[:, :], OP.add)
            if si == 0:
                em.copy(SKacc[:, ct:ct + 1], red1[:, :])
            else:
                em.tt(SKacc[:, ct:ct + 1], SKacc[:, ct:ct + 1], red1[:, :], OP.add)

    load_w(wA_b, wqT_r)
    load_w(wB_b, wo_r)
    attb = em.sb([128, DC, nT], BF16)
    r1 = em.sb([128, DC, nT], F32)
    x1f = em.sb([128, DC, nT], F32)
    x1b = em.sb([128, DC, nT], BF16)
    wstr = em.sb([128, DC, 128], F32)
    wstr_b = em.sb([128, DC, 128], BF16)
    w2str_b = em.sb([128, NFT, 128], BF16)
    h1b = em.sb([128, NFT, nT], BF16)
    yo = em.sb([128, nT], F32)

    for ti in range(TPC // nT):
        tsl = slice(ti * nT, (ti + 1) * nT)
        for c in range(DC):
            em.dma(xcol[:, c, :], xs_r[c, :, tsl])
        if variant == "first":
            rstd, negmr = ln_rows(em, xcol, nT, ones_f, qf, p_s1, p_s2)
            bcast_row(em, rstd, 128, nT, ones_f, rbb, pbb)
            bcast_row(em, negmr, 128, nT, ones_f, nbb, pbb)
            for c in range(DC):
                em.tt(scr[:, :], xcol[:, c, :], rbb[:, :], OP.mult)
                em.tt(xcol[:, c, :], scr[:, :], nbb[:, :], OP.add)
        em.copy(xb[:, :, :], xcol[:, :, :])
        for ct in range(DC):
            csl = slice(ct * 128, (ct + 1) * 128)
            for c in range(DC):
                em.mm(psK[:, :], wA_b[:, c, csl], xb[:, c, :], start=(c == 0), stop=(c == DC - 1))
            em.ts(t1[:, :], psK[:, :], bq_sb[:, ct:ct + 1], OP.add, 0.0, OP.min)
            em.act(e1[:, :], t1[:, :], AF.Exp)
            em.ts(t3[:, :], psK[:, :], bq_sb[:, ct:ct + 1], OP.add, 0.0, OP.max)
            em.tt(qf[:, ct, :], e1[:, :], t3[:, :], OP.add)
        for ct in range(DC):
            em.ts(t1[:, :], qf[:, ct, :], SKacc[:, ct:ct + 1], OP.mult, 1e-6, OP.add)
            em.recip(t1[:, :], t1[:, :])
            em.ts(t3[:, :], qf[:, ct, :], KVacc[:, ct:ct + 1], OP.mult)
            em.tt(attb[:, ct, :], t3[:, :], t1[:, :], OP.mult)
        for oc in range(DC):
            osl = slice(oc * 128, (oc + 1) * 128)
            for ct in range(DC):
                em.mm(psV[:, :], wB_b[:, ct, osl], attb[:, ct, :], start=(ct == 0), stop=(ct == DC - 1))
            em.act(e1[:, :], psV[:, :], AF.Gelu_apprx_tanh, bias=bo_sb[:, oc:oc + 1])
            em.tt(r1[:, oc, :], e1[:, :], xcol[:, oc, :], OP.add)
        rstd, negmr = ln_rows(em, r1, nT, ones_f, qf, p_s1, p_s2, eps=1e-6)
        bcast_row(em, rstd, 128, nT, ones_f, rbb, pbb)
        bcast_row(em, negmr, 128, nT, ones_f, nbb, pbb)
        for c in range(DC):
            em.tt(scr[:, :], r1[:, c, :], rbb[:, :], OP.mult)
            em.tt(scr[:, :], scr[:, :], nbb[:, :], OP.add)
            em.ts(x1f[:, c, :], scr[:, :], l1s[:, c:c + 1], OP.mult, l1b[:, c:c + 1], OP.add)
        em.copy(x1b[:, :, :], x1f[:, :, :])
        for ft in range(NFT):
            fsl = slice(ft * 128, (ft + 1) * 128)
            for c in range(DC):
                em.dma(wstr[:, c, :], w1_r[c][:, fsl])
            em.copy(wstr_b[:, :, :], wstr[:, :, :], eng="gpsimd")
            for c in range(DC):
                em.mm(psK[:, :], wstr_b[:, c, :], x1b[:, c, :], start=(c == 0), stop=(c == DC - 1))
            em.act(h1b[:, ft, :], psK[:, :], AF.Gelu_apprx_tanh, bias=b1_sb[:, ft:ft + 1])
        for oc in range(DC):
            osl = slice(oc * 128, (oc + 1) * 128)
            for grp in range(4):
                for j in range(DC):
                    fc = grp * DC + j
                    em.dma(wstr[:, j, :], w2_r[fc][:, osl])
                em.copy(w2str_b[:, grp * DC:(grp + 1) * DC, :], wstr[:, :, :], eng="gpsimd")
            for fc in range(NFT):
                em.mm(psV[:, :], w2str_b[:, fc, :], h1b[:, fc, :], start=(fc == 0), stop=(fc == NFT - 1))
            em.act(e1[:, :], psV[:, :], AF.Gelu_apprx_tanh, bias=b2_sb[:, oc:oc + 1])
            em.tt(r1[:, oc, :], e1[:, :], x1f[:, oc, :], OP.add)
        rstd, negmr = ln_rows(em, r1, nT, ones_f, qf, p_s1, p_s2, eps=1e-6)
        bcast_row(em, rstd, 128, nT, ones_f, rbb, pbb)
        bcast_row(em, negmr, 128, nT, ones_f, nbb, pbb)
        for c in range(DC):
            em.tt(scr[:, :], r1[:, c, :], rbb[:, :], OP.mult)
            em.tt(scr[:, :], scr[:, :], nbb[:, :], OP.add)
            em.ts(yo[:, :], scr[:, :], l2s[:, c:c + 1], OP.mult, l2b[:, c:c + 1], OP.add)
            if variant == "last":
                em.dma(t1[:, :], xtop_r[c, :, tsl])
                em.tt(yo[:, :], yo[:, :], t1[:, :], OP.add)
            em.dma(yT_r[c, :, tsl], yo[:, :])
    return em.emit()



def build_mla():
    em = Em("mla")
    xc = em.inp("xc", [D, S], F32)
    xs = em.inp("xs", [D, TPC], F32)
    wq = em.inp("wq", [D, D], F32)
    wkb = em.inp("wkb", [D, LAT], F32)
    R8T = em.inp("R8T", [D, D], F32)
    RT = em.inp("RT", [D, D], F32)
    wkaT = em.inp("wkaT", [D, LAT], F32)
    wvaT = em.inp("wvaT", [D, LAT], F32)
    wvbT = em.inp("wvbT", [LAT, D], F32)
    woT = em.inp("woT", [D, D], F32)
    yT = em.out("yT", [D, TPC], F32)

    xc_r = xc.rearrange("(c p) s -> c p s", p=128)
    xs_r = xs.rearrange("(c p) t -> c p t", p=128)
    yT_r = yT.rearrange("(c p) t -> c p t", p=128)
    wq_r = wq.rearrange("(c p) d -> c p d", p=128)
    wkb_r = wkb.rearrange("(c p) l -> c p l", p=128)
    R8T_r = R8T.rearrange("(c p) o -> c p o", p=128)
    RT_r = RT.rearrange("(c p) o -> c p o", p=128)
    wkaT_r = wkaT.rearrange("(c p) l -> c p l", p=128)
    wvaT_r = wvaT.rearrange("(c p) l -> c p l", p=128)
    wvbT_r = wvbT.rearrange("(c p) o -> c p o", p=128)
    woT_r = woT.rearrange("(c p) o -> c p o", p=128)

    wqT_d = em.dram("wqT_d", [D, D], BF16)
    qT_d = em.dram("qT_d", [NH, HD, TPC], BF16)
    attT_d = em.dram("attT_d", [D, TPC], BF16)
    wqT_dr = wqT_d.rearrange("(c p) o -> c p o", p=128)

    ones_f = em.sb([128, 128], F32)
    em.memset(ones_f[:, :], 1.0)

    wtile = em.sb([128, 128], F32)
    rtile = em.sb([128, 512], F32)
    qev = em.sb([128, 512], BF16)
    wkbT_b = em.sb([128, 2, D], BF16)
    pK = em.ps([128, 512])
    for dt_ in range(DC):
        for cg in range(2):
            for c in range(DC):
                em.dma(wtile[:, :], wq_r[c][:, dt_ * 128:(dt_ + 1) * 128])
                em.dma(rtile[:, :], R8T_r[c][:, cg * 512:(cg + 1) * 512])
                em.mm(pK[:, :], wtile[:, :], rtile[:, :], start=(c == 0), stop=(c == DC - 1))
            em.copy(qev[:, :], pK[:, :])
            em.dma(wqT_dr[dt_][:, cg * 512:(cg + 1) * 512], qev[:, :])
    for lt in range(2):
        for cg in range(2):
            for c in range(DC):
                em.dma(wtile[:, :], wkb_r[c][:, lt * 128:(lt + 1) * 128])
                em.dma(rtile[:, :], RT_r[c][:, cg * 512:(cg + 1) * 512])
                em.mm(pK[:, :], wtile[:, :], rtile[:, :], start=(c == 0), stop=(c == DC - 1))
            em.copy(wkbT_b[:, lt, cg * 512:(cg + 1) * 512], pK[:, :])
    wka_b = em.sb([128, DC, LAT], BF16)
    wva_b = em.sb([128, DC, LAT], BF16)
    wvbT_b = em.sb([128, 2, D], BF16)
    for c in range(DC):
        em.dma(rtile[:, :LAT], wkaT_r[c])
        em.copy(wka_b[:, c, :], rtile[:, :LAT], eng="gpsimd")
        em.dma(rtile[:, :LAT], wvaT_r[c])
        em.copy(wva_b[:, c, :], rtile[:, :LAT], eng="gpsimd")
    for lt in range(2):
        em.dma(rtile[:, :], wvbT_r[lt][:, 0:512])
        em.copy(wvbT_b[:, lt, 0:512], rtile[:, :], eng="gpsimd")
        em.dma(rtile[:, :], wvbT_r[lt][:, 512:])
        em.copy(wvbT_b[:, lt, 512:], rtile[:, :], eng="gpsimd")

    nT = 512
    xcol = em.sb([128, DC, nT], F32)
    xsq = em.sb([128, DC, nT], F32)
    xln = em.sb([128, DC, nT], BF16)
    rb = em.sb([128, nT], F32)
    nb = em.sb([128, nT], F32)
    scr = em.sb([128, nT], F32)
    klatT = em.sb([128, 2, S], BF16)
    vlatT = em.sb([128, 2, S], BF16)
    p_s1 = em.ps([1, nT])
    p_s2 = em.ps([1, nT])
    pb = em.ps([128, nT])
    for si in range(S // nT):
        ssl = slice(si * nT, (si + 1) * nT)
        for c in range(DC):
            em.dma(xcol[:, c, :], xc_r[c, :, ssl])
        rstd, negmr = ln_rows(em, xcol, nT, ones_f, xsq, p_s1, p_s2)
        bcast_row(em, rstd, 128, nT, ones_f, rb, pb)
        bcast_row(em, negmr, 128, nT, ones_f, nb, pb)
        apply_ln(em, xcol, xln, rb, nb, nT, scr)
        for lt in range(2):
            lsl = slice(lt * 128, (lt + 1) * 128)
            for c in range(DC):
                em.mm(pK[:, :], wka_b[:, c, lsl], xln[:, c, :], start=(c == 0), stop=(c == DC - 1))
            em.copy(klatT[:, lt, ssl], pK[:, :])
            for c in range(DC):
                em.mm(pK[:, :], wva_b[:, c, lsl], xln[:, c, :], start=(c == 0), stop=(c == DC - 1))
            em.copy(vlatT[:, lt, ssl], pK[:, :])
    qwt_b = em.sb([128, DC, 128], BF16)
    for ti in range(TPC // nT):
        tsl = slice(ti * nT, (ti + 1) * nT)
        for c in range(DC):
            em.dma(xcol[:, c, :], xs_r[c, :, tsl])
        rstd, negmr = ln_rows(em, xcol, nT, ones_f, xsq, p_s1, p_s2)
        bcast_row(em, rstd, 128, nT, ones_f, rb, pb)
        bcast_row(em, negmr, 128, nT, ones_f, nb, pb)
        apply_ln(em, xcol, xln, rb, nb, nT, scr)
        for oc in range(DC):
            osl = slice(oc * 128, (oc + 1) * 128)
            for c in range(DC):
                em.dma(qwt_b[:, c, :], wqT_dr[c][:, osl])
            for c in range(DC):
                em.mm(pK[:, :], qwt_b[:, c, :], xln[:, c, :], start=(c == 0), stop=(c == DC - 1))
            em.copy(qev[:, :], pK[:, :])
            em.dma(qT_d[2 * oc][:, tsl], qev[0:64, :])
            em.dma(qT_d[2 * oc + 1][:, tsl], qev[64:128, :])

    qh_sb = em.sb([64, TPC], BF16)
    KhT = em.sb([64, 16, nT], BF16)
    Vp = em.sb([128, 64, 132], BF16)
    pT = em.sb([128, nT], BF16)
    lrow = em.sb([1, nT], F32)
    rb64 = em.sb([64, nT], F32)
    attev = em.sb([64, nT], BF16)
    psS = em.ps([128, nT])
    psO = em.ps([128, nT])

    em.memset(Vp[:, :, 64:65], 1.0)
    em.memset(Vp[:, :, 130:131], 1.0)
    for pair in range(NH // 2):
        for st in range(64):
            for lt in range(2):
                em.mm(pK[:, 0:128], vlatT[:, lt, st * 128:(st + 1) * 128],
                      wvbT_b[:, lt, pair * 128:(pair + 1) * 128],
                      start=(lt == 0), stop=(lt == 1))
            em.copy(Vp[:, st, 0:64], pK[:, 0:64])
            em.copy(Vp[:, st, 66:130], pK[:, 64:128])
        for hh in range(2):
            h = pair * 2 + hh
            vofs = 0 if hh == 0 else 66
            em.dma(qh_sb[:, :], qT_d[h])
            for sj in range(S // nT):
                ssl = slice(sj * nT, (sj + 1) * nT)
                for lt in range(2):
                    em.mm(psS[0:64, :], wkbT_b[:, lt, h * 64:(h + 1) * 64],
                          klatT[:, lt, ssl], start=(lt == 0), stop=(lt == 1))
                em.copy(KhT[0:64, sj, :], psS[0:64, :])
            for tg in range(TPC // nT):
                gsl = slice(tg * nT, (tg + 1) * nT)
                for st in range(64):
                    em.mm(psS[:, :], KhT[0:64, st // 4, (st % 4) * 128:(st % 4) * 128 + 128],
                          qh_sb[0:64, gsl])
                    em.act(pT[:, :], psS[:, :], AF.Exp)
                    em.mm(psO[0:65, :], Vp[:, st, vofs:vofs + 65], pT[:, :],
                          start=(st == 0), stop=(st == 63))
                em.copy(lrow[:1, :], psO[64:65, :])
                em.recip(lrow[:1, :], lrow[:1, :])
                em.mm(pb[0:64, :], ones_f[0:1, 0:64], lrow[:1, :])
                em.copy(rb64[:, :], pb[0:64, :])
                em.tt(attev[:, :], psO[0:64, :], rb64[:, :], OP.mult)
                em.dma(attT_d[h * 64:(h + 1) * 64, gsl], attev[:, :])

    attld = em.sb([128, nT], BF16)
    yout = em.sb([128, nT], F32)
    xres = em.sb([128, nT], F32)
    wot_b = em.sb([128, DC, 128], BF16)
    for tg in range(TPC // nT):
        gsl = slice(tg * nT, (tg + 1) * nT)
        for oc in range(DC):
            osl = slice(oc * 128, (oc + 1) * 128)
            for cc in range(DC):
                em.dma(rtile[:, :128], woT_r[cc][:, osl])
                em.copy(wot_b[:, cc, :], rtile[:, :128])
                em.dma(attld[:, :], attT_d[cc * 128:(cc + 1) * 128, gsl])
                em.mm(pK[:, :], wot_b[:, cc, :], attld[:, :], start=(cc == 0), stop=(cc == DC - 1))
            em.dma(xres[:, :], xs_r[oc, :, gsl])
            em.tt(yout[:, :], pK[:, :], xres[:, :], OP.add)
            em.dma(yT_r[oc, :, gsl], yout[:, :])
    return em.emit()



def mha_group_nk(g):
    if g < 4:
        return 8 * g + 8
    return 64 - 8 * (g - 4)


def build_mha():
    em = Em("mha")
    xc = em.inp("xc", [D, S], F32)
    xs = em.inp("xs", [D, TPC], F32)
    wqkv = em.inp("wqkv", [D, 3 * D], F32)
    bqkvP = em.inp("bqkvP", [128, 24], F32)
    bqkvF = em.inp("bqkvF", [1, 3 * D], F32)
    wproj = em.inp("wproj", [D, D], F32)
    bpP = em.inp("bpP", [128, DC], F32)
    masks = em.inp("masks", [128, 64, 256], BF16)
    yT = em.out("yT", [D, TPC], F32)

    xc_r = xc.rearrange("(c p) s -> c p s", p=128)
    xs_r = xs.rearrange("(c p) t -> c p t", p=128)
    yT_r = yT.rearrange("(c p) t -> c p t", p=128)
    wqkv_r = wqkv.rearrange("(c p) o -> c p o", p=128)
    wproj_r = wproj.rearrange("(c p) o -> c p o", p=128)

    KT_d = em.dram("KT_d", [D, S], BF16)
    V_d = em.dram("V_d", [64, 128, D], BF16)
    qT_d = em.dram("qT_d", [NH, HD, TPC], BF16)
    attT_d = em.dram("attT_d", [D, TPC], BF16)
    KT_dr = KT_d.rearrange("(c p) s -> c p s", p=128)

    ones_f = em.sb([128, 128], F32)
    em.memset(ones_f[:, :], 1.0)

    wld = em.sb([128, DC, 512], F32)
    wbig_b = em.sb([128, DC, 2 * D], BF16)
    bqP_sb = em.sb([128, 24], F32)
    em.dma(bqP_sb[:, :], bqkvP)
    bqF_sb = em.sb([1, 3 * D], F32)
    em.dma(bqF_sb[:, :], bqkvF)
    bp_sb = em.sb([128, DC], F32)
    em.dma(bp_sb[:, :], bpP)
    msk_sb = em.sb([128, 64, 256], BF16)
    em.dma(msk_sb[:, :, :], masks)
    bcV = em.sb([128, D], F32)
    pb = em.ps([128, 512])
    for vg in range(2):
        em.mm(pb[:, :], ones_f[0:1, 0:128], bqF_sb[0:1, 2048 + vg * 512:2048 + (vg + 1) * 512])
        em.copy(bcV[:, vg * 512:(vg + 1) * 512], pb[:, :])

    def load_wbig(col0, ncols):
        for seg in range(ncols // 512):
            ssl_d = slice(col0 + seg * 512, col0 + (seg + 1) * 512)
            for c in range(DC):
                em.dma(wld[:, c, :], wqkv_r[c][:, ssl_d])
            em.copy(wbig_b[:, :, seg * 512:(seg + 1) * 512], wld[:, :, :], eng="gpsimd")

    nT = 512
    xcol = em.sb([128, DC, nT], F32)
    xsq = em.sb([128, DC, nT], F32)
    xln = em.sb([128, DC, nT], BF16)
    rb = em.sb([128, nT], F32)
    nb = em.sb([128, nT], F32)
    scr = em.sb([128, nT], F32)
    kvout = em.sb([128, nT], BF16)
    p_s1 = em.ps([1, nT])
    p_s2 = em.ps([1, nT])
    pK = em.ps([128, nT])
    pV = em.ps([128, nT])

    load_wbig(D, 2 * D)
    for si in range(S // nT):
        ssl = slice(si * nT, (si + 1) * nT)
        for c in range(DC):
            em.dma(xcol[:, c, :], xc_r[c, :, ssl])
        rstd, negmr = ln_rows(em, xcol, nT, ones_f, xsq, p_s1, p_s2)
        bcast_row(em, rstd, 128, nT, ones_f, rb, pb)
        bcast_row(em, negmr, 128, nT, ones_f, nb, pb)
        apply_ln(em, xcol, xln, rb, nb, nT, scr)
        for ct in range(DC):
            csl = slice(ct * 128, (ct + 1) * 128)
            for c in range(DC):
                em.mm(pK[:, :], wbig_b[:, c, csl], xln[:, c, :], start=(c == 0), stop=(c == DC - 1))
            em.ts(kvout[:, :], pK[:, :], bqP_sb[:, 8 + ct:9 + ct], OP.add)
            em.dma(KT_dr[ct][:, ssl], kvout[:, :])
        for sub in range(4):
            st = si * 4 + sub
            for vg in range(2):
                vsl = slice(D + vg * 512, D + (vg + 1) * 512)
                for c in range(DC):
                    em.mm(pV[:, :], xln[:, c, sub * 128:(sub + 1) * 128],
                          wbig_b[:, c, vsl], start=(c == 0), stop=(c == DC - 1))
                em.tt(kvout[:, :], pV[:, :], bcV[:, vg * 512:(vg + 1) * 512], OP.add)
                em.dma(V_d[st][:, vg * 512:(vg + 1) * 512], kvout[:, :])

    load_wbig(0, D)
    for ti in range(TPC // nT):
        tsl = slice(ti * nT, (ti + 1) * nT)
        for c in range(DC):
            em.dma(xcol[:, c, :], xs_r[c, :, tsl])
        rstd, negmr = ln_rows(em, xcol, nT, ones_f, xsq, p_s1, p_s2)
        bcast_row(em, rstd, 128, nT, ones_f, rb, pb)
        bcast_row(em, negmr, 128, nT, ones_f, nb, pb)
        apply_ln(em, xcol, xln, rb, nb, nT, scr)
        for oc in range(DC):
            osl = slice(oc * 128, (oc + 1) * 128)
            for c in range(DC):
                em.mm(pK[:, :], wbig_b[:, c, osl], xln[:, c, :], start=(c == 0), stop=(c == DC - 1))
            em.ts(scr[:, :], pK[:, :], bqP_sb[:, oc:oc + 1], OP.add)
            em.copy(kvout[:, :], scr[:, :])
            em.dma(qT_d[2 * oc][:, tsl], kvout[0:64, :])
            em.dma(qT_d[2 * oc + 1][:, tsl], kvout[64:128, :])

    nG = 256
    qh_sb = em.sb([64, TPC], BF16)
    KhT = em.sb([64, S], BF16)
    Vh = em.sb([128, 64, 65], BF16)
    pT = em.sb([128, nG], BF16)
    lrow = em.sb([1, nG], F32)
    rb64 = em.sb([64, nG], F32)
    attev = em.sb([64, nG], BF16)
    psS = em.ps([128, nG])
    psO = em.ps([128, nG])

    em.memset(Vh[:, :, 64:65], 1.0)
    for h in range(NH):
        em.dma(qh_sb[:, :], qT_d[h])
        em.dma(KhT[:, :], KT_d[h * 64:(h + 1) * 64, :])
        em.dma(Vh[:, :, 0:64], V_d.rearrange("st p c -> p st c")[:, :, h * 64:(h + 1) * 64])
        for g in range(8):
            gsl = slice(g * nG, (g + 1) * nG)
            NK = mha_group_nk(g)
            for st in range(NK):
                em.mm(psS[:, :], KhT[0:64, st * 128:(st + 1) * 128], qh_sb[0:64, gsl])
                j = st - (NK - 8)
                if j >= 0:
                    em.tt(psS[:, :], psS[:, :], msk_sb[:, g * 8 + j, :], OP.add)
                em.act(pT[:, :], psS[:, :], AF.Exp, scale=0.125)
                em.mm(psO[0:65, :], Vh[:, st, 0:65], pT[:, :], start=(st == 0), stop=(st == NK - 1))
            em.copy(lrow[:1, :], psO[64:65, :])
            em.recip(lrow[:1, :], lrow[:1, :])
            em.mm(pb[0:64, :nG], ones_f[0:1, 0:64], lrow[:1, :])
            em.copy(rb64[:, :], pb[0:64, :nG])
            em.tt(attev[:, :], psO[0:64, :], rb64[:, :], OP.mult)
            em.dma(attT_d[h * 64:(h + 1) * 64, gsl], attev[:, :])

    load_wbig_proj = True
    for seg in range(2):
        ssl_d = slice(seg * 512, (seg + 1) * 512)
        for c in range(DC):
            em.dma(wld[:, c, :], wproj_r[c][:, ssl_d])
        em.copy(wbig_b[:, :, seg * 512:(seg + 1) * 512], wld[:, :, :], eng="gpsimd")
    attld = em.sb([128, nT], BF16)
    yout = em.sb([128, nT], F32)
    xres = em.sb([128, nT], F32)
    for tg in range(TPC // nT):
        gsl = slice(tg * nT, (tg + 1) * nT)
        for oc in range(DC):
            osl = slice(oc * 128, (oc + 1) * 128)
            for cc in range(DC):
                em.dma(attld[:, :], attT_d[cc * 128:(cc + 1) * 128, gsl])
                em.mm(pK[:, :], wbig_b[:, cc, osl], attld[:, :], start=(cc == 0), stop=(cc == DC - 1))
            em.ts(scr[:, :], pK[:, :], bp_sb[:, oc:oc + 1], OP.add)
            em.dma(xres[:, :], xs_r[oc, :, gsl])
            em.tt(yout[:, :], scr[:, :], xres[:, :], OP.add)
            em.dma(yT_r[oc, :, gsl], yout[:, :])
    return em.emit()


_progs = {}


def get_prog(name, *args):
    key = (name,) + args
    if key not in _progs:
        _progs[key] = {"ffn": build_ffn, "moe": build_moe, "bank": build_bank,
                       "lab": build_lab, "mla": build_mla, "mha": build_mha}[name](*args)
    return _progs[key]


def shard_xT(x_T):
    shards = []
    for core in range(NCORES):
        b = core // 4
        cols = [x_T[b][:, g * 128:(g + 1) * 128] for g in core_qtiles(core)]
        shards.append(np.ascontiguousarray(np.concatenate(cols, axis=1)))
    return shards


def unshard_xT(shards):
    out = np.zeros((B, D, S), np.float32)
    for core in range(NCORES):
        b = core // 4
        sh = np.asarray(shards[core])
        for j, g in enumerate(core_qtiles(core)):
            out[b][:, g * 128:(g + 1) * 128] = sh[:, j * 128:(j + 1) * 128]
    return out


def run_cores(nc, per_core, shared):
    in_maps = []
    for c in range(NCORES):
        m = dict(shared)
        m.update(per_core[c])
        in_maps.append(m)
    res = run_bass_kernel_spmd(nc, in_maps, core_ids=list(range(NCORES)))
    return [np.asarray(res.results[c]["yT"]) for c in range(NCORES)]


def rope_R():
    R = np.eye(D, dtype=np.float64)
    for h in range(NH):
        for j in range(8):
            th = h * (10000.0 ** (-(2.0 * j) / HD))
            c, s = np.cos(th), np.sin(th)
            r0 = h * HD + 2 * j
            r1 = r0 + 1
            R[r0, r0] = c
            R[r0, r1] = -s
            R[r1, r0] = s
            R[r1, r1] = c
    return R


def mha_masks(core):
    import ml_dtypes
    qt = core_qtiles(core)
    M = np.zeros((128, 64, 256), np.float32)
    for g in range(8):
        NK = mha_group_nk(g)
        tglob = [qt[2 * g], qt[2 * g + 1]]
        qpos = np.concatenate([np.arange(t * 128, (t + 1) * 128) for t in tglob])
        for j in range(8):
            st = NK - 8 + j
            kpos = np.arange(st * 128, (st + 1) * 128)
            allowed = kpos[:, None] <= qpos[None, :]
            M[:, g * 8 + j, :] = np.where(allowed, 0.0, np.float32(-1e30))
    return M.astype(ml_dtypes.bfloat16)


def kernel(**inputs):
    g = {k: np.asarray(v) for k, v in inputs.items()}
    x = g["L0_attn_a_x"]
    xT = np.ascontiguousarray(x.transpose(0, 2, 1)).astype(np.float32, copy=False)
    shards = shard_xT(xT)

    def ctx(shards):
        xc = unshard_xT(shards)
        return [np.ascontiguousarray(xc[c // 4]) for c in range(NCORES)]

    C = np.ascontiguousarray

    R = rope_R()
    nc = get_prog("mla")
    shared = dict(
        wq=g["L0_attn_a_wq"], wkb=g["L0_attn_a_wk_b"],
        R8T=C((R / 8.0).T.astype(np.float32)), RT=C(R.T.astype(np.float32)),
        wkaT=C(g["L0_attn_a_wk_a"].T), wvaT=C(g["L0_attn_a_wv_a"].T),
        wvbT=C(g["L0_attn_a_wv_b"].T), woT=C(g["L0_attn_a_wo"].T))
    xcs = ctx(shards)
    shards = run_cores(nc, [dict(xc=xcs[c], xs=shards[c]) for c in range(NCORES)], shared)

    nc = get_prog("ffn")
    sh_ffn = dict(wgT=C(g["L0_ffn_w_gate"].T), wuT=C(g["L0_ffn_w_up"].T),
                  wdT=C(g["L0_ffn_w_down"].T))
    shards = run_cores(nc, [dict(xs=shards[c]) for c in range(NCORES)], sh_ffn)

    nc = get_prog("mha")
    bq = g["L0_attn_b_b_qkv"]
    shared = dict(wqkv=g["L0_attn_b_w_qkv"],
                  bqkvP=C(bq.reshape(24, 128).T),
                  bqkvF=C(bq.reshape(1, 3 * D)),
                  wproj=g["L0_attn_b_w_proj"],
                  bpP=C(g["L0_attn_b_b_proj"].reshape(DC, 128).T))
    xcs = ctx(shards)
    shards = run_cores(nc, [dict(xc=xcs[c], xs=shards[c], masks=mha_masks(c))
                            for c in range(NCORES)], shared)

    nc = get_prog("ffn")
    shards = run_cores(nc, [dict(xs=shards[c]) for c in range(NCORES)], sh_ffn)

    nc = get_prog("bank")
    shared = dict(kT=C(g["L1_attn_a_k"].T), v=g["L1_attn_a_v"])
    shards = run_cores(nc, [dict(xs=shards[c]) for c in range(NCORES)], shared)

    nc = get_prog("moe")
    sh_moe = dict(rw=g["L1_ffn_router_w"], win=g["L1_ffn_w_in"], wout=g["L1_ffn_w_out"],
                  binP=C(g["L1_ffn_b_in"].T.reshape(DC, 128, NE).transpose(1, 0, 2)),
                  boutT=C(g["L1_ffn_b_out"].reshape(1, NE, D)),
                  ident=np.eye(128, dtype=np.float32))
    shards = run_cores(nc, [dict(xs=shards[c]) for c in range(NCORES)], sh_moe)

    xtop = shards

    def perpart(vec):
        return C(vec.reshape(DC, 128).T)

    for i in range(8):
        variant = "first" if i == 0 else ("last" if i == 7 else "mid")
        nc = get_prog("lab", variant)
        shared = dict(
            wqT=C(g["L1_attn_b_Wq"][i].transpose(1, 0, 2).reshape(D, D)),
            wkT=C(g["L1_attn_b_Wk"][i].transpose(1, 0, 2).reshape(D, D)),
            wvT=C(g["L1_attn_b_Wv"][i].transpose(1, 0, 2).reshape(D, D)),
            wo=g["L1_attn_b_Wo"][i],
            w1=g["L1_attn_b_W1"][i], w2=g["L1_attn_b_W2"][i],
            bqP=perpart(g["L1_attn_b_bq"][i].reshape(D)),
            bkP=perpart(g["L1_attn_b_bk"][i].reshape(D)),
            bvP=perpart(g["L1_attn_b_bv"][i].reshape(D)),
            boP=perpart(g["L1_attn_b_bo"][i]),
            b1P=C(g["L1_attn_b_b1"][i].reshape(NFT, 128).T),
            b2P=perpart(g["L1_attn_b_b2"][i]),
            ln1sP=perpart(g["L1_attn_b_ln1_s"][i]),
            ln1bP=perpart(g["L1_attn_b_ln1_b"][i]),
            ln2sP=perpart(g["L1_attn_b_ln2_s"][i]),
            ln2bP=perpart(g["L1_attn_b_ln2_b"][i]))
        xcs = ctx(shards)
        per_core = []
        for c in range(NCORES):
            m = dict(xc=xcs[c], xs=shards[c])
            if variant == "last":
                m["xtop"] = xtop[c]
            per_core.append(m)
        shards = run_cores(nc, per_core, shared)

    nc = get_prog("moe")
    shards = run_cores(nc, [dict(xs=shards[c]) for c in range(NCORES)], sh_moe)

    out = unshard_xT(shards)
    return np.ascontiguousarray(out.transpose(0, 2, 1))


# version 04
# speedup vs baseline: 1.0826x; 1.0826x over previous
import numpy as np
import contextlib
import concourse.bass as bass
import concourse.mybir as mybir
from concourse.alu_op_type import AluOpType
from concourse.bass_utils import run_bass_kernel_spmd

F32 = mybir.dt.float32
BF16 = mybir.dt.bfloat16
I32 = mybir.dt.int32
AF = mybir.ActivationFunctionType
AX = mybir.AxisListType
OP = AluOpType

NCORES = 8
B = 2
S = 8192
D = 1024
TPC = 2048
NQT = 16
DC = D // 128

def core_qtiles(core):
    cc = core % 4
    return [4 * k + cc for k in range(8)] + [63 - 4 * k - cc for k in range(8)]


class Em:
    ENG = ("tensor", "vector", "scalar", "gpsimd", "sync")

    def __init__(self, name):
        self.nc = bass.Bass("TRN2", target_bir_lowering=False)
        self.stack = contextlib.ExitStack()
        self.ops = []
        self.counts = {}
        self.last = None
        self.n_sb = 0

    def inp(self, name, shape, dtype):
        return self.nc.dram_tensor(name, list(shape), dtype, kind="ExternalInput").ap()

    def out(self, name, shape, dtype):
        return self.nc.dram_tensor(name, list(shape), dtype, kind="ExternalOutput").ap()

    def dram(self, name, shape, dtype):
        return self.nc.dram_tensor(name, list(shape), dtype, kind="Internal").ap()

    def sb(self, shape, dtype, name=None):
        self.n_sb += 1
        return self.stack.enter_context(
            self.nc.sbuf_tensor(name or f"sb{self.n_sb}", list(shape), dtype))

    def ps(self, shape, dtype=F32, name=None):
        self.n_sb += 1
        return self.stack.enter_context(
            self.nc.psum_tensor(name or f"ps{self.n_sb}", list(shape), dtype))

    def sbc(self, name, shape, dtype):
        if not hasattr(self, "_cache"):
            self._cache = {}
        if name not in self._cache:
            self._cache[name] = self.sb(shape, dtype, name=name)
        return self._cache[name]

    def _rec(self, engine, kind, fn, deps):
        key = engine + ("_dma" if kind == "dma" else "")
        inc = 16 if kind == "dma" else 1
        self.counts[key] = self.counts.get(key, 0) + inc
        if deps is None:
            deps = [self.last] if self.last is not None else []
        deps = [d for d in deps if d is not None]
        op = dict(engine=engine, kind=kind, fn=fn, key=key,
                  cum=self.counts[key], deps=deps)
        self.ops.append(op)
        self.last = op
        return op

    def dma(self, dst, src, deps=None, eng="sync"):
        return self._rec(eng, "dma", lambda e: e.dma_start(out=dst, in_=src), deps)

    def mm(self, out, lhsT, rhs, start=True, stop=True, deps=None):
        return self._rec("tensor", "op",
                         lambda e: e.matmul(out=out, lhsT=lhsT, rhs=rhs, start=start,
                                            stop=stop, skip_group_check=True), deps)

    def transpose(self, out, in_, ident, deps=None):
        return self._rec("tensor", "op",
                         lambda e: e.transpose(out, in_, ident), deps)

    def act(self, out, in_, func, scale=1.0, bias=None, deps=None, accum_out=None):
        kw = {}
        if accum_out is not None:
            kw["accum_out"] = accum_out
        if bias is not None:
            assert not isinstance(bias, float), "float act bias needs const AP; use an AP"
            kw["bias"] = bias
        return self._rec("scalar", "op",
                         lambda e: e.activation(out=out, in_=in_, func=func,
                                                scale=scale, **kw), deps)

    def tt(self, out, a, b, op, deps=None, eng="vector"):
        return self._rec(eng, "op",
                         lambda e: e.tensor_tensor(out=out, in0=a, in1=b, op=op), deps)

    def ts(self, out, in0, s1, op0, s2=None, op1=None, deps=None, eng="vector"):
        if op1 is None:
            return self._rec(eng, "op",
                             lambda e: e.tensor_scalar(out=out, in0=in0, scalar1=s1,
                                                       scalar2=None, op0=op0), deps)
        return self._rec(eng, "op",
                         lambda e: e.tensor_scalar(out=out, in0=in0, scalar1=s1,
                                                   scalar2=s2, op0=op0, op1=op1), deps)

    def stt(self, out, in0, scalar, in1, op0, op1, deps=None):
        return self._rec("vector", "op",
                         lambda e: e.scalar_tensor_tensor(out=out, in0=in0, scalar=scalar,
                                                          in1=in1, op0=op0, op1=op1), deps)

    def red(self, out, in_, op, deps=None, negate=None):
        return self._rec("vector", "op",
                         lambda e: e.tensor_reduce(out=out, in_=in_, axis=AX.X, op=op,
                                                   negate=negate), deps)

    def recip(self, out, in_, deps=None):
        return self._rec("vector", "op", lambda e: e.reciprocal(out=out, in_=in_), deps)

    def copy(self, out, in_, deps=None, eng="vector"):
        return self._rec(eng, "op", lambda e: e.tensor_copy(out=out, in_=in_), deps)

    def memset(self, ap, v, deps=None, eng="vector"):
        return self._rec(eng, "op", lambda e: e.memset(ap, float(v)), deps)

    def coll(self, kind, op, ins, outs, replica_groups, deps=None):
        return self._rec("gpsimd", "dma",
                         lambda e: e.collective_compute(kind, op,
                                                        replica_groups=replica_groups,
                                                        ins=ins, outs=outs), deps)

    def emit(self):
        nc = self.nc
        sems = {}
        for k in self.counts:
            sems[k] = self.stack.enter_context(nc.semaphore(f"sem_{k}"))
        byeng = {e: [o for o in self.ops if o["engine"] == e] for e in self.ENG}
        finals = dict(self.counts)

        def run_stream(eng_obj, eng_name):
            water = {}
            for op in byeng[eng_name]:
                for d in op["deps"]:
                    if (d["key"] == op["key"] and eng_name == "tensor"
                            and d["kind"] == "op" and op["kind"] == "op"):
                        continue
                    if water.get(d["key"], -1) >= d["cum"]:
                        continue
                    eng_obj.wait_ge(sems[d["key"]], d["cum"])
                    water[d["key"]] = d["cum"]
                inst = op["fn"](eng_obj)
                inst.then_inc(sems[op["key"]], 16 if op["kind"] == "dma" else 1)
            dk = eng_name + "_dma"
            if dk in finals:
                eng_obj.wait_ge(sems[dk], finals[dk])

        with nc.Block() as block:
            @block.tensor
            def _(e):
                run_stream(e, "tensor")

            @block.vector
            def _(e):
                run_stream(e, "vector")

            @block.scalar
            def _(e):
                run_stream(e, "scalar")

            @block.gpsimd
            def _(e):
                run_stream(e, "gpsimd")

            @block.sync
            def _(e):
                run_stream(e, "sync")
        self.stack.close()
        return nc



def ln_rows(em, xcol, nT, ones_f, scratch_sq, psum, psq, eps=1e-5):
    for c in range(DC):
        em.mm(psum[:1, :], ones_f[:, :1], xcol[:, c, :], start=(c == 0), stop=(c == DC - 1))
    for c in range(DC):
        em.act(scratch_sq[:, c, :], xcol[:, c, :], AF.Square)
    for c in range(DC):
        em.mm(psq[:1, :], ones_f[:, :1], scratch_sq[:, c, :], start=(c == 0), stop=(c == DC - 1))
    m = em.sbc("lnr_m", [1, nT], F32)
    var = em.sbc("lnr_v", [1, nT], F32)
    rstd = em.sbc("lnr_r", [1, nT], F32)
    negmr = em.sbc("lnr_n", [1, nT], F32)
    em.ts(m[:1, :], psum[:1, :], 1.0 / D, OP.mult)
    m2 = em.sbc("lnr_m2", [1, nT], F32)
    em.tt(m2[:1, :], m[:1, :], m[:1, :], OP.mult)
    em.ts(m2[:1, :], m2[:1, :], float(eps), OP.subtract)
    em.stt(var[:1, :], psq[:1, :], 1.0 / D, m2[:1, :], OP.mult, OP.subtract)
    sq = em.sbc("lnr_sq", [1, nT], F32)
    em.act(sq[:1, :], var[:1, :], AF.Sqrt)
    em.recip(rstd[:1, :], sq[:1, :])
    em.tt(negmr[:1, :], m[:1, :], rstd[:1, :], OP.mult)
    em.ts(negmr[:1, :], negmr[:1, :], -1.0, OP.mult)
    return rstd, negmr


def bcast_row(em, row, nP, nT, ones_f, out_sb, pbank):
    em.mm(pbank[:nP, :nT], ones_f[:1, :nP], row[:1, :nT])
    em.copy(out_sb[:nP, :nT], pbank[:nP, :nT])


def apply_ln(em, xcol, out_bf, rstd_b, negmr_b, nT, scratch, scale_ap=None, bias_ap=None):
    for c in range(DC):
        em.tt(scratch[:, :nT], xcol[:, c, :], rstd_b[:, :nT], OP.mult)
        if scale_ap is None:
            em.tt(out_bf[:, c, :], scratch[:, :nT], negmr_b[:, :nT], OP.add)
        else:
            em.tt(scratch[:, :nT], scratch[:, :nT], negmr_b[:, :nT], OP.add)
            em.ts(out_bf[:, c, :], scratch[:, :nT], scale_ap(c), OP.mult,
                  bias_ap(c), OP.add)


H_FFN = 3072
NHT = H_FFN // 128


def build_ffn():
    em = Em("ffn")
    xs = em.inp("xs", [D, TPC], F32)
    wgT = em.inp("wgT", [D, H_FFN], I32)
    wuT = em.inp("wuT", [D, H_FFN], I32)
    wdT = em.inp("wdT", [H_FFN, D], I32)
    yT = em.out("yT", [D, TPC], F32)

    xs_r = xs.rearrange("(c p) t -> c p t", p=128)
    yT_r = yT.rearrange("(c p) t -> c p t", p=128)
    wgT_r = wgT.rearrange("(c p) h -> c p h", p=128)
    wuT_r = wuT.rearrange("(c p) h -> c p h", p=128)
    wdT_r = wdT.rearrange("(c p) o -> c p o", p=128)

    ones_f = em.sb([128, 128], F32)
    em.memset(ones_f[:, :], 1.0)

    nT = 512
    xcol = em.sb([128, DC, nT], F32)
    xsq = em.sb([128, DC, nT], F32)
    xln = em.sb([128, DC, nT], BF16)
    rb = em.sb([128, nT], F32)
    nb = em.sb([128, nT], F32)
    wg_i = em.sb([128, DC, 128], I32)
    wg_b = em.sb([128, DC, 128], BF16)
    wu_i = em.sb([128, DC, 128], I32)
    wu_b = em.sb([128, DC, 128], BF16)
    wd_i = em.sb([128, NHT, 128], I32)
    wd_b = em.sb([128, NHT, 128], BF16)
    sg = em.sb([128, nT], BF16)
    su = em.sb([128, nT], F32)
    hsb = em.sb([128, NHT, nT], BF16)
    xres = em.sb([128, nT], F32)
    yout = em.sb([128, nT], F32)

    p_s1 = em.ps([1, nT])
    p_s2 = em.ps([1, nT])
    pb = em.ps([128, nT])
    pg = em.ps([128, nT])
    pu = em.ps([128, nT])
    pd = em.ps([128, nT])

    for ti in range(TPC // nT):
        tsl = slice(ti * nT, (ti + 1) * nT)
        for c in range(DC):
            em.dma(xcol[:, c, :], xs_r[c, :, tsl])
        rstd, negmr = ln_rows(em, xcol, nT, ones_f, xsq, p_s1, p_s2)
        bcast_row(em, rstd, 128, nT, ones_f, rb, pb)
        bcast_row(em, negmr, 128, nT, ones_f, nb, pb)
        apply_ln(em, xcol, xln, rb, nb, nT, xres)
        for ht in range(NHT):
            hsl = slice(ht * 128, (ht + 1) * 128)
            for c in range(DC):
                em.dma(wg_i[:, c, :], wgT_r[c, :, hsl])
                em.dma(wu_i[:, c, :], wuT_r[c, :, hsl])
            em.copy(wg_b[:, :, :], wg_i[:, :, :], eng="gpsimd")
            em.copy(wu_b[:, :, :], wu_i[:, :, :], eng="gpsimd")
            for c in range(DC):
                em.mm(pg[:, :], wg_b[:, c, :], xln[:, c, :], start=(c == 0), stop=(c == DC - 1))
            for c in range(DC):
                em.mm(pu[:, :], wu_b[:, c, :], xln[:, c, :], start=(c == 0), stop=(c == DC - 1))
            em.act(sg[:, :], pg[:, :], AF.Silu, scale=1.0 / 127.0)
            em.ts(su[:, :], pu[:, :], 1.0 / 127.0, OP.mult)
            em.tt(hsb[:, ht, :], sg[:, :], su[:, :], OP.mult)
        for oc in range(DC):
            osl = slice(oc * 128, (oc + 1) * 128)
            for ht in range(NHT):
                em.dma(wd_i[:, ht, :], wdT_r[ht, :, osl])
            em.copy(wd_b[:, :, :], wd_i[:, :, :], eng="gpsimd")
            for ht in range(NHT):
                em.mm(pd[:, :], wd_b[:, ht, :], hsb[:, ht, :], start=(ht == 0), stop=(ht == NHT - 1))
            em.dma(xres[:, :], xs_r[oc, :, tsl])
            em.stt(yout[:, :], pd[:, :], 1.0 / 127.0, xres[:, :], OP.mult, OP.add)
            em.dma(yT_r[oc, :, tsl], yout[:, :])
    return em.emit()


NE = 8
NH = 16
HD = 64
LAT = 256


def build_moe():
    em = Em("moe")
    xs = em.inp("xs", [D, TPC], F32)
    rw = em.inp("rw", [D, NE], F32)
    win = em.inp("win", [NE, D, D], F32)
    wout = em.inp("wout", [NE, D, D], F32)
    binP = em.inp("binP", [128, DC, NE], F32)
    boutT = em.inp("boutT", [1, NE, D], F32)
    ident = em.inp("ident", [128, 128], F32)
    yT = em.out("yT", [D, TPC], F32)

    xs_r = xs.rearrange("(c p) t -> c p t", p=128)
    yT_r = yT.rearrange("(c p) t -> c p t", p=128)
    rw_r = rw.rearrange("(c p) e -> c p e", p=128)
    win_r = win.rearrange("e (c p) h -> e c p h", p=128)
    wout_r = wout.rearrange("e (c p) o -> e c p o", p=128)

    ones_f = em.sb([128, 128], F32)
    em.memset(ones_f[:, :], 1.0)
    id_sb = em.sb([128, 128], F32)
    em.dma(id_sb[:, :], ident)
    rw_f = em.sb([128, DC, NE], F32)
    for c in range(DC):
        em.dma(rw_f[:, c, :], rw_r[c])
    rw_sb = em.sb([128, DC, NE], BF16)
    em.copy(rw_sb[:, :, :], rw_f[:, :, :])
    binP_sb = em.sb([128, DC, NE], F32)
    em.dma(binP_sb[:, :, :], binP)
    boutT_sb = em.sb([1, NE, D], F32)
    em.dma(boutT_sb[:, :, :], boutT)

    nT = 512
    xcol = em.sb([128, DC, nT], F32)
    xsq = em.sb([128, DC, nT], F32)
    xln = em.sb([128, DC, nT], BF16)
    rb = em.sb([128, nT], F32)
    nb = em.sb([128, nT], F32)
    scr = em.sb([128, nT], F32)
    lg_sb = em.sb([8, nT], F32)
    ltok = em.sb([128, 4, 8], F32)
    ptok = em.sb([128, 4, 8], F32)
    wtok = em.sb([128, 4, 8], F32)
    tmp8 = em.sb([128, 8], F32)
    col1 = em.sb([128, 1], F32)
    wT_sb = em.sb([8, nT], F32)
    we_row = em.sb([1, nT], F32)
    web = em.sb([128, nT], F32)
    wf32 = em.sb([128, DC, D], F32)
    win_b = em.sb([128, DC, D], BF16)
    wout_b = em.sb([128, DC, D], BF16)
    hg = em.sb([128, nT], F32)
    hs = em.sb([128, DC, nT], BF16)
    yacc = em.sb([128, DC, nT], F32)
    xres = em.sb([128, nT], F32)
    yout = em.sb([128, nT], F32)

    p_s1 = em.ps([1, nT])
    p_s2 = em.ps([1, nT])
    pb = em.ps([128, nT])
    pr = em.ps([8, nT])
    ph = em.ps([128, nT])
    py = em.ps([128, nT])

    for ti in range(TPC // nT):
        tsl = slice(ti * nT, (ti + 1) * nT)
        for c in range(DC):
            em.dma(xcol[:, c, :], xs_r[c, :, tsl])
        rstd, negmr = ln_rows(em, xcol, nT, ones_f, xsq, p_s1, p_s2)
        bcast_row(em, rstd, 128, nT, ones_f, rb, pb)
        bcast_row(em, negmr, 128, nT, ones_f, nb, pb)
        apply_ln(em, xcol, xln, rb, nb, nT, scr)
        for c in range(DC):
            em.mm(pr[:8, :], rw_sb[:, c, :], xln[:, c, :], start=(c == 0), stop=(c == DC - 1))
        em.copy(lg_sb[:8, :], pr[:8, :])
        for q in range(nT // 128):
            qsl = slice(q * 128, (q + 1) * 128)
            em.transpose(pb[:, 0:8], lg_sb[0:8, qsl], id_sb[0:8, 0:8])
            em.copy(ltok[:, q, :], pb[:, 0:8])
            em.red(col1[:, :], ltok[:, q, :], OP.max)
            em.ts(tmp8[:, :], ltok[:, q, :], col1[:, 0:1], OP.subtract)
            em.act(tmp8[:, :], tmp8[:, :], AF.Exp)
            em.red(col1[:, :], tmp8[:, :], OP.add)
            em.recip(col1[:, :], col1[:, :])
            em.ts(ptok[:, q, :], tmp8[:, :], col1[:, 0:1], OP.mult)
            em.red(col1[:, :], ptok[:, q, :], OP.max)
            em.ts(tmp8[:, :], ptok[:, q, :], col1[:, 0:1], OP.is_ge)
            em.stt(tmp8[:, :], tmp8[:, :], -1e30, ptok[:, q, :], OP.mult, OP.add)
            em.red(col1[:, :], tmp8[:, :], OP.max)
            em.ts(tmp8[:, :], ptok[:, q, :], col1[:, 0:1], OP.is_ge)
            em.tt(wtok[:, q, :], ptok[:, q, :], tmp8[:, :], OP.mult)
            em.transpose(pr[:8, qsl], wtok[:, q, :], id_sb[0:128, 0:128])
        em.copy(wT_sb[:8, :], pr[:8, :])
        for e in range(NE):
            em.mm(p_s1[:1, :], id_sb[0:8, e:e + 1], wT_sb[0:8, :])
            em.copy(we_row[:1, :], p_s1[:1, :])
            em.mm(pb[:, :], ones_f[0:1, 0:128], we_row[:1, :])
            em.copy(web[:, :], pb[:, :])
            for c in range(DC):
                em.dma(wf32[:, c, :], win_r[e, c])
            em.copy(win_b[:, :, :], wf32[:, :, :], eng="gpsimd")
            for hc in range(DC):
                hsl = slice(hc * 128, (hc + 1) * 128)
                for c in range(DC):
                    em.mm(ph[:, :], win_b[:, c, hsl], xln[:, c, :], start=(c == 0), stop=(c == DC - 1))
                em.act(hg[:, :], ph[:, :], AF.Gelu_apprx_tanh, bias=binP_sb[:, hc, e:e + 1])
                em.tt(hs[:, hc, :], hg[:, :], web[:, :], OP.mult)
            for c in range(DC):
                em.dma(wf32[:, c, :], wout_r[e, c])
            em.copy(wout_b[:, :, :], wf32[:, :, :], eng="gpsimd")
            for oc in range(DC):
                osl = slice(oc * 128, (oc + 1) * 128)
                for hc in range(DC):
                    em.mm(py[:, :], wout_b[:, hc, osl], hs[:, hc, :], start=(hc == 0), stop=False)
                em.mm(py[:, :], boutT_sb[0:1, e, osl], we_row[:1, :], start=False, stop=True)
                if e == 0:
                    em.copy(yacc[:, oc, :], py[:, :])
                else:
                    em.tt(yacc[:, oc, :], yacc[:, oc, :], py[:, :], OP.add)
        for oc in range(DC):
            em.dma(xres[:, :], xs_r[oc, :, tsl])
            em.tt(yout[:, :], yacc[:, oc, :], xres[:, :], OP.add)
            em.dma(yT_r[oc, :, tsl], yout[:, :])
    return em.emit()


NBT = 64


def build_bank():
    em = Em("bank")
    xs = em.inp("xs", [D, TPC], F32)
    kT = em.inp("kT", [D, S], F32)
    v = em.inp("v", [S, D], F32)
    yT = em.out("yT", [D, TPC], F32)

    xs_r = xs.rearrange("(c p) t -> c p t", p=128)
    yT_r = yT.rearrange("(c p) t -> c p t", p=128)
    kT_r = kT.rearrange("(c p) s -> c p s", p=128)
    v_r = v.rearrange("(st p) o -> st p o", p=128)

    kTb2 = em.dram("kTb2", [NBT, DC, 128, 128], BF16)
    vb = em.dram("vb", [NBT, 128, D], BF16)
    kTb2_rd = kTb2.rearrange("n c p m -> n p c m")

    ones_f = em.sb([128, 128], F32)
    em.memset(ones_f[:, :], 1.0)
    ones_b = em.sb([128, 1], BF16)
    em.memset(ones_b[:, :], 1.0)

    cast_f = em.sb([128, 2048], F32)
    cast_b = em.sb([128, 2048], BF16)
    for c in range(DC):
        for j in range(4):
            ssl = slice(j * 2048, (j + 1) * 2048)
            em.dma(cast_f[:, :], kT_r[c][:, ssl])
            em.copy(cast_b[:, :], cast_f[:, :])
            for q in range(16):
                nt = j * 16 + q
                em.dma(kTb2[nt, c], cast_b[:, q * 128:(q + 1) * 128])
    for st in range(NBT):
        em.dma(cast_f[:, :1024], v_r[st])
        em.copy(cast_b[:, :1024], cast_f[:, :1024])
        em.dma(vb[st], cast_b[:, :1024])

    nT = 512
    xcol = em.sb([128, DC, nT], F32)
    xsq = em.sb([128, DC, nT], F32)
    rb = em.sb([128, nT], F32)
    nb = em.sb([128, nT], F32)
    scr = em.sb([128, nT], F32)
    xlnb = em.sb([128, DC, TPC], BF16)
    p_s1 = em.ps([1, nT])
    p_s2 = em.ps([1, nT])
    pbB = em.ps([128, nT])
    for ti in range(TPC // nT):
        tsl = slice(ti * nT, (ti + 1) * nT)
        for c in range(DC):
            em.dma(xcol[:, c, :], xs_r[c, :, tsl])
        rstd, negmr = ln_rows(em, xcol, nT, ones_f, xsq, p_s1, p_s2)
        bcast_row(em, rstd, 128, nT, ones_f, rb, pbB)
        bcast_row(em, negmr, 128, nT, ones_f, nb, pbB)
        apply_ln(em, xcol, xlnb[:, :, tsl], rb, nb, nT, scr)

    nG = 256
    kt_sb = em.sb([128, DC, 128], BF16)
    sc_sb = em.sb([128, NBT, nG], F32)
    sqt = em.sb([128, nG], BF16)
    gt = em.sb([128, nG], BF16)
    vt_sb = em.sb([128, D], BF16)
    srow = em.sb([1, nG], F32)
    cb = em.sb([128, nG], F32)
    g1 = em.sb([128, nG], F32)
    yout = em.sb([128, nG], F32)
    xres = em.sb([128, nG], F32)
    psC = em.ps([128, nG])
    py4 = [em.ps([128, nG]) for _ in range(4)]

    SQRT_N = float(np.sqrt(S))
    for tg in range(TPC // nG):
        gsl = slice(tg * nG, (tg + 1) * nG)
        for nt in range(NBT):
            em.dma(kt_sb[:, :, :], kTb2_rd[nt])
            for c in range(DC):
                em.mm(psC[:, :], kt_sb[:, c, :], xlnb[:, c, gsl], start=(c == 0), stop=(c == DC - 1))
            em.copy(sc_sb[:, nt, :], psC[:, :])
            em.act(sqt[:, :], psC[:, :], AF.Square)
            em.mm(p_s1[:1, :nG], ones_b[:, 0:1], sqt[:, :], start=(nt == 0), stop=(nt == NBT - 1))
        em.copy(srow[:1, :], p_s1[:1, :nG])
        em.act(srow[:1, :], srow[:1, :], AF.Sqrt)
        em.recip(srow[:1, :], srow[:1, :])
        em.ts(srow[:1, :], srow[:1, :], SQRT_N, OP.mult)
        em.mm(pbB[:, :nG], ones_f[0:1, 0:128], srow[:1, :])
        em.copy(cb[:, :], pbB[:, :nG])
        for half in range(2):
            for nt in range(NBT):
                em.dma(vt_sb[:, :], vb[nt])
                em.tt(g1[:, :], sc_sb[:, nt, :], cb[:, :], OP.mult)
                em.act(gt[:, :], g1[:, :], AF.Gelu)
                for j in range(4):
                    oc = half * 4 + j
                    em.mm(py4[j][:, :], vt_sb[:, oc * 128:(oc + 1) * 128], gt[:, :],
                          start=(nt == 0), stop=(nt == NBT - 1))
            for j in range(4):
                oc = half * 4 + j
                em.dma(xres[:, :], xs_r[oc, :, gsl])
                em.tt(yout[:, :], py4[j][:, :], xres[:, :], OP.add)
                em.dma(yT_r[oc, :, gsl], yout[:, :])
    return em.emit()


F_LAB = 4096
NFT = F_LAB // 128


def build_lab(variant):
    em = Em("lab_" + variant)
    xc = em.inp("xc", [D, S], F32)
    xs = em.inp("xs", [D, TPC], F32)
    wqT = em.inp("wqT", [D, D], F32)
    wkT = em.inp("wkT", [D, D], F32)
    wvT = em.inp("wvT", [D, D], F32)
    wo = em.inp("wo", [D, D], F32)
    w1 = em.inp("w1", [D, F_LAB], F32)
    w2 = em.inp("w2", [F_LAB, D], F32)
    bqP = em.inp("bqP", [128, DC], F32)
    bkP = em.inp("bkP", [128, DC], F32)
    bvP = em.inp("bvP", [128, DC], F32)
    boP = em.inp("boP", [128, DC], F32)
    b1P = em.inp("b1P", [128, NFT], F32)
    b2P = em.inp("b2P", [128, DC], F32)
    ln1sP = em.inp("ln1sP", [128, DC], F32)
    ln1bP = em.inp("ln1bP", [128, DC], F32)
    ln2sP = em.inp("ln2sP", [128, DC], F32)
    ln2bP = em.inp("ln2bP", [128, DC], F32)
    if variant == "last":
        xtop = em.inp("xtop", [D, TPC], F32)
        xtop_r = xtop.rearrange("(c p) t -> c p t", p=128)
    yT = em.out("yT", [D, TPC], F32)

    xc_r = xc.rearrange("(c p) s -> c p s", p=128)
    xs_r = xs.rearrange("(c p) t -> c p t", p=128)
    yT_r = yT.rearrange("(c p) t -> c p t", p=128)
    wqT_r = wqT.rearrange("(c p) o -> c p o", p=128)
    wkT_r = wkT.rearrange("(c p) o -> c p o", p=128)
    wvT_r = wvT.rearrange("(c p) o -> c p o", p=128)
    wo_r = wo.rearrange("(c p) o -> c p o", p=128)
    w1_r = w1.rearrange("(c p) f -> c p f", p=128)
    w2_r = w2.rearrange("(c p) o -> c p o", p=128)

    ones_f = em.sb([128, 128], F32)
    em.memset(ones_f[:, :], 1.0)

    nT = 512
    xcol = em.sb([128, DC, nT], F32)
    qf = em.sb([128, DC, nT], F32)
    xb = em.sb([128, DC, nT], BF16)
    rbb = em.sb([128, nT], F32)
    nbb = em.sb([128, nT], F32)
    scr = em.sb([128, nT], F32)
    t1 = em.sb([128, nT], F32)
    e1 = em.sb([128, nT], F32)
    t3 = em.sb([128, nT], F32)
    kf = em.sb([128, nT], F32)
    vv = em.sb([128, nT], F32)
    red1 = em.sb([128, 1], F32)
    KVacc = em.sb([128, DC], F32)
    SKacc = em.sb([128, DC], F32)
    wA_b = em.sb([128, DC, D], BF16)
    wB_b = em.sb([128, DC, D], BF16)

    bq_sb = em.sb([128, DC], F32)
    em.dma(bq_sb[:, :], bqP)
    bk_sb = em.sb([128, DC], F32)
    em.dma(bk_sb[:, :], bkP)
    bv_sb = em.sb([128, DC], F32)
    em.dma(bv_sb[:, :], bvP)
    bo_sb = em.sb([128, DC], F32)
    em.dma(bo_sb[:, :], boP)
    b1_sb = em.sb([128, NFT], F32)
    em.dma(b1_sb[:, :], b1P)
    b2_sb = em.sb([128, DC], F32)
    em.dma(b2_sb[:, :], b2P)
    l1s = em.sb([128, DC], F32)
    em.dma(l1s[:, :], ln1sP)
    l1b = em.sb([128, DC], F32)
    em.dma(l1b[:, :], ln1bP)
    l2s = em.sb([128, DC], F32)
    em.dma(l2s[:, :], ln2sP)
    l2b = em.sb([128, DC], F32)
    em.dma(l2b[:, :], ln2bP)

    p_s1 = em.ps([1, nT])
    p_s2 = em.ps([1, nT])
    pbb = em.ps([128, nT])
    psK = em.ps([128, nT])
    psV = em.ps([128, nT])

    def load_w(dst_b, src_r):
        for c in range(DC):
            em.dma(xcol[:, c, :], src_r[c][:, 0:512])
        em.copy(dst_b[:, :, 0:512], xcol[:, :, :], eng="gpsimd")
        for c in range(DC):
            em.dma(xcol[:, c, :], src_r[c][:, 512:])
        em.copy(dst_b[:, :, 512:], xcol[:, :, :], eng="gpsimd")

    load_w(wA_b, wkT_r)
    load_w(wB_b, wvT_r)

    for si in range(S // nT):
        ssl = slice(si * nT, (si + 1) * nT)
        for c in range(DC):
            em.dma(xcol[:, c, :], xc_r[c, :, ssl])
        if variant == "first":
            rstd, negmr = ln_rows(em, xcol, nT, ones_f, qf, p_s1, p_s2)
            bcast_row(em, rstd, 128, nT, ones_f, rbb, pbb)
            bcast_row(em, negmr, 128, nT, ones_f, nbb, pbb)
            apply_ln(em, xcol, xb, rbb, nbb, nT, scr)
        else:
            em.copy(xb[:, :, :], xcol[:, :, :])
        for ct in range(DC):
            csl = slice(ct * 128, (ct + 1) * 128)
            for c in range(DC):
                em.mm(psK[:, :], wA_b[:, c, csl], xb[:, c, :], start=(c == 0), stop=(c == DC - 1))
            for c in range(DC):
                em.mm(psV[:, :], wB_b[:, c, csl], xb[:, c, :], start=(c == 0), stop=(c == DC - 1))
            em.ts(t1[:, :], psK[:, :], bk_sb[:, ct:ct + 1], OP.add, 0.0, OP.min)
            em.act(e1[:, :], t1[:, :], AF.Exp)
            em.ts(t3[:, :], psK[:, :], bk_sb[:, ct:ct + 1], OP.add, 0.0, OP.max)
            em.tt(kf[:, :], e1[:, :], t3[:, :], OP.add)
            em.ts(vv[:, :], psV[:, :], bv_sb[:, ct:ct + 1], OP.add)
            em.tt(e1[:, :], kf[:, :], vv[:, :], OP.mult)
            em.red(red1[:, :], e1[:, :], OP.add)
            if si == 0:
                em.copy(KVacc[:, ct:ct + 1], red1[:, :])
            else:
                em.tt(KVacc[:, ct:ct + 1], KVacc[:, ct:ct + 1], red1[:, :], OP.add)
            em.red(red1[:, :], kf[:, :], OP.add)
            if si == 0:
                em.copy(SKacc[:, ct:ct + 1], red1[:, :])
            else:
                em.tt(SKacc[:, ct:ct + 1], SKacc[:, ct:ct + 1], red1[:, :], OP.add)

    load_w(wA_b, wqT_r)
    load_w(wB_b, wo_r)
    attb = em.sb([128, DC, nT], BF16)
    r1 = em.sb([128, DC, nT], F32)
    x1f = em.sb([128, DC, nT], F32)
    x1b = em.sb([128, DC, nT], BF16)
    wstr = em.sb([128, DC, 128], F32)
    wstr_b = em.sb([128, DC, 128], BF16)
    w2str_b = em.sb([128, NFT, 128], BF16)
    h1b = em.sb([128, NFT, nT], BF16)
    yo = em.sb([128, nT], F32)

    for ti in range(TPC // nT):
        tsl = slice(ti * nT, (ti + 1) * nT)
        for c in range(DC):
            em.dma(xcol[:, c, :], xs_r[c, :, tsl])
        if variant == "first":
            rstd, negmr = ln_rows(em, xcol, nT, ones_f, qf, p_s1, p_s2)
            bcast_row(em, rstd, 128, nT, ones_f, rbb, pbb)
            bcast_row(em, negmr, 128, nT, ones_f, nbb, pbb)
            for c in range(DC):
                em.tt(scr[:, :], xcol[:, c, :], rbb[:, :], OP.mult)
                em.tt(xcol[:, c, :], scr[:, :], nbb[:, :], OP.add)
        em.copy(xb[:, :, :], xcol[:, :, :])
        for ct in range(DC):
            csl = slice(ct * 128, (ct + 1) * 128)
            for c in range(DC):
                em.mm(psK[:, :], wA_b[:, c, csl], xb[:, c, :], start=(c == 0), stop=(c == DC - 1))
            em.ts(t1[:, :], psK[:, :], bq_sb[:, ct:ct + 1], OP.add, 0.0, OP.min)
            em.act(e1[:, :], t1[:, :], AF.Exp)
            em.ts(t3[:, :], psK[:, :], bq_sb[:, ct:ct + 1], OP.add, 0.0, OP.max)
            em.tt(qf[:, ct, :], e1[:, :], t3[:, :], OP.add)
        for ct in range(DC):
            em.ts(t1[:, :], qf[:, ct, :], SKacc[:, ct:ct + 1], OP.mult, 1e-6, OP.add)
            em.recip(t1[:, :], t1[:, :])
            em.ts(t3[:, :], qf[:, ct, :], KVacc[:, ct:ct + 1], OP.mult)
            em.tt(attb[:, ct, :], t3[:, :], t1[:, :], OP.mult)
        for oc in range(DC):
            osl = slice(oc * 128, (oc + 1) * 128)
            for ct in range(DC):
                em.mm(psV[:, :], wB_b[:, ct, osl], attb[:, ct, :], start=(ct == 0), stop=(ct == DC - 1))
            em.act(e1[:, :], psV[:, :], AF.Gelu_apprx_tanh, bias=bo_sb[:, oc:oc + 1])
            em.tt(r1[:, oc, :], e1[:, :], xcol[:, oc, :], OP.add)
        rstd, negmr = ln_rows(em, r1, nT, ones_f, qf, p_s1, p_s2, eps=1e-6)
        bcast_row(em, rstd, 128, nT, ones_f, rbb, pbb)
        bcast_row(em, negmr, 128, nT, ones_f, nbb, pbb)
        for c in range(DC):
            em.tt(scr[:, :], r1[:, c, :], rbb[:, :], OP.mult)
            em.tt(scr[:, :], scr[:, :], nbb[:, :], OP.add)
            em.ts(x1f[:, c, :], scr[:, :], l1s[:, c:c + 1], OP.mult, l1b[:, c:c + 1], OP.add)
        em.copy(x1b[:, :, :], x1f[:, :, :])
        for ft in range(NFT):
            fsl = slice(ft * 128, (ft + 1) * 128)
            for c in range(DC):
                em.dma(wstr[:, c, :], w1_r[c][:, fsl])
            em.copy(wstr_b[:, :, :], wstr[:, :, :], eng="gpsimd")
            for c in range(DC):
                em.mm(psK[:, :], wstr_b[:, c, :], x1b[:, c, :], start=(c == 0), stop=(c == DC - 1))
            em.act(h1b[:, ft, :], psK[:, :], AF.Gelu_apprx_tanh, bias=b1_sb[:, ft:ft + 1])
        for oc in range(DC):
            osl = slice(oc * 128, (oc + 1) * 128)
            for grp in range(4):
                for j in range(DC):
                    fc = grp * DC + j
                    em.dma(wstr[:, j, :], w2_r[fc][:, osl])
                em.copy(w2str_b[:, grp * DC:(grp + 1) * DC, :], wstr[:, :, :], eng="gpsimd")
            for fc in range(NFT):
                em.mm(psV[:, :], w2str_b[:, fc, :], h1b[:, fc, :], start=(fc == 0), stop=(fc == NFT - 1))
            em.act(e1[:, :], psV[:, :], AF.Gelu_apprx_tanh, bias=b2_sb[:, oc:oc + 1])
            em.tt(r1[:, oc, :], e1[:, :], x1f[:, oc, :], OP.add)
        rstd, negmr = ln_rows(em, r1, nT, ones_f, qf, p_s1, p_s2, eps=1e-6)
        bcast_row(em, rstd, 128, nT, ones_f, rbb, pbb)
        bcast_row(em, negmr, 128, nT, ones_f, nbb, pbb)
        for c in range(DC):
            em.tt(scr[:, :], r1[:, c, :], rbb[:, :], OP.mult)
            em.tt(scr[:, :], scr[:, :], nbb[:, :], OP.add)
            em.ts(yo[:, :], scr[:, :], l2s[:, c:c + 1], OP.mult, l2b[:, c:c + 1], OP.add)
            if variant == "last":
                em.dma(t1[:, :], xtop_r[c, :, tsl])
                em.tt(yo[:, :], yo[:, :], t1[:, :], OP.add)
            em.dma(yT_r[c, :, tsl], yo[:, :])
    return em.emit()



def build_lab_p1(variant):
    em = Em("lab_p1_" + variant)
    xs = em.inp("xs", [D, TPC], F32)
    wkT = em.inp("wkT", [D, D], F32)
    wvT = em.inp("wvT", [D, D], F32)
    bkP = em.inp("bkP", [128, DC], F32)
    bvP = em.inp("bvP", [128, DC], F32)
    pacc = em.out("pacc", [128, 2 * DC], F32)

    xs_r = xs.rearrange("(c p) t -> c p t", p=128)
    wkT_r = wkT.rearrange("(c p) o -> c p o", p=128)
    wvT_r = wvT.rearrange("(c p) o -> c p o", p=128)

    ones_f = em.sb([128, 128], F32)
    em.memset(ones_f[:, :], 1.0)

    nT = 512
    xcol = em.sb([128, DC, nT], F32)
    qf = em.sb([128, DC, nT], F32)
    xb = em.sb([128, DC, nT], BF16)
    rbb = em.sb([128, nT], F32)
    nbb = em.sb([128, nT], F32)
    scr = em.sb([128, nT], F32)
    t1 = em.sb([128, nT], F32)
    e1 = em.sb([128, nT], F32)
    t3 = em.sb([128, nT], F32)
    kf = em.sb([128, nT], F32)
    vv = em.sb([128, nT], F32)
    red1 = em.sb([128, 1], F32)
    accs = em.sb([128, 2 * DC], F32)
    wA_b = em.sb([128, DC, D], BF16)
    wB_b = em.sb([128, DC, D], BF16)

    bk_sb = em.sb([128, DC], F32)
    em.dma(bk_sb[:, :], bkP)
    bv_sb = em.sb([128, DC], F32)
    em.dma(bv_sb[:, :], bvP)

    p_s1 = em.ps([1, nT])
    p_s2 = em.ps([1, nT])
    pbb = em.ps([128, nT])
    psK = em.ps([128, nT])
    psV = em.ps([128, nT])

    def load_w(dst_b, src_r):
        for c in range(DC):
            em.dma(xcol[:, c, :], src_r[c][:, 0:512])
        em.copy(dst_b[:, :, 0:512], xcol[:, :, :], eng="gpsimd")
        for c in range(DC):
            em.dma(xcol[:, c, :], src_r[c][:, 512:])
        em.copy(dst_b[:, :, 512:], xcol[:, :, :], eng="gpsimd")

    load_w(wA_b, wkT_r)
    load_w(wB_b, wvT_r)

    for si in range(TPC // nT):
        ssl = slice(si * nT, (si + 1) * nT)
        for c in range(DC):
            em.dma(xcol[:, c, :], xs_r[c, :, ssl])
        if variant == "first":
            rstd, negmr = ln_rows(em, xcol, nT, ones_f, qf, p_s1, p_s2)
            bcast_row(em, rstd, 128, nT, ones_f, rbb, pbb)
            bcast_row(em, negmr, 128, nT, ones_f, nbb, pbb)
            apply_ln(em, xcol, xb, rbb, nbb, nT, scr)
        else:
            em.copy(xb[:, :, :], xcol[:, :, :])
        for ct in range(DC):
            csl = slice(ct * 128, (ct + 1) * 128)
            for c in range(DC):
                em.mm(psK[:, :], wA_b[:, c, csl], xb[:, c, :], start=(c == 0), stop=(c == DC - 1))
            for c in range(DC):
                em.mm(psV[:, :], wB_b[:, c, csl], xb[:, c, :], start=(c == 0), stop=(c == DC - 1))
            em.ts(t1[:, :], psK[:, :], bk_sb[:, ct:ct + 1], OP.add, 0.0, OP.min)
            em.act(e1[:, :], t1[:, :], AF.Exp)
            em.ts(t3[:, :], psK[:, :], bk_sb[:, ct:ct + 1], OP.add, 0.0, OP.max)
            em.tt(kf[:, :], e1[:, :], t3[:, :], OP.add)
            em.ts(vv[:, :], psV[:, :], bv_sb[:, ct:ct + 1], OP.add)
            em.tt(e1[:, :], kf[:, :], vv[:, :], OP.mult)
            em.red(red1[:, :], e1[:, :], OP.add)
            if si == 0:
                em.copy(accs[:, ct:ct + 1], red1[:, :])
            else:
                em.tt(accs[:, ct:ct + 1], accs[:, ct:ct + 1], red1[:, :], OP.add)
            em.red(red1[:, :], kf[:, :], OP.add)
            if si == 0:
                em.copy(accs[:, DC + ct:DC + ct + 1], red1[:, :])
            else:
                em.tt(accs[:, DC + ct:DC + ct + 1], accs[:, DC + ct:DC + ct + 1], red1[:, :], OP.add)
    em.dma(pacc, accs[:, :])
    return em.emit()


def build_lab_p2(variant):
    em = Em("lab_p2_" + variant)
    xs = em.inp("xs", [D, TPC], F32)
    pacc4 = em.inp("pacc4", [4, 128, 2 * DC], F32)
    wqT = em.inp("wqT", [D, D], F32)
    wo = em.inp("wo", [D, D], F32)
    w1 = em.inp("w1", [D, F_LAB], F32)
    w2 = em.inp("w2", [F_LAB, D], F32)
    bqP = em.inp("bqP", [128, DC], F32)
    boP = em.inp("boP", [128, DC], F32)
    b1P = em.inp("b1P", [128, NFT], F32)
    b2P = em.inp("b2P", [128, DC], F32)
    ln1sP = em.inp("ln1sP", [128, DC], F32)
    ln1bP = em.inp("ln1bP", [128, DC], F32)
    ln2sP = em.inp("ln2sP", [128, DC], F32)
    ln2bP = em.inp("ln2bP", [128, DC], F32)
    if variant == "last":
        xtop = em.inp("xtop", [D, TPC], F32)
        xtop_r = xtop.rearrange("(c p) t -> c p t", p=128)
    yT = em.out("yT", [D, TPC], F32)

    xs_r = xs.rearrange("(c p) t -> c p t", p=128)
    yT_r = yT.rearrange("(c p) t -> c p t", p=128)
    wqT_r = wqT.rearrange("(c p) o -> c p o", p=128)
    wo_r = wo.rearrange("(c p) o -> c p o", p=128)
    w1_r = w1.rearrange("(c p) f -> c p f", p=128)
    w2_r = w2.rearrange("(c p) o -> c p o", p=128)

    ones_f = em.sb([128, 128], F32)
    em.memset(ones_f[:, :], 1.0)

    nT = 512
    xcol = em.sb([128, DC, nT], F32)
    qf = em.sb([128, DC, nT], F32)
    xb = em.sb([128, DC, nT], BF16)
    rbb = em.sb([128, nT], F32)
    nbb = em.sb([128, nT], F32)
    scr = em.sb([128, nT], F32)
    t1 = em.sb([128, nT], F32)
    e1 = em.sb([128, nT], F32)
    t3 = em.sb([128, nT], F32)
    wA_b = em.sb([128, DC, D], BF16)
    wB_b = em.sb([128, DC, D], BF16)

    bq_sb = em.sb([128, DC], F32)
    em.dma(bq_sb[:, :], bqP)
    bo_sb = em.sb([128, DC], F32)
    em.dma(bo_sb[:, :], boP)
    b1_sb = em.sb([128, NFT], F32)
    em.dma(b1_sb[:, :], b1P)
    b2_sb = em.sb([128, DC], F32)
    em.dma(b2_sb[:, :], b2P)
    l1s = em.sb([128, DC], F32)
    em.dma(l1s[:, :], ln1sP)
    l1b = em.sb([128, DC], F32)
    em.dma(l1b[:, :], ln1bP)
    l2s = em.sb([128, DC], F32)
    em.dma(l2s[:, :], ln2sP)
    l2b = em.sb([128, DC], F32)
    em.dma(l2b[:, :], ln2bP)

    accs = em.sb([128, 2 * DC], F32)
    pld = em.sb([128, 2 * DC], F32)
    em.dma(accs[:, :], pacc4[0])
    for j in range(1, 4):
        em.dma(pld[:, :], pacc4[j])
        em.tt(accs[:, :], accs[:, :], pld[:, :], OP.add)

    p_s1 = em.ps([1, nT])
    p_s2 = em.ps([1, nT])
    pbb = em.ps([128, nT])
    psK = em.ps([128, nT])
    psV = em.ps([128, nT])

    def load_w(dst_b, src_r):
        for c in range(DC):
            em.dma(xcol[:, c, :], src_r[c][:, 0:512])
        em.copy(dst_b[:, :, 0:512], xcol[:, :, :], eng="gpsimd")
        for c in range(DC):
            em.dma(xcol[:, c, :], src_r[c][:, 512:])
        em.copy(dst_b[:, :, 512:], xcol[:, :, :], eng="gpsimd")

    load_w(wA_b, wqT_r)
    load_w(wB_b, wo_r)
    attb = em.sb([128, DC, nT], BF16)
    r1 = em.sb([128, DC, nT], F32)
    x1f = em.sb([128, DC, nT], F32)
    x1b = em.sb([128, DC, nT], BF16)
    wstr = em.sb([128, DC, 128], F32)
    wstr_b = em.sb([128, DC, 128], BF16)
    w2str_b = em.sb([128, NFT, 128], BF16)
    h1b = em.sb([128, NFT, nT], BF16)
    yo = em.sb([128, nT], F32)

    for ti in range(TPC // nT):
        tsl = slice(ti * nT, (ti + 1) * nT)
        for c in range(DC):
            em.dma(xcol[:, c, :], xs_r[c, :, tsl])
        if variant == "first":
            rstd, negmr = ln_rows(em, xcol, nT, ones_f, qf, p_s1, p_s2)
            bcast_row(em, rstd, 128, nT, ones_f, rbb, pbb)
            bcast_row(em, negmr, 128, nT, ones_f, nbb, pbb)
            for c in range(DC):
                em.tt(scr[:, :], xcol[:, c, :], rbb[:, :], OP.mult)
                em.tt(xcol[:, c, :], scr[:, :], nbb[:, :], OP.add)
        em.copy(xb[:, :, :], xcol[:, :, :])
        for ct in range(DC):
            csl = slice(ct * 128, (ct + 1) * 128)
            for c in range(DC):
                em.mm(psK[:, :], wA_b[:, c, csl], xb[:, c, :], start=(c == 0), stop=(c == DC - 1))
            em.ts(t1[:, :], psK[:, :], bq_sb[:, ct:ct + 1], OP.add, 0.0, OP.min)
            em.act(e1[:, :], t1[:, :], AF.Exp)
            em.ts(t3[:, :], psK[:, :], bq_sb[:, ct:ct + 1], OP.add, 0.0, OP.max)
            em.tt(qf[:, ct, :], e1[:, :], t3[:, :], OP.add)
        for ct in range(DC):
            em.ts(t1[:, :], qf[:, ct, :], accs[:, DC + ct:DC + ct + 1], OP.mult, 1e-6, OP.add)
            em.recip(t1[:, :], t1[:, :])
            em.ts(t3[:, :], qf[:, ct, :], accs[:, ct:ct + 1], OP.mult)
            em.tt(attb[:, ct, :], t3[:, :], t1[:, :], OP.mult)
        for oc in range(DC):
            osl = slice(oc * 128, (oc + 1) * 128)
            for ct in range(DC):
                em.mm(psV[:, :], wB_b[:, ct, osl], attb[:, ct, :], start=(ct == 0), stop=(ct == DC - 1))
            em.act(e1[:, :], psV[:, :], AF.Gelu_apprx_tanh, bias=bo_sb[:, oc:oc + 1])
            em.tt(r1[:, oc, :], e1[:, :], xcol[:, oc, :], OP.add)
        rstd, negmr = ln_rows(em, r1, nT, ones_f, qf, p_s1, p_s2, eps=1e-6)
        bcast_row(em, rstd, 128, nT, ones_f, rbb, pbb)
        bcast_row(em, negmr, 128, nT, ones_f, nbb, pbb)
        for c in range(DC):
            em.tt(scr[:, :], r1[:, c, :], rbb[:, :], OP.mult)
            em.tt(scr[:, :], scr[:, :], nbb[:, :], OP.add)
            em.ts(x1f[:, c, :], scr[:, :], l1s[:, c:c + 1], OP.mult, l1b[:, c:c + 1], OP.add)
        em.copy(x1b[:, :, :], x1f[:, :, :])
        for ft in range(NFT):
            fsl = slice(ft * 128, (ft + 1) * 128)
            for c in range(DC):
                em.dma(wstr[:, c, :], w1_r[c][:, fsl])
            em.copy(wstr_b[:, :, :], wstr[:, :, :], eng="gpsimd")
            for c in range(DC):
                em.mm(psK[:, :], wstr_b[:, c, :], x1b[:, c, :], start=(c == 0), stop=(c == DC - 1))
            em.act(h1b[:, ft, :], psK[:, :], AF.Gelu_apprx_tanh, bias=b1_sb[:, ft:ft + 1])
        for oc in range(DC):
            osl = slice(oc * 128, (oc + 1) * 128)
            for grp in range(4):
                for j in range(DC):
                    fc = grp * DC + j
                    em.dma(wstr[:, j, :], w2_r[fc][:, osl])
                em.copy(w2str_b[:, grp * DC:(grp + 1) * DC, :], wstr[:, :, :], eng="gpsimd")
            for fc in range(NFT):
                em.mm(psV[:, :], w2str_b[:, fc, :], h1b[:, fc, :], start=(fc == 0), stop=(fc == NFT - 1))
            em.act(e1[:, :], psV[:, :], AF.Gelu_apprx_tanh, bias=b2_sb[:, oc:oc + 1])
            em.tt(r1[:, oc, :], e1[:, :], x1f[:, oc, :], OP.add)
        rstd, negmr = ln_rows(em, r1, nT, ones_f, qf, p_s1, p_s2, eps=1e-6)
        bcast_row(em, rstd, 128, nT, ones_f, rbb, pbb)
        bcast_row(em, negmr, 128, nT, ones_f, nbb, pbb)
        for c in range(DC):
            em.tt(scr[:, :], r1[:, c, :], rbb[:, :], OP.mult)
            em.tt(scr[:, :], scr[:, :], nbb[:, :], OP.add)
            em.ts(yo[:, :], scr[:, :], l2s[:, c:c + 1], OP.mult, l2b[:, c:c + 1], OP.add)
            if variant == "last":
                em.dma(t1[:, :], xtop_r[c, :, tsl])
                em.tt(yo[:, :], yo[:, :], t1[:, :], OP.add)
            em.dma(yT_r[c, :, tsl], yo[:, :])
    return em.emit()



def build_mla():
    em = Em("mla")
    xc = em.inp("xc", [D, S], F32)
    xs = em.inp("xs", [D, TPC], F32)
    wq = em.inp("wq", [D, D], F32)
    wkb = em.inp("wkb", [D, LAT], F32)
    R8T = em.inp("R8T", [D, D], F32)
    RT = em.inp("RT", [D, D], F32)
    wkaT = em.inp("wkaT", [D, LAT], F32)
    wvaT = em.inp("wvaT", [D, LAT], F32)
    wvbT = em.inp("wvbT", [LAT, D], F32)
    woT = em.inp("woT", [D, D], F32)
    yT = em.out("yT", [D, TPC], F32)

    xc_r = xc.rearrange("(c p) s -> c p s", p=128)
    xs_r = xs.rearrange("(c p) t -> c p t", p=128)
    yT_r = yT.rearrange("(c p) t -> c p t", p=128)
    wq_r = wq.rearrange("(c p) d -> c p d", p=128)
    wkb_r = wkb.rearrange("(c p) l -> c p l", p=128)
    R8T_r = R8T.rearrange("(c p) o -> c p o", p=128)
    RT_r = RT.rearrange("(c p) o -> c p o", p=128)
    wkaT_r = wkaT.rearrange("(c p) l -> c p l", p=128)
    wvaT_r = wvaT.rearrange("(c p) l -> c p l", p=128)
    wvbT_r = wvbT.rearrange("(c p) o -> c p o", p=128)
    woT_r = woT.rearrange("(c p) o -> c p o", p=128)

    wqT_d = em.dram("wqT_d", [D, D], BF16)
    qT_d = em.dram("qT_d", [NH, HD, TPC], BF16)
    attT_d = em.dram("attT_d", [D, TPC], BF16)
    wqT_dr = wqT_d.rearrange("(c p) o -> c p o", p=128)

    ones_f = em.sb([128, 128], F32)
    em.memset(ones_f[:, :], 1.0)

    wtile = em.sb([128, 128], F32)
    rtile = em.sb([128, 512], F32)
    qev = em.sb([128, 512], BF16)
    wkbT_b = em.sb([128, 2, D], BF16)
    pK = em.ps([128, 512])
    for dt_ in range(DC):
        for cg in range(2):
            for c in range(DC):
                em.dma(wtile[:, :], wq_r[c][:, dt_ * 128:(dt_ + 1) * 128])
                em.dma(rtile[:, :], R8T_r[c][:, cg * 512:(cg + 1) * 512])
                em.mm(pK[:, :], wtile[:, :], rtile[:, :], start=(c == 0), stop=(c == DC - 1))
            em.copy(qev[:, :], pK[:, :])
            em.dma(wqT_dr[dt_][:, cg * 512:(cg + 1) * 512], qev[:, :])
    for lt in range(2):
        for cg in range(2):
            for c in range(DC):
                em.dma(wtile[:, :], wkb_r[c][:, lt * 128:(lt + 1) * 128])
                em.dma(rtile[:, :], RT_r[c][:, cg * 512:(cg + 1) * 512])
                em.mm(pK[:, :], wtile[:, :], rtile[:, :], start=(c == 0), stop=(c == DC - 1))
            em.copy(wkbT_b[:, lt, cg * 512:(cg + 1) * 512], pK[:, :])
    wka_b = em.sb([128, DC, LAT], BF16)
    wva_b = em.sb([128, DC, LAT], BF16)
    wvbT_b = em.sb([128, 2, D], BF16)
    for c in range(DC):
        em.dma(rtile[:, :LAT], wkaT_r[c])
        em.copy(wka_b[:, c, :], rtile[:, :LAT], eng="gpsimd")
        em.dma(rtile[:, :LAT], wvaT_r[c])
        em.copy(wva_b[:, c, :], rtile[:, :LAT], eng="gpsimd")
    for lt in range(2):
        em.dma(rtile[:, :], wvbT_r[lt][:, 0:512])
        em.copy(wvbT_b[:, lt, 0:512], rtile[:, :], eng="gpsimd")
        em.dma(rtile[:, :], wvbT_r[lt][:, 512:])
        em.copy(wvbT_b[:, lt, 512:], rtile[:, :], eng="gpsimd")

    nT = 512
    xcol = em.sb([128, DC, nT], F32)
    xsq = em.sb([128, DC, nT], F32)
    xln = em.sb([128, DC, nT], BF16)
    rb = em.sb([128, nT], F32)
    nb = em.sb([128, nT], F32)
    scr = em.sb([128, nT], F32)
    klatT = em.sb([128, 2, S], BF16)
    vlatT = em.sb([128, 2, S], BF16)
    p_s1 = em.ps([1, nT])
    p_s2 = em.ps([1, nT])
    pb = em.ps([128, nT])
    for si in range(S // nT):
        ssl = slice(si * nT, (si + 1) * nT)
        for c in range(DC):
            em.dma(xcol[:, c, :], xc_r[c, :, ssl])
        rstd, negmr = ln_rows(em, xcol, nT, ones_f, xsq, p_s1, p_s2)
        bcast_row(em, rstd, 128, nT, ones_f, rb, pb)
        bcast_row(em, negmr, 128, nT, ones_f, nb, pb)
        apply_ln(em, xcol, xln, rb, nb, nT, scr)
        for lt in range(2):
            lsl = slice(lt * 128, (lt + 1) * 128)
            for c in range(DC):
                em.mm(pK[:, :], wka_b[:, c, lsl], xln[:, c, :], start=(c == 0), stop=(c == DC - 1))
            em.copy(klatT[:, lt, ssl], pK[:, :])
            for c in range(DC):
                em.mm(pK[:, :], wva_b[:, c, lsl], xln[:, c, :], start=(c == 0), stop=(c == DC - 1))
            em.copy(vlatT[:, lt, ssl], pK[:, :])
    qwt_b = em.sb([128, DC, 128], BF16)
    for ti in range(TPC // nT):
        tsl = slice(ti * nT, (ti + 1) * nT)
        for c in range(DC):
            em.dma(xcol[:, c, :], xs_r[c, :, tsl])
        rstd, negmr = ln_rows(em, xcol, nT, ones_f, xsq, p_s1, p_s2)
        bcast_row(em, rstd, 128, nT, ones_f, rb, pb)
        bcast_row(em, negmr, 128, nT, ones_f, nb, pb)
        apply_ln(em, xcol, xln, rb, nb, nT, scr)
        for oc in range(DC):
            osl = slice(oc * 128, (oc + 1) * 128)
            for c in range(DC):
                em.dma(qwt_b[:, c, :], wqT_dr[c][:, osl])
            for c in range(DC):
                em.mm(pK[:, :], qwt_b[:, c, :], xln[:, c, :], start=(c == 0), stop=(c == DC - 1))
            em.copy(qev[:, :], pK[:, :])
            em.dma(qT_d[2 * oc][:, tsl], qev[0:64, :])
            em.dma(qT_d[2 * oc + 1][:, tsl], qev[64:128, :])

    qh_sb = em.sb([64, TPC], BF16)
    KhT = em.sb([64, 16, nT], BF16)
    Vp = em.sb([128, 64, 132], BF16)
    pT = em.sb([128, nT], BF16)
    lrow = em.sb([1, nT], F32)
    rb64 = em.sb([64, nT], F32)
    attev = em.sb([64, nT], BF16)
    psS = em.ps([128, nT])
    psO = em.ps([128, nT])

    em.memset(Vp[:, :, 64:65], 1.0)
    em.memset(Vp[:, :, 130:131], 1.0)
    for pair in range(NH // 2):
        for st in range(64):
            for lt in range(2):
                em.mm(pK[:, 0:128], vlatT[:, lt, st * 128:(st + 1) * 128],
                      wvbT_b[:, lt, pair * 128:(pair + 1) * 128],
                      start=(lt == 0), stop=(lt == 1))
            em.copy(Vp[:, st, 0:64], pK[:, 0:64])
            em.copy(Vp[:, st, 66:130], pK[:, 64:128])
        for hh in range(2):
            h = pair * 2 + hh
            vofs = 0 if hh == 0 else 66
            em.dma(qh_sb[:, :], qT_d[h])
            for sj in range(S // nT):
                ssl = slice(sj * nT, (sj + 1) * nT)
                for lt in range(2):
                    em.mm(psS[0:64, :], wkbT_b[:, lt, h * 64:(h + 1) * 64],
                          klatT[:, lt, ssl], start=(lt == 0), stop=(lt == 1))
                em.copy(KhT[0:64, sj, :], psS[0:64, :])
            for tg in range(TPC // nT):
                gsl = slice(tg * nT, (tg + 1) * nT)
                for st in range(64):
                    em.mm(psS[:, :], KhT[0:64, st // 4, (st % 4) * 128:(st % 4) * 128 + 128],
                          qh_sb[0:64, gsl])
                    em.act(pT[:, :], psS[:, :], AF.Exp)
                    em.mm(psO[0:65, :], Vp[:, st, vofs:vofs + 65], pT[:, :],
                          start=(st == 0), stop=(st == 63))
                em.copy(lrow[:1, :], psO[64:65, :])
                em.recip(lrow[:1, :], lrow[:1, :])
                em.mm(pb[0:64, :], ones_f[0:1, 0:64], lrow[:1, :])
                em.copy(rb64[:, :], pb[0:64, :])
                em.tt(attev[:, :], psO[0:64, :], rb64[:, :], OP.mult)
                em.dma(attT_d[h * 64:(h + 1) * 64, gsl], attev[:, :])

    attld = em.sb([128, nT], BF16)
    yout = em.sb([128, nT], F32)
    xres = em.sb([128, nT], F32)
    wot_b = em.sb([128, DC, 128], BF16)
    for tg in range(TPC // nT):
        gsl = slice(tg * nT, (tg + 1) * nT)
        for oc in range(DC):
            osl = slice(oc * 128, (oc + 1) * 128)
            for cc in range(DC):
                em.dma(rtile[:, :128], woT_r[cc][:, osl])
                em.copy(wot_b[:, cc, :], rtile[:, :128])
                em.dma(attld[:, :], attT_d[cc * 128:(cc + 1) * 128, gsl])
                em.mm(pK[:, :], wot_b[:, cc, :], attld[:, :], start=(cc == 0), stop=(cc == DC - 1))
            em.dma(xres[:, :], xs_r[oc, :, gsl])
            em.tt(yout[:, :], pK[:, :], xres[:, :], OP.add)
            em.dma(yT_r[oc, :, gsl], yout[:, :])
    return em.emit()



def mha_group_nk(g):
    if g < 4:
        return 8 * g + 8
    return 64 - 8 * (g - 4)


def build_mha():
    em = Em("mha")
    xc = em.inp("xc", [D, S], F32)
    xs = em.inp("xs", [D, TPC], F32)
    wqkv = em.inp("wqkv", [D, 3 * D], F32)
    bqkvP = em.inp("bqkvP", [128, 24], F32)
    bqkvF = em.inp("bqkvF", [1, 3 * D], F32)
    wproj = em.inp("wproj", [D, D], F32)
    bpP = em.inp("bpP", [128, DC], F32)
    masks = em.inp("masks", [128, 64, 256], BF16)
    yT = em.out("yT", [D, TPC], F32)

    xc_r = xc.rearrange("(c p) s -> c p s", p=128)
    xs_r = xs.rearrange("(c p) t -> c p t", p=128)
    yT_r = yT.rearrange("(c p) t -> c p t", p=128)
    wqkv_r = wqkv.rearrange("(c p) o -> c p o", p=128)
    wproj_r = wproj.rearrange("(c p) o -> c p o", p=128)

    KT_d = em.dram("KT_d", [D, S], BF16)
    V_d = em.dram("V_d", [64, 128, D], BF16)
    qT_d = em.dram("qT_d", [NH, HD, TPC], BF16)
    attT_d = em.dram("attT_d", [D, TPC], BF16)
    KT_dr = KT_d.rearrange("(c p) s -> c p s", p=128)

    ones_f = em.sb([128, 128], F32)
    em.memset(ones_f[:, :], 1.0)

    wld = em.sb([128, DC, 512], F32)
    wbig_b = em.sb([128, DC, 2 * D], BF16)
    bqP_sb = em.sb([128, 24], F32)
    em.dma(bqP_sb[:, :], bqkvP)
    bqF_sb = em.sb([1, 3 * D], F32)
    em.dma(bqF_sb[:, :], bqkvF)
    bp_sb = em.sb([128, DC], F32)
    em.dma(bp_sb[:, :], bpP)
    msk_sb = em.sb([128, 64, 256], BF16)
    em.dma(msk_sb[:, :, :], masks)
    bcV = em.sb([128, D], F32)
    pb = em.ps([128, 512])
    for vg in range(2):
        em.mm(pb[:, :], ones_f[0:1, 0:128], bqF_sb[0:1, 2048 + vg * 512:2048 + (vg + 1) * 512])
        em.copy(bcV[:, vg * 512:(vg + 1) * 512], pb[:, :])

    def load_wbig(col0, ncols):
        for seg in range(ncols // 512):
            ssl_d = slice(col0 + seg * 512, col0 + (seg + 1) * 512)
            for c in range(DC):
                em.dma(wld[:, c, :], wqkv_r[c][:, ssl_d])
            em.copy(wbig_b[:, :, seg * 512:(seg + 1) * 512], wld[:, :, :], eng="gpsimd")

    nT = 512
    xcol = em.sb([128, DC, nT], F32)
    xsq = em.sb([128, DC, nT], F32)
    xln = em.sb([128, DC, nT], BF16)
    rb = em.sb([128, nT], F32)
    nb = em.sb([128, nT], F32)
    scr = em.sb([128, nT], F32)
    kvout = em.sb([128, nT], BF16)
    p_s1 = em.ps([1, nT])
    p_s2 = em.ps([1, nT])
    pK = em.ps([128, nT])
    pV = em.ps([128, nT])

    load_wbig(D, 2 * D)
    for si in range(S // nT):
        ssl = slice(si * nT, (si + 1) * nT)
        for c in range(DC):
            em.dma(xcol[:, c, :], xc_r[c, :, ssl])
        rstd, negmr = ln_rows(em, xcol, nT, ones_f, xsq, p_s1, p_s2)
        bcast_row(em, rstd, 128, nT, ones_f, rb, pb)
        bcast_row(em, negmr, 128, nT, ones_f, nb, pb)
        apply_ln(em, xcol, xln, rb, nb, nT, scr)
        for ct in range(DC):
            csl = slice(ct * 128, (ct + 1) * 128)
            for c in range(DC):
                em.mm(pK[:, :], wbig_b[:, c, csl], xln[:, c, :], start=(c == 0), stop=(c == DC - 1))
            em.ts(kvout[:, :], pK[:, :], bqP_sb[:, 8 + ct:9 + ct], OP.add)
            em.dma(KT_dr[ct][:, ssl], kvout[:, :])
        for sub in range(4):
            st = si * 4 + sub
            for vg in range(2):
                vsl = slice(D + vg * 512, D + (vg + 1) * 512)
                for c in range(DC):
                    em.mm(pV[:, :], xln[:, c, sub * 128:(sub + 1) * 128],
                          wbig_b[:, c, vsl], start=(c == 0), stop=(c == DC - 1))
                em.tt(kvout[:, :], pV[:, :], bcV[:, vg * 512:(vg + 1) * 512], OP.add)
                em.dma(V_d[st][:, vg * 512:(vg + 1) * 512], kvout[:, :])

    load_wbig(0, D)
    for ti in range(TPC // nT):
        tsl = slice(ti * nT, (ti + 1) * nT)
        for c in range(DC):
            em.dma(xcol[:, c, :], xs_r[c, :, tsl])
        rstd, negmr = ln_rows(em, xcol, nT, ones_f, xsq, p_s1, p_s2)
        bcast_row(em, rstd, 128, nT, ones_f, rb, pb)
        bcast_row(em, negmr, 128, nT, ones_f, nb, pb)
        apply_ln(em, xcol, xln, rb, nb, nT, scr)
        for oc in range(DC):
            osl = slice(oc * 128, (oc + 1) * 128)
            for c in range(DC):
                em.mm(pK[:, :], wbig_b[:, c, osl], xln[:, c, :], start=(c == 0), stop=(c == DC - 1))
            em.ts(scr[:, :], pK[:, :], bqP_sb[:, oc:oc + 1], OP.add)
            em.copy(kvout[:, :], scr[:, :])
            em.dma(qT_d[2 * oc][:, tsl], kvout[0:64, :])
            em.dma(qT_d[2 * oc + 1][:, tsl], kvout[64:128, :])

    nG = 256
    qh_sb = em.sb([64, TPC], BF16)
    KhT = em.sb([64, S], BF16)
    Vh = em.sb([128, 64, 65], BF16)
    pT = em.sb([128, nG], BF16)
    lrow = em.sb([1, nG], F32)
    rb64 = em.sb([64, nG], F32)
    attev = em.sb([64, nG], BF16)
    psS = em.ps([128, nG])
    psO = em.ps([128, nG])

    em.memset(Vh[:, :, 64:65], 1.0)
    for h in range(NH):
        em.dma(qh_sb[:, :], qT_d[h])
        em.dma(KhT[:, :], KT_d[h * 64:(h + 1) * 64, :])
        em.dma(Vh[:, :, 0:64], V_d.rearrange("st p c -> p st c")[:, :, h * 64:(h + 1) * 64])
        for g in range(8):
            gsl = slice(g * nG, (g + 1) * nG)
            NK = mha_group_nk(g)
            for st in range(NK):
                em.mm(psS[:, :], KhT[0:64, st * 128:(st + 1) * 128], qh_sb[0:64, gsl])
                j = st - (NK - 8)
                if j >= 0:
                    em.tt(psS[:, :], psS[:, :], msk_sb[:, g * 8 + j, :], OP.add)
                em.act(pT[:, :], psS[:, :], AF.Exp, scale=0.125)
                em.mm(psO[0:65, :], Vh[:, st, 0:65], pT[:, :], start=(st == 0), stop=(st == NK - 1))
            em.copy(lrow[:1, :], psO[64:65, :])
            em.recip(lrow[:1, :], lrow[:1, :])
            em.mm(pb[0:64, :nG], ones_f[0:1, 0:64], lrow[:1, :])
            em.copy(rb64[:, :], pb[0:64, :nG])
            em.tt(attev[:, :], psO[0:64, :], rb64[:, :], OP.mult)
            em.dma(attT_d[h * 64:(h + 1) * 64, gsl], attev[:, :])

    load_wbig_proj = True
    for seg in range(2):
        ssl_d = slice(seg * 512, (seg + 1) * 512)
        for c in range(DC):
            em.dma(wld[:, c, :], wproj_r[c][:, ssl_d])
        em.copy(wbig_b[:, :, seg * 512:(seg + 1) * 512], wld[:, :, :], eng="gpsimd")
    attld = em.sb([128, nT], BF16)
    yout = em.sb([128, nT], F32)
    xres = em.sb([128, nT], F32)
    for tg in range(TPC // nT):
        gsl = slice(tg * nT, (tg + 1) * nT)
        for oc in range(DC):
            osl = slice(oc * 128, (oc + 1) * 128)
            for cc in range(DC):
                em.dma(attld[:, :], attT_d[cc * 128:(cc + 1) * 128, gsl])
                em.mm(pK[:, :], wbig_b[:, cc, osl], attld[:, :], start=(cc == 0), stop=(cc == DC - 1))
            em.ts(scr[:, :], pK[:, :], bp_sb[:, oc:oc + 1], OP.add)
            em.dma(xres[:, :], xs_r[oc, :, gsl])
            em.tt(yout[:, :], scr[:, :], xres[:, :], OP.add)
            em.dma(yT_r[oc, :, gsl], yout[:, :])
    return em.emit()


_progs = {}


def get_prog(name, *args):
    key = (name,) + args
    if key not in _progs:
        _progs[key] = {"ffn": build_ffn, "moe": build_moe, "bank": build_bank,
                       "lab": build_lab, "lab_p1": build_lab_p1, "lab_p2": build_lab_p2,
                       "mla": build_mla, "mha": build_mha}[name](*args)
    return _progs[key]


def shard_xT(x_T):
    shards = []
    for core in range(NCORES):
        b = core // 4
        cols = [x_T[b][:, g * 128:(g + 1) * 128] for g in core_qtiles(core)]
        shards.append(np.ascontiguousarray(np.concatenate(cols, axis=1)))
    return shards


def unshard_xT(shards):
    out = np.zeros((B, D, S), np.float32)
    for core in range(NCORES):
        b = core // 4
        sh = np.asarray(shards[core])
        for j, g in enumerate(core_qtiles(core)):
            out[b][:, g * 128:(g + 1) * 128] = sh[:, j * 128:(j + 1) * 128]
    return out


def run_cores(nc, per_core, shared, out="yT"):
    in_maps = []
    for c in range(NCORES):
        m = dict(shared)
        m.update(per_core[c])
        in_maps.append(m)
    res = run_bass_kernel_spmd(nc, in_maps, core_ids=list(range(NCORES)))
    return [np.asarray(res.results[c][out]) for c in range(NCORES)]


def rope_R():
    R = np.eye(D, dtype=np.float64)
    for h in range(NH):
        for j in range(8):
            th = h * (10000.0 ** (-(2.0 * j) / HD))
            c, s = np.cos(th), np.sin(th)
            r0 = h * HD + 2 * j
            r1 = r0 + 1
            R[r0, r0] = c
            R[r0, r1] = -s
            R[r1, r0] = s
            R[r1, r1] = c
    return R


def mha_masks(core):
    import ml_dtypes
    qt = core_qtiles(core)
    M = np.zeros((128, 64, 256), np.float32)
    for g in range(8):
        NK = mha_group_nk(g)
        tglob = [qt[2 * g], qt[2 * g + 1]]
        qpos = np.concatenate([np.arange(t * 128, (t + 1) * 128) for t in tglob])
        for j in range(8):
            st = NK - 8 + j
            kpos = np.arange(st * 128, (st + 1) * 128)
            allowed = kpos[:, None] <= qpos[None, :]
            M[:, g * 8 + j, :] = np.where(allowed, 0.0, np.float32(-1e30))
    return M.astype(ml_dtypes.bfloat16)


def kernel(**inputs):
    g = {k: np.asarray(v) for k, v in inputs.items()}
    x = g["L0_attn_a_x"]
    xT = np.ascontiguousarray(x.transpose(0, 2, 1)).astype(np.float32, copy=False)
    shards = shard_xT(xT)

    def ctx(shards):
        xc = unshard_xT(shards)
        return [np.ascontiguousarray(xc[c // 4]) for c in range(NCORES)]

    C = np.ascontiguousarray

    R = rope_R()
    nc = get_prog("mla")
    shared = dict(
        wq=g["L0_attn_a_wq"], wkb=g["L0_attn_a_wk_b"],
        R8T=C((R / 8.0).T.astype(np.float32)), RT=C(R.T.astype(np.float32)),
        wkaT=C(g["L0_attn_a_wk_a"].T), wvaT=C(g["L0_attn_a_wv_a"].T),
        wvbT=C(g["L0_attn_a_wv_b"].T), woT=C(g["L0_attn_a_wo"].T))
    xcs = ctx(shards)
    shards = run_cores(nc, [dict(xc=xcs[c], xs=shards[c]) for c in range(NCORES)], shared)

    nc = get_prog("ffn")
    sh_ffn = dict(wgT=C(g["L0_ffn_w_gate"].T), wuT=C(g["L0_ffn_w_up"].T),
                  wdT=C(g["L0_ffn_w_down"].T))
    shards = run_cores(nc, [dict(xs=shards[c]) for c in range(NCORES)], sh_ffn)

    nc = get_prog("mha")
    bq = g["L0_attn_b_b_qkv"]
    shared = dict(wqkv=g["L0_attn_b_w_qkv"],
                  bqkvP=C(bq.reshape(24, 128).T),
                  bqkvF=C(bq.reshape(1, 3 * D)),
                  wproj=g["L0_attn_b_w_proj"],
                  bpP=C(g["L0_attn_b_b_proj"].reshape(DC, 128).T))
    xcs = ctx(shards)
    shards = run_cores(nc, [dict(xc=xcs[c], xs=shards[c], masks=mha_masks(c))
                            for c in range(NCORES)], shared)

    nc = get_prog("ffn")
    shards = run_cores(nc, [dict(xs=shards[c]) for c in range(NCORES)], sh_ffn)

    nc = get_prog("bank")
    shared = dict(kT=C(g["L1_attn_a_k"].T), v=g["L1_attn_a_v"])
    shards = run_cores(nc, [dict(xs=shards[c]) for c in range(NCORES)], shared)

    nc = get_prog("moe")
    sh_moe = dict(rw=g["L1_ffn_router_w"], win=g["L1_ffn_w_in"], wout=g["L1_ffn_w_out"],
                  binP=C(g["L1_ffn_b_in"].T.reshape(DC, 128, NE).transpose(1, 0, 2)),
                  boutT=C(g["L1_ffn_b_out"].reshape(1, NE, D)),
                  ident=np.eye(128, dtype=np.float32))
    shards = run_cores(nc, [dict(xs=shards[c]) for c in range(NCORES)], sh_moe)

    xtop = shards

    def perpart(vec):
        return C(vec.reshape(DC, 128).T)

    for i in range(8):
        pvar = "first" if i == 0 else "rest"
        nc1 = get_prog("lab_p1", pvar)
        sh1 = dict(
            wkT=C(g["L1_attn_b_Wk"][i].transpose(1, 0, 2).reshape(D, D)),
            wvT=C(g["L1_attn_b_Wv"][i].transpose(1, 0, 2).reshape(D, D)),
            bkP=perpart(g["L1_attn_b_bk"][i].reshape(D)),
            bvP=perpart(g["L1_attn_b_bv"][i].reshape(D)))
        parts = run_cores(nc1, [dict(xs=shards[c]) for c in range(NCORES)],
                          sh1, out="pacc")
        variant = "first" if i == 0 else ("last" if i == 7 else "mid")
        nc = get_prog("lab_p2", variant)
        shared = dict(
            wqT=C(g["L1_attn_b_Wq"][i].transpose(1, 0, 2).reshape(D, D)),
            wo=g["L1_attn_b_Wo"][i],
            w1=g["L1_attn_b_W1"][i], w2=g["L1_attn_b_W2"][i],
            bqP=perpart(g["L1_attn_b_bq"][i].reshape(D)),
            boP=perpart(g["L1_attn_b_bo"][i]),
            b1P=C(g["L1_attn_b_b1"][i].reshape(NFT, 128).T),
            b2P=perpart(g["L1_attn_b_b2"][i]),
            ln1sP=perpart(g["L1_attn_b_ln1_s"][i]),
            ln1bP=perpart(g["L1_attn_b_ln1_b"][i]),
            ln2sP=perpart(g["L1_attn_b_ln2_s"][i]),
            ln2bP=perpart(g["L1_attn_b_ln2_b"][i]))
        per_core = []
        for c in range(NCORES):
            grp = (c // 4) * 4
            pstack = C(np.stack([parts[j] for j in range(grp, grp + 4)]))
            m = dict(xs=shards[c], pacc4=pstack)
            if variant == "last":
                m["xtop"] = xtop[c]
            per_core.append(m)
        shards = run_cores(nc, per_core, shared)

    nc = get_prog("moe")
    shards = run_cores(nc, [dict(xs=shards[c]) for c in range(NCORES)], sh_moe)

    out = unshard_xT(shards)
    return np.ascontiguousarray(out.transpose(0, 2, 1))


# version 06
# speedup vs baseline: 1.5618x; 1.4427x over previous
import numpy as np
import contextlib
import concourse.bass as bass
import concourse.mybir as mybir
from concourse.alu_op_type import AluOpType
from concourse.bass_utils import run_bass_kernel_spmd

F32 = mybir.dt.float32
BF16 = mybir.dt.bfloat16
I32 = mybir.dt.int32
AF = mybir.ActivationFunctionType
AX = mybir.AxisListType
OP = AluOpType

NCORES = 8
B = 2
S = 8192
D = 1024
TPC = 2048
NQT = 16
DC = D // 128

def core_qtiles(core):
    cc = core % 4
    return [4 * k + cc for k in range(8)] + [63 - 4 * k - cc for k in range(8)]


class Em:
    ENG = ("tensor", "vector", "scalar", "gpsimd", "sync")

    def __init__(self, name):
        self.nc = bass.Bass("TRN2", target_bir_lowering=False)
        self.stack = contextlib.ExitStack()
        self.ops = []
        self.counts = {}
        self.last = None
        self.n_sb = 0

    def inp(self, name, shape, dtype):
        return self.nc.dram_tensor(name, list(shape), dtype, kind="ExternalInput").ap()

    def out(self, name, shape, dtype):
        return self.nc.dram_tensor(name, list(shape), dtype, kind="ExternalOutput").ap()

    def dram(self, name, shape, dtype):
        return self.nc.dram_tensor(name, list(shape), dtype, kind="Internal").ap()

    def sb(self, shape, dtype, name=None):
        self.n_sb += 1
        return self.stack.enter_context(
            self.nc.sbuf_tensor(name or f"sb{self.n_sb}", list(shape), dtype))

    def ps(self, shape, dtype=F32, name=None):
        self.n_sb += 1
        return self.stack.enter_context(
            self.nc.psum_tensor(name or f"ps{self.n_sb}", list(shape), dtype))

    def sbc(self, name, shape, dtype):
        if not hasattr(self, "_cache"):
            self._cache = {}
        if name not in self._cache:
            self._cache[name] = self.sb(shape, dtype, name=name)
        return self._cache[name]

    def _rec(self, engine, kind, fn, deps):
        key = engine + ("_dma" if kind == "dma" else "")
        inc = 16 if kind == "dma" else 1
        self.counts[key] = self.counts.get(key, 0) + inc
        if deps is None:
            deps = [self.last] if self.last is not None else []
        deps = [d for d in deps if d is not None]
        op = dict(engine=engine, kind=kind, fn=fn, key=key,
                  cum=self.counts[key], deps=deps)
        self.ops.append(op)
        self.last = op
        return op

    def dma(self, dst, src, deps=None, eng="sync"):
        return self._rec(eng, "dma", lambda e: e.dma_start(out=dst, in_=src), deps)

    def mm(self, out, lhsT, rhs, start=True, stop=True, deps=None):
        return self._rec("tensor", "op",
                         lambda e: e.matmul(out=out, lhsT=lhsT, rhs=rhs, start=start,
                                            stop=stop, skip_group_check=True), deps)

    def transpose(self, out, in_, ident, deps=None):
        return self._rec("tensor", "op",
                         lambda e: e.transpose(out, in_, ident), deps)

    def act(self, out, in_, func, scale=1.0, bias=None, deps=None, accum_out=None):
        kw = {}
        if accum_out is not None:
            kw["accum_out"] = accum_out
        if bias is not None:
            assert not isinstance(bias, float), "float act bias needs const AP; use an AP"
            kw["bias"] = bias
        return self._rec("scalar", "op",
                         lambda e: e.activation(out=out, in_=in_, func=func,
                                                scale=scale, **kw), deps)

    def tt(self, out, a, b, op, deps=None, eng="vector"):
        return self._rec(eng, "op",
                         lambda e: e.tensor_tensor(out=out, in0=a, in1=b, op=op), deps)

    def ts(self, out, in0, s1, op0, s2=None, op1=None, deps=None, eng="vector"):
        if op1 is None:
            return self._rec(eng, "op",
                             lambda e: e.tensor_scalar(out=out, in0=in0, scalar1=s1,
                                                       scalar2=None, op0=op0), deps)
        return self._rec(eng, "op",
                         lambda e: e.tensor_scalar(out=out, in0=in0, scalar1=s1,
                                                   scalar2=s2, op0=op0, op1=op1), deps)

    def stt(self, out, in0, scalar, in1, op0, op1, deps=None):
        return self._rec("vector", "op",
                         lambda e: e.scalar_tensor_tensor(out=out, in0=in0, scalar=scalar,
                                                          in1=in1, op0=op0, op1=op1), deps)

    def red(self, out, in_, op, deps=None, negate=None):
        return self._rec("vector", "op",
                         lambda e: e.tensor_reduce(out=out, in_=in_, axis=AX.X, op=op,
                                                   negate=negate), deps)

    def recip(self, out, in_, deps=None):
        return self._rec("vector", "op", lambda e: e.reciprocal(out=out, in_=in_), deps)

    def copy(self, out, in_, deps=None, eng="vector"):
        return self._rec(eng, "op", lambda e: e.tensor_copy(out=out, in_=in_), deps)

    def memset(self, ap, v, deps=None, eng="vector"):
        return self._rec(eng, "op", lambda e: e.memset(ap, float(v)), deps)

    def coll(self, kind, op, ins, outs, replica_groups, deps=None):
        return self._rec("gpsimd", "dma",
                         lambda e: e.collective_compute(kind, op,
                                                        replica_groups=replica_groups,
                                                        ins=ins, outs=outs), deps)

    def emit(self):
        nc = self.nc
        sems = {}
        for k in self.counts:
            sems[k] = self.stack.enter_context(nc.semaphore(f"sem_{k}"))
        byeng = {e: [o for o in self.ops if o["engine"] == e] for e in self.ENG}
        finals = dict(self.counts)

        def run_stream(eng_obj, eng_name):
            water = {}
            for op in byeng[eng_name]:
                for d in op["deps"]:
                    if (d["key"] == op["key"] and eng_name == "tensor"
                            and d["kind"] == "op" and op["kind"] == "op"):
                        continue
                    if water.get(d["key"], -1) >= d["cum"]:
                        continue
                    eng_obj.wait_ge(sems[d["key"]], d["cum"])
                    water[d["key"]] = d["cum"]
                inst = op["fn"](eng_obj)
                inst.then_inc(sems[op["key"]], 16 if op["kind"] == "dma" else 1)
            dk = eng_name + "_dma"
            if dk in finals:
                eng_obj.wait_ge(sems[dk], finals[dk])

        with nc.Block() as block:
            @block.tensor
            def _(e):
                run_stream(e, "tensor")

            @block.vector
            def _(e):
                run_stream(e, "vector")

            @block.scalar
            def _(e):
                run_stream(e, "scalar")

            @block.gpsimd
            def _(e):
                run_stream(e, "gpsimd")

            @block.sync
            def _(e):
                run_stream(e, "sync")
        self.stack.close()
        return nc



def ln_rows(em, xcol, nT, ones_f, scratch_sq, psum, psq, eps=1e-5):
    for c in range(DC):
        em.mm(psum[:1, :], ones_f[:, :1], xcol[:, c, :], start=(c == 0), stop=(c == DC - 1))
    for c in range(DC):
        em.act(scratch_sq[:, c, :], xcol[:, c, :], AF.Square)
    for c in range(DC):
        em.mm(psq[:1, :], ones_f[:, :1], scratch_sq[:, c, :], start=(c == 0), stop=(c == DC - 1))
    m = em.sbc("lnr_m", [1, nT], F32)
    var = em.sbc("lnr_v", [1, nT], F32)
    rstd = em.sbc("lnr_r", [1, nT], F32)
    negmr = em.sbc("lnr_n", [1, nT], F32)
    em.ts(m[:1, :], psum[:1, :], 1.0 / D, OP.mult)
    m2 = em.sbc("lnr_m2", [1, nT], F32)
    em.tt(m2[:1, :], m[:1, :], m[:1, :], OP.mult)
    em.ts(m2[:1, :], m2[:1, :], float(eps), OP.subtract)
    em.stt(var[:1, :], psq[:1, :], 1.0 / D, m2[:1, :], OP.mult, OP.subtract)
    sq = em.sbc("lnr_sq", [1, nT], F32)
    em.act(sq[:1, :], var[:1, :], AF.Sqrt)
    em.recip(rstd[:1, :], sq[:1, :])
    em.tt(negmr[:1, :], m[:1, :], rstd[:1, :], OP.mult)
    em.ts(negmr[:1, :], negmr[:1, :], -1.0, OP.mult)
    return rstd, negmr


def bcast_row(em, row, nP, nT, ones_f, out_sb, pbank):
    em.mm(pbank[:nP, :nT], ones_f[:1, :nP], row[:1, :nT])
    em.copy(out_sb[:nP, :nT], pbank[:nP, :nT])


def apply_ln(em, xcol, out_bf, rstd_b, negmr_b, nT, scratch, scale_ap=None, bias_ap=None):
    for c in range(DC):
        em.tt(scratch[:, :nT], xcol[:, c, :], rstd_b[:, :nT], OP.mult)
        if scale_ap is None:
            em.tt(out_bf[:, c, :], scratch[:, :nT], negmr_b[:, :nT], OP.add)
        else:
            em.tt(scratch[:, :nT], scratch[:, :nT], negmr_b[:, :nT], OP.add)
            em.ts(out_bf[:, c, :], scratch[:, :nT], scale_ap(c), OP.mult,
                  bias_ap(c), OP.add)


H_FFN = 3072
NHT = H_FFN // 128


def build_ffn():
    em = Em("ffn")
    xs = em.inp("xs", [D, TPC], F32)
    wgT = em.inp("wgT", [D, H_FFN], I32)
    wuT = em.inp("wuT", [D, H_FFN], I32)
    wdT = em.inp("wdT", [H_FFN, D], I32)
    yT = em.out("yT", [D, TPC], F32)

    xs_r = xs.rearrange("(c p) t -> c p t", p=128)
    xs_p = xs.rearrange("(c p) t -> p c t", p=128)
    yT_r = yT.rearrange("(c p) t -> c p t", p=128)
    wgT_p = wgT.rearrange("(c p) h -> p c h", p=128)
    wuT_p = wuT.rearrange("(c p) h -> p c h", p=128)
    wdT_p = wdT.rearrange("(h p) o -> p h o", p=128)

    ones_f = em.sb([128, 128], F32)
    em.memset(ones_f[:, :], 1.0)

    nT = 512
    xcol = em.sb([128, DC, nT], F32)
    xsq = em.sb([128, DC, nT], F32)
    xln = em.sb([128, DC, nT], BF16)
    rb = em.sb([128, nT], F32)
    nb = em.sb([128, nT], F32)
    wg_i = em.sb([128, DC, 128], I32)
    wg_b = em.sb([128, DC, 128], BF16)
    wu_i = em.sb([128, DC, 128], I32)
    wu_b = em.sb([128, DC, 128], BF16)
    wd_i = em.sb([128, NHT, 128], I32)
    wd_b = em.sb([128, NHT, 128], BF16)
    sg = em.sb([128, nT], BF16)
    su = em.sb([128, nT], F32)
    hsb = em.sb([128, NHT, nT], BF16)
    xres = em.sb([128, nT], F32)
    yout = em.sb([128, nT], F32)

    p_s1 = em.ps([1, nT])
    p_s2 = em.ps([1, nT])
    pb = em.ps([128, nT])
    pg = em.ps([128, nT])
    pu = em.ps([128, nT])
    pd = em.ps([128, nT])

    for ti in range(TPC // nT):
        tsl = slice(ti * nT, (ti + 1) * nT)
        em.dma(xcol[:, :, :], xs_p[:, :, tsl])
        rstd, negmr = ln_rows(em, xcol, nT, ones_f, xsq, p_s1, p_s2)
        bcast_row(em, rstd, 128, nT, ones_f, rb, pb)
        bcast_row(em, negmr, 128, nT, ones_f, nb, pb)
        apply_ln(em, xcol, xln, rb, nb, nT, xres)
        for ht in range(NHT):
            hsl = slice(ht * 128, (ht + 1) * 128)
            em.dma(wg_i[:, :, :], wgT_p[:, :, hsl])
            em.dma(wu_i[:, :, :], wuT_p[:, :, hsl])
            em.copy(wg_b[:, :, :], wg_i[:, :, :], eng="gpsimd")
            em.copy(wu_b[:, :, :], wu_i[:, :, :], eng="gpsimd")
            for c in range(DC):
                em.mm(pg[:, :], wg_b[:, c, :], xln[:, c, :], start=(c == 0), stop=(c == DC - 1))
            for c in range(DC):
                em.mm(pu[:, :], wu_b[:, c, :], xln[:, c, :], start=(c == 0), stop=(c == DC - 1))
            em.act(sg[:, :], pg[:, :], AF.Silu, scale=1.0 / 127.0)
            em.ts(su[:, :], pu[:, :], 1.0 / 127.0, OP.mult)
            em.tt(hsb[:, ht, :], sg[:, :], su[:, :], OP.mult)
        for oc in range(DC):
            osl = slice(oc * 128, (oc + 1) * 128)
            em.dma(wd_i[:, :, :], wdT_p[:, :, osl])
            em.copy(wd_b[:, :, :], wd_i[:, :, :], eng="gpsimd")
            for ht in range(NHT):
                em.mm(pd[:, :], wd_b[:, ht, :], hsb[:, ht, :], start=(ht == 0), stop=(ht == NHT - 1))
            em.dma(xres[:, :], xs_r[oc, :, tsl])
            em.stt(yout[:, :], pd[:, :], 1.0 / 127.0, xres[:, :], OP.mult, OP.add)
            em.dma(yT_r[oc, :, tsl], yout[:, :])
    return em.emit()


NE = 8
NH = 16
HD = 64
LAT = 256


def build_moe():
    em = Em("moe")
    xs = em.inp("xs", [D, TPC], F32)
    rw = em.inp("rw", [D, NE], F32)
    win = em.inp("win", [NE, D, D], F32)
    wout = em.inp("wout", [NE, D, D], F32)
    binP = em.inp("binP", [128, DC, NE], F32)
    boutT = em.inp("boutT", [1, NE, D], F32)
    ident = em.inp("ident", [128, 128], F32)
    yT = em.out("yT", [D, TPC], F32)

    xs_r = xs.rearrange("(c p) t -> c p t", p=128)
    xs_p = xs.rearrange("(c p) t -> p c t", p=128)
    yT_r = yT.rearrange("(c p) t -> c p t", p=128)
    rw_p = rw.rearrange("(c p) e -> p c e", p=128)
    win_p = win.rearrange("e (c p) h -> e p c h", p=128)
    wout_p = wout.rearrange("e (c p) o -> e p c o", p=128)

    ones_f = em.sb([128, 128], F32)
    em.memset(ones_f[:, :], 1.0)
    id_sb = em.sb([128, 128], F32)
    em.dma(id_sb[:, :], ident)
    rw_f = em.sb([128, DC, NE], F32)
    em.dma(rw_f[:, :, :], rw_p)
    rw_sb = em.sb([128, DC, NE], BF16)
    em.copy(rw_sb[:, :, :], rw_f[:, :, :])
    binP_sb = em.sb([128, DC, NE], F32)
    em.dma(binP_sb[:, :, :], binP)
    boutT_sb = em.sb([1, NE, D], F32)
    em.dma(boutT_sb[:, :, :], boutT)

    nT = 512
    xcol = em.sb([128, DC, nT], F32)
    xsq = em.sb([128, DC, nT], F32)
    xln = em.sb([128, DC, nT], BF16)
    rb = em.sb([128, nT], F32)
    nb = em.sb([128, nT], F32)
    scr = em.sb([128, nT], F32)
    lg_sb = em.sb([8, nT], F32)
    ltok = em.sb([128, 4, 8], F32)
    ptok = em.sb([128, 4, 8], F32)
    wtok = em.sb([128, 4, 8], F32)
    tmp8 = em.sb([128, 8], F32)
    col1 = em.sb([128, 1], F32)
    wT_sb = em.sb([8, nT], F32)
    we_row = em.sb([1, nT], F32)
    web = em.sb([128, nT], F32)
    wf32 = em.sb([128, DC, D], F32)
    win_b = em.sb([128, DC, D], BF16)
    wout_b = em.sb([128, DC, D], BF16)
    hg = em.sb([128, nT], F32)
    hs = em.sb([128, DC, nT], BF16)
    yacc = em.sb([128, DC, nT], F32)
    xres = em.sb([128, nT], F32)
    yout = em.sb([128, nT], F32)

    p_s1 = em.ps([1, nT])
    p_s2 = em.ps([1, nT])
    pb = em.ps([128, nT])
    pr = em.ps([8, nT])
    ph = em.ps([128, nT])
    py = em.ps([128, nT])

    for ti in range(TPC // nT):
        tsl = slice(ti * nT, (ti + 1) * nT)
        em.dma(xcol[:, :, :], xs_p[:, :, tsl])
        rstd, negmr = ln_rows(em, xcol, nT, ones_f, xsq, p_s1, p_s2)
        bcast_row(em, rstd, 128, nT, ones_f, rb, pb)
        bcast_row(em, negmr, 128, nT, ones_f, nb, pb)
        apply_ln(em, xcol, xln, rb, nb, nT, scr)
        for c in range(DC):
            em.mm(pr[:8, :], rw_sb[:, c, :], xln[:, c, :], start=(c == 0), stop=(c == DC - 1))
        em.copy(lg_sb[:8, :], pr[:8, :])
        for q in range(nT // 128):
            qsl = slice(q * 128, (q + 1) * 128)
            em.transpose(pb[:, 0:8], lg_sb[0:8, qsl], id_sb[0:8, 0:8])
            em.copy(ltok[:, q, :], pb[:, 0:8])
            em.red(col1[:, :], ltok[:, q, :], OP.max)
            em.ts(tmp8[:, :], ltok[:, q, :], col1[:, 0:1], OP.subtract)
            em.act(tmp8[:, :], tmp8[:, :], AF.Exp)
            em.red(col1[:, :], tmp8[:, :], OP.add)
            em.recip(col1[:, :], col1[:, :])
            em.ts(ptok[:, q, :], tmp8[:, :], col1[:, 0:1], OP.mult)
            em.red(col1[:, :], ptok[:, q, :], OP.max)
            em.ts(tmp8[:, :], ptok[:, q, :], col1[:, 0:1], OP.is_ge)
            em.stt(tmp8[:, :], tmp8[:, :], -1e30, ptok[:, q, :], OP.mult, OP.add)
            em.red(col1[:, :], tmp8[:, :], OP.max)
            em.ts(tmp8[:, :], ptok[:, q, :], col1[:, 0:1], OP.is_ge)
            em.tt(wtok[:, q, :], ptok[:, q, :], tmp8[:, :], OP.mult)
            em.transpose(pr[:8, qsl], wtok[:, q, :], id_sb[0:128, 0:128])
        em.copy(wT_sb[:8, :], pr[:8, :])
        for e in range(NE):
            em.mm(p_s1[:1, :], id_sb[0:8, e:e + 1], wT_sb[0:8, :])
            em.copy(we_row[:1, :], p_s1[:1, :])
            em.mm(pb[:, :], ones_f[0:1, 0:128], we_row[:1, :])
            em.copy(web[:, :], pb[:, :])
            em.dma(wf32[:, :, :], win_p[e])
            em.copy(win_b[:, :, :], wf32[:, :, :], eng="gpsimd")
            for hc in range(DC):
                hsl = slice(hc * 128, (hc + 1) * 128)
                for c in range(DC):
                    em.mm(ph[:, :], win_b[:, c, hsl], xln[:, c, :], start=(c == 0), stop=(c == DC - 1))
                em.act(hg[:, :], ph[:, :], AF.Gelu_apprx_tanh, bias=binP_sb[:, hc, e:e + 1])
                em.tt(hs[:, hc, :], hg[:, :], web[:, :], OP.mult)
            em.dma(wf32[:, :, :], wout_p[e])
            em.copy(wout_b[:, :, :], wf32[:, :, :], eng="gpsimd")
            for oc in range(DC):
                osl = slice(oc * 128, (oc + 1) * 128)
                for hc in range(DC):
                    em.mm(py[:, :], wout_b[:, hc, osl], hs[:, hc, :], start=(hc == 0), stop=False)
                em.mm(py[:, :], boutT_sb[0:1, e, osl], we_row[:1, :], start=False, stop=True)
                if e == 0:
                    em.copy(yacc[:, oc, :], py[:, :])
                else:
                    em.tt(yacc[:, oc, :], yacc[:, oc, :], py[:, :], OP.add)
        for oc in range(DC):
            em.dma(xres[:, :], xs_r[oc, :, tsl])
            em.tt(yout[:, :], yacc[:, oc, :], xres[:, :], OP.add)
            em.dma(yT_r[oc, :, tsl], yout[:, :])
    return em.emit()


NBT = 64


def build_bank():
    em = Em("bank")
    xs = em.inp("xs", [D, TPC], F32)
    kT = em.inp("kT", [D, S], F32)
    v = em.inp("v", [S, D], F32)
    yT = em.out("yT", [D, TPC], F32)

    xs_r = xs.rearrange("(c p) t -> c p t", p=128)
    yT_r = yT.rearrange("(c p) t -> c p t", p=128)
    kT_r = kT.rearrange("(c p) s -> c p s", p=128)
    v_r = v.rearrange("(st p) o -> st p o", p=128)

    kTb2 = em.dram("kTb2", [NBT, DC, 128, 128], BF16)
    vb = em.dram("vb", [NBT, 128, D], BF16)
    kTb2_rd = kTb2.rearrange("n c p m -> n p c m")

    ones_f = em.sb([128, 128], F32)
    em.memset(ones_f[:, :], 1.0)
    ones_b = em.sb([128, 1], BF16)
    em.memset(ones_b[:, :], 1.0)

    cast_f = em.sb([128, 2048], F32)
    cast_b = em.sb([128, 2048], BF16)
    for c in range(DC):
        for j in range(4):
            ssl = slice(j * 2048, (j + 1) * 2048)
            em.dma(cast_f[:, :], kT_r[c][:, ssl])
            em.copy(cast_b[:, :], cast_f[:, :])
            for q in range(16):
                nt = j * 16 + q
                em.dma(kTb2[nt, c], cast_b[:, q * 128:(q + 1) * 128])
    for st in range(NBT):
        em.dma(cast_f[:, :1024], v_r[st])
        em.copy(cast_b[:, :1024], cast_f[:, :1024])
        em.dma(vb[st], cast_b[:, :1024])

    nT = 512
    xcol = em.sb([128, DC, nT], F32)
    xsq = em.sb([128, DC, nT], F32)
    rb = em.sb([128, nT], F32)
    nb = em.sb([128, nT], F32)
    scr = em.sb([128, nT], F32)
    xlnb = em.sb([128, DC, TPC], BF16)
    p_s1 = em.ps([1, nT])
    p_s2 = em.ps([1, nT])
    pbB = em.ps([128, nT])
    for ti in range(TPC // nT):
        tsl = slice(ti * nT, (ti + 1) * nT)
        for c in range(DC):
            em.dma(xcol[:, c, :], xs_r[c, :, tsl])
        rstd, negmr = ln_rows(em, xcol, nT, ones_f, xsq, p_s1, p_s2)
        bcast_row(em, rstd, 128, nT, ones_f, rb, pbB)
        bcast_row(em, negmr, 128, nT, ones_f, nb, pbB)
        apply_ln(em, xcol, xlnb[:, :, tsl], rb, nb, nT, scr)

    nG = 256
    kt_sb = em.sb([128, DC, 128], BF16)
    sc_sb = em.sb([128, NBT, nG], F32)
    sqt = em.sb([128, nG], BF16)
    gt = em.sb([128, nG], BF16)
    vt_sb = em.sb([128, D], BF16)
    srow = em.sb([1, nG], F32)
    cb = em.sb([128, nG], F32)
    g1 = em.sb([128, nG], F32)
    yout = em.sb([128, nG], F32)
    xres = em.sb([128, nG], F32)
    psC = em.ps([128, nG])
    py4 = [em.ps([128, nG]) for _ in range(4)]

    SQRT_N = float(np.sqrt(S))
    for tg in range(TPC // nG):
        gsl = slice(tg * nG, (tg + 1) * nG)
        for nt in range(NBT):
            em.dma(kt_sb[:, :, :], kTb2_rd[nt])
            for c in range(DC):
                em.mm(psC[:, :], kt_sb[:, c, :], xlnb[:, c, gsl], start=(c == 0), stop=(c == DC - 1))
            em.copy(sc_sb[:, nt, :], psC[:, :])
            em.act(sqt[:, :], psC[:, :], AF.Square)
            em.mm(p_s1[:1, :nG], ones_b[:, 0:1], sqt[:, :], start=(nt == 0), stop=(nt == NBT - 1))
        em.copy(srow[:1, :], p_s1[:1, :nG])
        em.act(srow[:1, :], srow[:1, :], AF.Sqrt)
        em.recip(srow[:1, :], srow[:1, :])
        em.ts(srow[:1, :], srow[:1, :], SQRT_N, OP.mult)
        em.mm(pbB[:, :nG], ones_f[0:1, 0:128], srow[:1, :])
        em.copy(cb[:, :], pbB[:, :nG])
        for half in range(2):
            for nt in range(NBT):
                em.dma(vt_sb[:, :], vb[nt])
                em.tt(g1[:, :], sc_sb[:, nt, :], cb[:, :], OP.mult)
                em.act(gt[:, :], g1[:, :], AF.Gelu)
                for j in range(4):
                    oc = half * 4 + j
                    em.mm(py4[j][:, :], vt_sb[:, oc * 128:(oc + 1) * 128], gt[:, :],
                          start=(nt == 0), stop=(nt == NBT - 1))
            for j in range(4):
                oc = half * 4 + j
                em.dma(xres[:, :], xs_r[oc, :, gsl])
                em.tt(yout[:, :], py4[j][:, :], xres[:, :], OP.add)
                em.dma(yT_r[oc, :, gsl], yout[:, :])
    return em.emit()


F_LAB = 4096
NFT = F_LAB // 128



def build_lab_p1(variant):
    em = Em("lab_p1_" + variant)
    xs = em.inp("xs", [D, TPC], F32)
    wkT = em.inp("wkT", [D, D], F32)
    wvT = em.inp("wvT", [D, D], F32)
    bkP = em.inp("bkP", [128, DC], F32)
    bvP = em.inp("bvP", [128, DC], F32)
    pacc = em.out("pacc", [128, 2 * DC], F32)

    xs_p = xs.rearrange("(c p) t -> p c t", p=128)
    wkT_p = wkT.rearrange("(c p) o -> p c o", p=128)
    wvT_p = wvT.rearrange("(c p) o -> p c o", p=128)

    ones_f = em.sb([128, 128], F32)
    em.memset(ones_f[:, :], 1.0)

    nT = 512
    xcol = em.sb([128, DC, nT], F32)
    qf = em.sb([128, DC, nT], F32)
    xb = em.sb([128, DC, nT], BF16)
    rbb = em.sb([128, nT], F32)
    nbb = em.sb([128, nT], F32)
    scr = em.sb([128, nT], F32)
    t1 = em.sb([128, nT], F32)
    e1 = em.sb([128, nT], F32)
    t3 = em.sb([128, nT], F32)
    kf = em.sb([128, nT], F32)
    vv = em.sb([128, nT], F32)
    red1 = em.sb([128, 1], F32)
    accs = em.sb([128, 2 * DC], F32)
    wA_b = em.sb([128, DC, D], BF16)
    wB_b = em.sb([128, DC, D], BF16)

    bk_sb = em.sb([128, DC], F32)
    em.dma(bk_sb[:, :], bkP)
    bv_sb = em.sb([128, DC], F32)
    em.dma(bv_sb[:, :], bvP)

    p_s1 = em.ps([1, nT])
    p_s2 = em.ps([1, nT])
    pbb = em.ps([128, nT])
    psK = em.ps([128, nT])
    psV = em.ps([128, nT])

    def load_w(dst_b, src_p):
        em.dma(xcol[:, :, :], src_p[:, :, 0:512])
        em.copy(dst_b[:, :, 0:512], xcol[:, :, :], eng="gpsimd")
        em.dma(xcol[:, :, :], src_p[:, :, 512:])
        em.copy(dst_b[:, :, 512:], xcol[:, :, :], eng="gpsimd")

    load_w(wA_b, wkT_p)
    load_w(wB_b, wvT_p)

    for si in range(TPC // nT):
        ssl = slice(si * nT, (si + 1) * nT)
        em.dma(xcol[:, :, :], xs_p[:, :, ssl])
        if variant == "first":
            rstd, negmr = ln_rows(em, xcol, nT, ones_f, qf, p_s1, p_s2)
            bcast_row(em, rstd, 128, nT, ones_f, rbb, pbb)
            bcast_row(em, negmr, 128, nT, ones_f, nbb, pbb)
            apply_ln(em, xcol, xb, rbb, nbb, nT, scr)
        else:
            em.copy(xb[:, :, :], xcol[:, :, :])
        for ct in range(DC):
            csl = slice(ct * 128, (ct + 1) * 128)
            for c in range(DC):
                em.mm(psK[:, :], wA_b[:, c, csl], xb[:, c, :], start=(c == 0), stop=(c == DC - 1))
            for c in range(DC):
                em.mm(psV[:, :], wB_b[:, c, csl], xb[:, c, :], start=(c == 0), stop=(c == DC - 1))
            em.ts(t1[:, :], psK[:, :], bk_sb[:, ct:ct + 1], OP.add, 0.0, OP.min)
            em.act(e1[:, :], t1[:, :], AF.Exp)
            em.ts(t3[:, :], psK[:, :], bk_sb[:, ct:ct + 1], OP.add, 0.0, OP.max)
            em.tt(kf[:, :], e1[:, :], t3[:, :], OP.add)
            em.ts(vv[:, :], psV[:, :], bv_sb[:, ct:ct + 1], OP.add)
            em.tt(e1[:, :], kf[:, :], vv[:, :], OP.mult)
            em.red(red1[:, :], e1[:, :], OP.add)
            if si == 0:
                em.copy(accs[:, ct:ct + 1], red1[:, :])
            else:
                em.tt(accs[:, ct:ct + 1], accs[:, ct:ct + 1], red1[:, :], OP.add)
            em.red(red1[:, :], kf[:, :], OP.add)
            if si == 0:
                em.copy(accs[:, DC + ct:DC + ct + 1], red1[:, :])
            else:
                em.tt(accs[:, DC + ct:DC + ct + 1], accs[:, DC + ct:DC + ct + 1], red1[:, :], OP.add)
    em.dma(pacc, accs[:, :])
    return em.emit()


def build_lab_p2(variant):
    em = Em("lab_p2_" + variant)
    xs = em.inp("xs", [D, TPC], F32)
    pacc4 = em.inp("pacc4", [4, 128, 2 * DC], F32)
    wqT = em.inp("wqT", [D, D], F32)
    wo = em.inp("wo", [D, D], F32)
    w1 = em.inp("w1", [D, F_LAB], F32)
    w2 = em.inp("w2", [F_LAB, D], F32)
    bqP = em.inp("bqP", [128, DC], F32)
    boP = em.inp("boP", [128, DC], F32)
    b1P = em.inp("b1P", [128, NFT], F32)
    b2P = em.inp("b2P", [128, DC], F32)
    ln1sP = em.inp("ln1sP", [128, DC], F32)
    ln1bP = em.inp("ln1bP", [128, DC], F32)
    ln2sP = em.inp("ln2sP", [128, DC], F32)
    ln2bP = em.inp("ln2bP", [128, DC], F32)
    if variant == "last":
        xtop = em.inp("xtop", [D, TPC], F32)
        xtop_r = xtop.rearrange("(c p) t -> c p t", p=128)
    yT = em.out("yT", [D, TPC], F32)

    xs_p = xs.rearrange("(c p) t -> p c t", p=128)
    yT_r = yT.rearrange("(c p) t -> c p t", p=128)
    wqT_p = wqT.rearrange("(c p) o -> p c o", p=128)
    wo_p = wo.rearrange("(c p) o -> p c o", p=128)
    w1_p = w1.rearrange("(c p) f -> p c f", p=128)
    w2_p = w2.rearrange("(f p) o -> p f o", p=128)

    ones_f = em.sb([128, 128], F32)
    em.memset(ones_f[:, :], 1.0)

    nT = 512
    xcol = em.sb([128, DC, nT], F32)
    qf = em.sb([128, DC, nT], F32)
    xb = em.sb([128, DC, nT], BF16)
    rbb = em.sb([128, nT], F32)
    nbb = em.sb([128, nT], F32)
    scr = em.sb([128, nT], F32)
    t1 = em.sb([128, nT], F32)
    e1 = em.sb([128, nT], F32)
    t3 = em.sb([128, nT], F32)
    wA_b = em.sb([128, DC, D], BF16)
    wB_b = em.sb([128, DC, D], BF16)

    bq_sb = em.sb([128, DC], F32)
    em.dma(bq_sb[:, :], bqP)
    bo_sb = em.sb([128, DC], F32)
    em.dma(bo_sb[:, :], boP)
    b1_sb = em.sb([128, NFT], F32)
    em.dma(b1_sb[:, :], b1P)
    b2_sb = em.sb([128, DC], F32)
    em.dma(b2_sb[:, :], b2P)
    l1s = em.sb([128, DC], F32)
    em.dma(l1s[:, :], ln1sP)
    l1b = em.sb([128, DC], F32)
    em.dma(l1b[:, :], ln1bP)
    l2s = em.sb([128, DC], F32)
    em.dma(l2s[:, :], ln2sP)
    l2b = em.sb([128, DC], F32)
    em.dma(l2b[:, :], ln2bP)

    accs = em.sb([128, 2 * DC], F32)
    pld = em.sb([128, 2 * DC], F32)
    em.dma(accs[:, :], pacc4[0])
    for j in range(1, 4):
        em.dma(pld[:, :], pacc4[j])
        em.tt(accs[:, :], accs[:, :], pld[:, :], OP.add)

    p_s1 = em.ps([1, nT])
    p_s2 = em.ps([1, nT])
    pbb = em.ps([128, nT])
    psK = em.ps([128, nT])
    psV = em.ps([128, nT])

    def load_w(dst_b, src_p):
        em.dma(xcol[:, :, :], src_p[:, :, 0:512])
        em.copy(dst_b[:, :, 0:512], xcol[:, :, :], eng="gpsimd")
        em.dma(xcol[:, :, :], src_p[:, :, 512:])
        em.copy(dst_b[:, :, 512:], xcol[:, :, :], eng="gpsimd")

    load_w(wA_b, wqT_p)
    load_w(wB_b, wo_p)
    attb = em.sb([128, DC, nT], BF16)
    r1 = em.sb([128, DC, nT], F32)
    x1f = em.sb([128, DC, nT], F32)
    x1b = em.sb([128, DC, nT], BF16)
    wstr = em.sb([128, DC, 128], F32)
    wstr_b = em.sb([128, DC, 128], BF16)
    w2str_b = em.sb([128, NFT, 128], BF16)
    h1b = em.sb([128, NFT, nT], BF16)
    yo = em.sb([128, nT], F32)

    for ti in range(TPC // nT):
        tsl = slice(ti * nT, (ti + 1) * nT)
        em.dma(xcol[:, :, :], xs_p[:, :, tsl])
        if variant == "first":
            rstd, negmr = ln_rows(em, xcol, nT, ones_f, qf, p_s1, p_s2)
            bcast_row(em, rstd, 128, nT, ones_f, rbb, pbb)
            bcast_row(em, negmr, 128, nT, ones_f, nbb, pbb)
            for c in range(DC):
                em.tt(scr[:, :], xcol[:, c, :], rbb[:, :], OP.mult)
                em.tt(xcol[:, c, :], scr[:, :], nbb[:, :], OP.add)
        em.copy(xb[:, :, :], xcol[:, :, :])
        for ct in range(DC):
            csl = slice(ct * 128, (ct + 1) * 128)
            for c in range(DC):
                em.mm(psK[:, :], wA_b[:, c, csl], xb[:, c, :], start=(c == 0), stop=(c == DC - 1))
            em.ts(t1[:, :], psK[:, :], bq_sb[:, ct:ct + 1], OP.add, 0.0, OP.min)
            em.act(e1[:, :], t1[:, :], AF.Exp)
            em.ts(t3[:, :], psK[:, :], bq_sb[:, ct:ct + 1], OP.add, 0.0, OP.max)
            em.tt(qf[:, ct, :], e1[:, :], t3[:, :], OP.add)
        for ct in range(DC):
            em.ts(t1[:, :], qf[:, ct, :], accs[:, DC + ct:DC + ct + 1], OP.mult, 1e-6, OP.add)
            em.recip(t1[:, :], t1[:, :])
            em.ts(t3[:, :], qf[:, ct, :], accs[:, ct:ct + 1], OP.mult)
            em.tt(attb[:, ct, :], t3[:, :], t1[:, :], OP.mult)
        for oc in range(DC):
            osl = slice(oc * 128, (oc + 1) * 128)
            for ct in range(DC):
                em.mm(psV[:, :], wB_b[:, ct, osl], attb[:, ct, :], start=(ct == 0), stop=(ct == DC - 1))
            em.act(e1[:, :], psV[:, :], AF.Gelu_apprx_tanh, bias=bo_sb[:, oc:oc + 1])
            em.tt(r1[:, oc, :], e1[:, :], xcol[:, oc, :], OP.add)
        rstd, negmr = ln_rows(em, r1, nT, ones_f, qf, p_s1, p_s2, eps=1e-6)
        bcast_row(em, rstd, 128, nT, ones_f, rbb, pbb)
        bcast_row(em, negmr, 128, nT, ones_f, nbb, pbb)
        for c in range(DC):
            em.tt(scr[:, :], r1[:, c, :], rbb[:, :], OP.mult)
            em.tt(scr[:, :], scr[:, :], nbb[:, :], OP.add)
            em.ts(x1f[:, c, :], scr[:, :], l1s[:, c:c + 1], OP.mult, l1b[:, c:c + 1], OP.add)
        em.copy(x1b[:, :, :], x1f[:, :, :])
        for ft in range(NFT):
            fsl = slice(ft * 128, (ft + 1) * 128)
            em.dma(wstr[:, :, :], w1_p[:, :, fsl])
            em.copy(wstr_b[:, :, :], wstr[:, :, :], eng="gpsimd")
            for c in range(DC):
                em.mm(psK[:, :], wstr_b[:, c, :], x1b[:, c, :], start=(c == 0), stop=(c == DC - 1))
            em.act(h1b[:, ft, :], psK[:, :], AF.Gelu_apprx_tanh, bias=b1_sb[:, ft:ft + 1])
        for oc in range(DC):
            osl = slice(oc * 128, (oc + 1) * 128)
            for grp in range(4):
                em.dma(wstr[:, :, :], w2_p[:, grp * DC:(grp + 1) * DC, osl])
                em.copy(w2str_b[:, grp * DC:(grp + 1) * DC, :], wstr[:, :, :], eng="gpsimd")
            for fc in range(NFT):
                em.mm(psV[:, :], w2str_b[:, fc, :], h1b[:, fc, :], start=(fc == 0), stop=(fc == NFT - 1))
            em.act(e1[:, :], psV[:, :], AF.Gelu_apprx_tanh, bias=b2_sb[:, oc:oc + 1])
            em.tt(r1[:, oc, :], e1[:, :], x1f[:, oc, :], OP.add)
        rstd, negmr = ln_rows(em, r1, nT, ones_f, qf, p_s1, p_s2, eps=1e-6)
        bcast_row(em, rstd, 128, nT, ones_f, rbb, pbb)
        bcast_row(em, negmr, 128, nT, ones_f, nbb, pbb)
        for c in range(DC):
            em.tt(scr[:, :], r1[:, c, :], rbb[:, :], OP.mult)
            em.tt(scr[:, :], scr[:, :], nbb[:, :], OP.add)
            em.ts(yo[:, :], scr[:, :], l2s[:, c:c + 1], OP.mult, l2b[:, c:c + 1], OP.add)
            if variant == "last":
                em.dma(t1[:, :], xtop_r[c, :, tsl])
                em.tt(yo[:, :], yo[:, :], t1[:, :], OP.add)
            em.dma(yT_r[c, :, tsl], yo[:, :])
    return em.emit()



def build_mla():
    em = Em("mla")
    xc = em.inp("xc", [D, S], F32)
    xs = em.inp("xs", [D, TPC], F32)
    wq = em.inp("wq", [D, D], F32)
    wkb = em.inp("wkb", [D, LAT], F32)
    R8T = em.inp("R8T", [D, D], F32)
    RT = em.inp("RT", [D, D], F32)
    wkaT = em.inp("wkaT", [D, LAT], F32)
    wvaT = em.inp("wvaT", [D, LAT], F32)
    wvbT = em.inp("wvbT", [LAT, D], F32)
    woT = em.inp("woT", [D, D], F32)
    yT = em.out("yT", [D, TPC], F32)

    xc_r = xc.rearrange("(c p) s -> c p s", p=128)
    xs_r = xs.rearrange("(c p) t -> c p t", p=128)
    yT_r = yT.rearrange("(c p) t -> c p t", p=128)
    wq_r = wq.rearrange("(c p) d -> c p d", p=128)
    wkb_r = wkb.rearrange("(c p) l -> c p l", p=128)
    R8T_r = R8T.rearrange("(c p) o -> c p o", p=128)
    RT_r = RT.rearrange("(c p) o -> c p o", p=128)
    wkaT_r = wkaT.rearrange("(c p) l -> c p l", p=128)
    wvaT_r = wvaT.rearrange("(c p) l -> c p l", p=128)
    wvbT_r = wvbT.rearrange("(c p) o -> c p o", p=128)
    woT_r = woT.rearrange("(c p) o -> c p o", p=128)

    wqT_d = em.dram("wqT_d", [D, D], BF16)
    qT_d = em.dram("qT_d", [NH, HD, TPC], BF16)
    attT_d = em.dram("attT_d", [D, TPC], BF16)
    wqT_dr = wqT_d.rearrange("(c p) o -> c p o", p=128)

    ones_f = em.sb([128, 128], F32)
    em.memset(ones_f[:, :], 1.0)

    wtile = em.sb([128, 128], F32)
    rtile = em.sb([128, 512], F32)
    qev = em.sb([128, 512], BF16)
    wkbT_b = em.sb([128, 2, D], BF16)
    pK = em.ps([128, 512])
    for dt_ in range(DC):
        for cg in range(2):
            for c in range(DC):
                em.dma(wtile[:, :], wq_r[c][:, dt_ * 128:(dt_ + 1) * 128])
                em.dma(rtile[:, :], R8T_r[c][:, cg * 512:(cg + 1) * 512])
                em.mm(pK[:, :], wtile[:, :], rtile[:, :], start=(c == 0), stop=(c == DC - 1))
            em.copy(qev[:, :], pK[:, :])
            em.dma(wqT_dr[dt_][:, cg * 512:(cg + 1) * 512], qev[:, :])
    for lt in range(2):
        for cg in range(2):
            for c in range(DC):
                em.dma(wtile[:, :], wkb_r[c][:, lt * 128:(lt + 1) * 128])
                em.dma(rtile[:, :], RT_r[c][:, cg * 512:(cg + 1) * 512])
                em.mm(pK[:, :], wtile[:, :], rtile[:, :], start=(c == 0), stop=(c == DC - 1))
            em.copy(wkbT_b[:, lt, cg * 512:(cg + 1) * 512], pK[:, :])
    wka_b = em.sb([128, DC, LAT], BF16)
    wva_b = em.sb([128, DC, LAT], BF16)
    wvbT_b = em.sb([128, 2, D], BF16)
    for c in range(DC):
        em.dma(rtile[:, :LAT], wkaT_r[c])
        em.copy(wka_b[:, c, :], rtile[:, :LAT], eng="gpsimd")
        em.dma(rtile[:, :LAT], wvaT_r[c])
        em.copy(wva_b[:, c, :], rtile[:, :LAT], eng="gpsimd")
    for lt in range(2):
        em.dma(rtile[:, :], wvbT_r[lt][:, 0:512])
        em.copy(wvbT_b[:, lt, 0:512], rtile[:, :], eng="gpsimd")
        em.dma(rtile[:, :], wvbT_r[lt][:, 512:])
        em.copy(wvbT_b[:, lt, 512:], rtile[:, :], eng="gpsimd")

    nT = 512
    xcol = em.sb([128, DC, nT], F32)
    xsq = em.sb([128, DC, nT], F32)
    xln = em.sb([128, DC, nT], BF16)
    rb = em.sb([128, nT], F32)
    nb = em.sb([128, nT], F32)
    scr = em.sb([128, nT], F32)
    klatT = em.sb([128, 2, S], BF16)
    vlatT = em.sb([128, 2, S], BF16)
    p_s1 = em.ps([1, nT])
    p_s2 = em.ps([1, nT])
    pb = em.ps([128, nT])
    for si in range(S // nT):
        ssl = slice(si * nT, (si + 1) * nT)
        for c in range(DC):
            em.dma(xcol[:, c, :], xc_r[c, :, ssl])
        rstd, negmr = ln_rows(em, xcol, nT, ones_f, xsq, p_s1, p_s2)
        bcast_row(em, rstd, 128, nT, ones_f, rb, pb)
        bcast_row(em, negmr, 128, nT, ones_f, nb, pb)
        apply_ln(em, xcol, xln, rb, nb, nT, scr)
        for lt in range(2):
            lsl = slice(lt * 128, (lt + 1) * 128)
            for c in range(DC):
                em.mm(pK[:, :], wka_b[:, c, lsl], xln[:, c, :], start=(c == 0), stop=(c == DC - 1))
            em.copy(klatT[:, lt, ssl], pK[:, :])
            for c in range(DC):
                em.mm(pK[:, :], wva_b[:, c, lsl], xln[:, c, :], start=(c == 0), stop=(c == DC - 1))
            em.copy(vlatT[:, lt, ssl], pK[:, :])
    qwt_b = em.sb([128, DC, 128], BF16)
    for ti in range(TPC // nT):
        tsl = slice(ti * nT, (ti + 1) * nT)
        for c in range(DC):
            em.dma(xcol[:, c, :], xs_r[c, :, tsl])
        rstd, negmr = ln_rows(em, xcol, nT, ones_f, xsq, p_s1, p_s2)
        bcast_row(em, rstd, 128, nT, ones_f, rb, pb)
        bcast_row(em, negmr, 128, nT, ones_f, nb, pb)
        apply_ln(em, xcol, xln, rb, nb, nT, scr)
        for oc in range(DC):
            osl = slice(oc * 128, (oc + 1) * 128)
            for c in range(DC):
                em.dma(qwt_b[:, c, :], wqT_dr[c][:, osl])
            for c in range(DC):
                em.mm(pK[:, :], qwt_b[:, c, :], xln[:, c, :], start=(c == 0), stop=(c == DC - 1))
            em.copy(qev[:, :], pK[:, :])
            em.dma(qT_d[2 * oc][:, tsl], qev[0:64, :])
            em.dma(qT_d[2 * oc + 1][:, tsl], qev[64:128, :])

    qh_sb = em.sb([64, TPC], BF16)
    KhT = em.sb([64, 16, nT], BF16)
    Vp = em.sb([128, 64, 132], BF16)
    pT = em.sb([128, nT], BF16)
    lrow = em.sb([1, nT], F32)
    rb64 = em.sb([64, nT], F32)
    attev = em.sb([64, nT], BF16)
    psS = em.ps([128, nT])
    psO = em.ps([128, nT])

    em.memset(Vp[:, :, 64:65], 1.0)
    em.memset(Vp[:, :, 130:131], 1.0)
    for pair in range(NH // 2):
        for st in range(64):
            for lt in range(2):
                em.mm(pK[:, 0:128], vlatT[:, lt, st * 128:(st + 1) * 128],
                      wvbT_b[:, lt, pair * 128:(pair + 1) * 128],
                      start=(lt == 0), stop=(lt == 1))
            em.copy(Vp[:, st, 0:64], pK[:, 0:64])
            em.copy(Vp[:, st, 66:130], pK[:, 64:128])
        for hh in range(2):
            h = pair * 2 + hh
            vofs = 0 if hh == 0 else 66
            em.dma(qh_sb[:, :], qT_d[h])
            for sj in range(S // nT):
                ssl = slice(sj * nT, (sj + 1) * nT)
                for lt in range(2):
                    em.mm(psS[0:64, :], wkbT_b[:, lt, h * 64:(h + 1) * 64],
                          klatT[:, lt, ssl], start=(lt == 0), stop=(lt == 1))
                em.copy(KhT[0:64, sj, :], psS[0:64, :])
            for tg in range(TPC // nT):
                gsl = slice(tg * nT, (tg + 1) * nT)
                for st in range(64):
                    em.mm(psS[:, :], KhT[0:64, st // 4, (st % 4) * 128:(st % 4) * 128 + 128],
                          qh_sb[0:64, gsl])
                    em.act(pT[:, :], psS[:, :], AF.Exp)
                    em.mm(psO[0:65, :], Vp[:, st, vofs:vofs + 65], pT[:, :],
                          start=(st == 0), stop=(st == 63))
                em.copy(lrow[:1, :], psO[64:65, :])
                em.recip(lrow[:1, :], lrow[:1, :])
                em.mm(pb[0:64, :], ones_f[0:1, 0:64], lrow[:1, :])
                em.copy(rb64[:, :], pb[0:64, :])
                em.tt(attev[:, :], psO[0:64, :], rb64[:, :], OP.mult)
                em.dma(attT_d[h * 64:(h + 1) * 64, gsl], attev[:, :])

    attld = em.sb([128, nT], BF16)
    yout = em.sb([128, nT], F32)
    xres = em.sb([128, nT], F32)
    wot_b = em.sb([128, DC, 128], BF16)
    for tg in range(TPC // nT):
        gsl = slice(tg * nT, (tg + 1) * nT)
        for oc in range(DC):
            osl = slice(oc * 128, (oc + 1) * 128)
            for cc in range(DC):
                em.dma(rtile[:, :128], woT_r[cc][:, osl])
                em.copy(wot_b[:, cc, :], rtile[:, :128])
                em.dma(attld[:, :], attT_d[cc * 128:(cc + 1) * 128, gsl])
                em.mm(pK[:, :], wot_b[:, cc, :], attld[:, :], start=(cc == 0), stop=(cc == DC - 1))
            em.dma(xres[:, :], xs_r[oc, :, gsl])
            em.tt(yout[:, :], pK[:, :], xres[:, :], OP.add)
            em.dma(yT_r[oc, :, gsl], yout[:, :])
    return em.emit()



def mha_group_nk(g):
    if g < 4:
        return 8 * g + 8
    return 64 - 8 * (g - 4)


def build_mha():
    em = Em("mha")
    xc = em.inp("xc", [D, S], F32)
    xs = em.inp("xs", [D, TPC], F32)
    wqkv = em.inp("wqkv", [D, 3 * D], F32)
    bqkvP = em.inp("bqkvP", [128, 24], F32)
    bqkvF = em.inp("bqkvF", [1, 3 * D], F32)
    wproj = em.inp("wproj", [D, D], F32)
    bpP = em.inp("bpP", [128, DC], F32)
    masks = em.inp("masks", [128, 64, 256], BF16)
    yT = em.out("yT", [D, TPC], F32)

    xc_r = xc.rearrange("(c p) s -> c p s", p=128)
    xs_r = xs.rearrange("(c p) t -> c p t", p=128)
    yT_r = yT.rearrange("(c p) t -> c p t", p=128)
    wqkv_r = wqkv.rearrange("(c p) o -> c p o", p=128)
    wproj_r = wproj.rearrange("(c p) o -> c p o", p=128)

    KT_d = em.dram("KT_d", [D, S], BF16)
    V_d = em.dram("V_d", [64, 128, D], BF16)
    qT_d = em.dram("qT_d", [NH, HD, TPC], BF16)
    attT_d = em.dram("attT_d", [D, TPC], BF16)
    KT_dr = KT_d.rearrange("(c p) s -> c p s", p=128)

    ones_f = em.sb([128, 128], F32)
    em.memset(ones_f[:, :], 1.0)

    wld = em.sb([128, DC, 512], F32)
    wbig_b = em.sb([128, DC, 2 * D], BF16)
    bqP_sb = em.sb([128, 24], F32)
    em.dma(bqP_sb[:, :], bqkvP)
    bqF_sb = em.sb([1, 3 * D], F32)
    em.dma(bqF_sb[:, :], bqkvF)
    bp_sb = em.sb([128, DC], F32)
    em.dma(bp_sb[:, :], bpP)
    msk_sb = em.sb([128, 64, 256], BF16)
    em.dma(msk_sb[:, :, :], masks)
    bcV = em.sb([128, D], F32)
    pb = em.ps([128, 512])
    for vg in range(2):
        em.mm(pb[:, :], ones_f[0:1, 0:128], bqF_sb[0:1, 2048 + vg * 512:2048 + (vg + 1) * 512])
        em.copy(bcV[:, vg * 512:(vg + 1) * 512], pb[:, :])

    def load_wbig(col0, ncols):
        for seg in range(ncols // 512):
            ssl_d = slice(col0 + seg * 512, col0 + (seg + 1) * 512)
            for c in range(DC):
                em.dma(wld[:, c, :], wqkv_r[c][:, ssl_d])
            em.copy(wbig_b[:, :, seg * 512:(seg + 1) * 512], wld[:, :, :], eng="gpsimd")

    nT = 512
    xcol = em.sb([128, DC, nT], F32)
    xsq = em.sb([128, DC, nT], F32)
    xln = em.sb([128, DC, nT], BF16)
    rb = em.sb([128, nT], F32)
    nb = em.sb([128, nT], F32)
    scr = em.sb([128, nT], F32)
    kvout = em.sb([128, nT], BF16)
    p_s1 = em.ps([1, nT])
    p_s2 = em.ps([1, nT])
    pK = em.ps([128, nT])
    pV = em.ps([128, nT])

    load_wbig(D, 2 * D)
    for si in range(S // nT):
        ssl = slice(si * nT, (si + 1) * nT)
        for c in range(DC):
            em.dma(xcol[:, c, :], xc_r[c, :, ssl])
        rstd, negmr = ln_rows(em, xcol, nT, ones_f, xsq, p_s1, p_s2)
        bcast_row(em, rstd, 128, nT, ones_f, rb, pb)
        bcast_row(em, negmr, 128, nT, ones_f, nb, pb)
        apply_ln(em, xcol, xln, rb, nb, nT, scr)
        for ct in range(DC):
            csl = slice(ct * 128, (ct + 1) * 128)
            for c in range(DC):
                em.mm(pK[:, :], wbig_b[:, c, csl], xln[:, c, :], start=(c == 0), stop=(c == DC - 1))
            em.ts(kvout[:, :], pK[:, :], bqP_sb[:, 8 + ct:9 + ct], OP.add)
            em.dma(KT_dr[ct][:, ssl], kvout[:, :])
        for sub in range(4):
            st = si * 4 + sub
            for vg in range(2):
                vsl = slice(D + vg * 512, D + (vg + 1) * 512)
                for c in range(DC):
                    em.mm(pV[:, :], xln[:, c, sub * 128:(sub + 1) * 128],
                          wbig_b[:, c, vsl], start=(c == 0), stop=(c == DC - 1))
                em.tt(kvout[:, :], pV[:, :], bcV[:, vg * 512:(vg + 1) * 512], OP.add)
                em.dma(V_d[st][:, vg * 512:(vg + 1) * 512], kvout[:, :])

    load_wbig(0, D)
    for ti in range(TPC // nT):
        tsl = slice(ti * nT, (ti + 1) * nT)
        for c in range(DC):
            em.dma(xcol[:, c, :], xs_r[c, :, tsl])
        rstd, negmr = ln_rows(em, xcol, nT, ones_f, xsq, p_s1, p_s2)
        bcast_row(em, rstd, 128, nT, ones_f, rb, pb)
        bcast_row(em, negmr, 128, nT, ones_f, nb, pb)
        apply_ln(em, xcol, xln, rb, nb, nT, scr)
        for oc in range(DC):
            osl = slice(oc * 128, (oc + 1) * 128)
            for c in range(DC):
                em.mm(pK[:, :], wbig_b[:, c, osl], xln[:, c, :], start=(c == 0), stop=(c == DC - 1))
            em.ts(scr[:, :], pK[:, :], bqP_sb[:, oc:oc + 1], OP.add)
            em.copy(kvout[:, :], scr[:, :])
            em.dma(qT_d[2 * oc][:, tsl], kvout[0:64, :])
            em.dma(qT_d[2 * oc + 1][:, tsl], kvout[64:128, :])

    nG = 256
    qh_sb = em.sb([64, TPC], BF16)
    KhT = em.sb([64, S], BF16)
    Vh = em.sb([128, 64, 65], BF16)
    pT = em.sb([128, nG], BF16)
    lrow = em.sb([1, nG], F32)
    rb64 = em.sb([64, nG], F32)
    attev = em.sb([64, nG], BF16)
    psS = em.ps([128, nG])
    psO = em.ps([128, nG])

    em.memset(Vh[:, :, 64:65], 1.0)
    for h in range(NH):
        em.dma(qh_sb[:, :], qT_d[h])
        em.dma(KhT[:, :], KT_d[h * 64:(h + 1) * 64, :])
        em.dma(Vh[:, :, 0:64], V_d.rearrange("st p c -> p st c")[:, :, h * 64:(h + 1) * 64])
        for g in range(8):
            gsl = slice(g * nG, (g + 1) * nG)
            NK = mha_group_nk(g)
            for st in range(NK):
                em.mm(psS[:, :], KhT[0:64, st * 128:(st + 1) * 128], qh_sb[0:64, gsl])
                j = st - (NK - 8)
                if j >= 0:
                    em.tt(psS[:, :], psS[:, :], msk_sb[:, g * 8 + j, :], OP.add)
                em.act(pT[:, :], psS[:, :], AF.Exp, scale=0.125)
                em.mm(psO[0:65, :], Vh[:, st, 0:65], pT[:, :], start=(st == 0), stop=(st == NK - 1))
            em.copy(lrow[:1, :], psO[64:65, :])
            em.recip(lrow[:1, :], lrow[:1, :])
            em.mm(pb[0:64, :nG], ones_f[0:1, 0:64], lrow[:1, :])
            em.copy(rb64[:, :], pb[0:64, :nG])
            em.tt(attev[:, :], psO[0:64, :], rb64[:, :], OP.mult)
            em.dma(attT_d[h * 64:(h + 1) * 64, gsl], attev[:, :])

    load_wbig_proj = True
    for seg in range(2):
        ssl_d = slice(seg * 512, (seg + 1) * 512)
        for c in range(DC):
            em.dma(wld[:, c, :], wproj_r[c][:, ssl_d])
        em.copy(wbig_b[:, :, seg * 512:(seg + 1) * 512], wld[:, :, :], eng="gpsimd")
    attld = em.sb([128, nT], BF16)
    yout = em.sb([128, nT], F32)
    xres = em.sb([128, nT], F32)
    for tg in range(TPC // nT):
        gsl = slice(tg * nT, (tg + 1) * nT)
        for oc in range(DC):
            osl = slice(oc * 128, (oc + 1) * 128)
            for cc in range(DC):
                em.dma(attld[:, :], attT_d[cc * 128:(cc + 1) * 128, gsl])
                em.mm(pK[:, :], wbig_b[:, cc, osl], attld[:, :], start=(cc == 0), stop=(cc == DC - 1))
            em.ts(scr[:, :], pK[:, :], bp_sb[:, oc:oc + 1], OP.add)
            em.dma(xres[:, :], xs_r[oc, :, gsl])
            em.tt(yout[:, :], scr[:, :], xres[:, :], OP.add)
            em.dma(yT_r[oc, :, gsl], yout[:, :])
    return em.emit()


_progs = {}


def get_prog(name, *args):
    key = (name,) + args
    if key not in _progs:
        _progs[key] = {"ffn": build_ffn, "moe": build_moe, "bank": build_bank,
                       "lab_p1": build_lab_p1, "lab_p2": build_lab_p2,
                       "mla": build_mla, "mha": build_mha}[name](*args)
    return _progs[key]


def shard_xT(x_T):
    shards = []
    for core in range(NCORES):
        b = core // 4
        cols = [x_T[b][:, g * 128:(g + 1) * 128] for g in core_qtiles(core)]
        shards.append(np.ascontiguousarray(np.concatenate(cols, axis=1)))
    return shards


def unshard_xT(shards):
    out = np.zeros((B, D, S), np.float32)
    for core in range(NCORES):
        b = core // 4
        sh = np.asarray(shards[core])
        for j, g in enumerate(core_qtiles(core)):
            out[b][:, g * 128:(g + 1) * 128] = sh[:, j * 128:(j + 1) * 128]
    return out


def run_cores(nc, per_core, shared, out="yT"):
    in_maps = []
    for c in range(NCORES):
        m = dict(shared)
        m.update(per_core[c])
        in_maps.append(m)
    res = run_bass_kernel_spmd(nc, in_maps, core_ids=list(range(NCORES)))
    return [np.asarray(res.results[c][out]) for c in range(NCORES)]


def rope_R():
    R = np.eye(D, dtype=np.float64)
    for h in range(NH):
        for j in range(8):
            th = h * (10000.0 ** (-(2.0 * j) / HD))
            c, s = np.cos(th), np.sin(th)
            r0 = h * HD + 2 * j
            r1 = r0 + 1
            R[r0, r0] = c
            R[r0, r1] = -s
            R[r1, r0] = s
            R[r1, r1] = c
    return R


def mha_masks(core):
    import ml_dtypes
    qt = core_qtiles(core)
    M = np.zeros((128, 64, 256), np.float32)
    for g in range(8):
        NK = mha_group_nk(g)
        tglob = [qt[2 * g], qt[2 * g + 1]]
        qpos = np.concatenate([np.arange(t * 128, (t + 1) * 128) for t in tglob])
        for j in range(8):
            st = NK - 8 + j
            kpos = np.arange(st * 128, (st + 1) * 128)
            allowed = kpos[:, None] <= qpos[None, :]
            M[:, g * 8 + j, :] = np.where(allowed, 0.0, np.float32(-1e30))
    return M.astype(ml_dtypes.bfloat16)


def kernel(**inputs):
    g = {k: np.asarray(v) for k, v in inputs.items()}
    x = g["L0_attn_a_x"]
    xT = np.ascontiguousarray(x.transpose(0, 2, 1)).astype(np.float32, copy=False)
    shards = shard_xT(xT)

    def ctx(shards):
        xc = unshard_xT(shards)
        return [np.ascontiguousarray(xc[c // 4]) for c in range(NCORES)]

    C = np.ascontiguousarray

    R = rope_R()
    nc = get_prog("mla")
    shared = dict(
        wq=g["L0_attn_a_wq"], wkb=g["L0_attn_a_wk_b"],
        R8T=C((R / 8.0).T.astype(np.float32)), RT=C(R.T.astype(np.float32)),
        wkaT=C(g["L0_attn_a_wk_a"].T), wvaT=C(g["L0_attn_a_wv_a"].T),
        wvbT=C(g["L0_attn_a_wv_b"].T), woT=C(g["L0_attn_a_wo"].T))
    xcs = ctx(shards)
    shards = run_cores(nc, [dict(xc=xcs[c], xs=shards[c]) for c in range(NCORES)], shared)

    nc = get_prog("ffn")
    sh_ffn = dict(wgT=C(g["L0_ffn_w_gate"].T), wuT=C(g["L0_ffn_w_up"].T),
                  wdT=C(g["L0_ffn_w_down"].T))
    shards = run_cores(nc, [dict(xs=shards[c]) for c in range(NCORES)], sh_ffn)

    nc = get_prog("mha")
    bq = g["L0_attn_b_b_qkv"]
    shared = dict(wqkv=g["L0_attn_b_w_qkv"],
                  bqkvP=C(bq.reshape(24, 128).T),
                  bqkvF=C(bq.reshape(1, 3 * D)),
                  wproj=g["L0_attn_b_w_proj"],
                  bpP=C(g["L0_attn_b_b_proj"].reshape(DC, 128).T))
    xcs = ctx(shards)
    shards = run_cores(nc, [dict(xc=xcs[c], xs=shards[c], masks=mha_masks(c))
                            for c in range(NCORES)], shared)

    nc = get_prog("ffn")
    shards = run_cores(nc, [dict(xs=shards[c]) for c in range(NCORES)], sh_ffn)

    nc = get_prog("bank")
    shared = dict(kT=C(g["L1_attn_a_k"].T), v=g["L1_attn_a_v"])
    shards = run_cores(nc, [dict(xs=shards[c]) for c in range(NCORES)], shared)

    nc = get_prog("moe")
    sh_moe = dict(rw=g["L1_ffn_router_w"], win=g["L1_ffn_w_in"], wout=g["L1_ffn_w_out"],
                  binP=C(g["L1_ffn_b_in"].T.reshape(DC, 128, NE).transpose(1, 0, 2)),
                  boutT=C(g["L1_ffn_b_out"].reshape(1, NE, D)),
                  ident=np.eye(128, dtype=np.float32))
    shards = run_cores(nc, [dict(xs=shards[c]) for c in range(NCORES)], sh_moe)

    xtop = shards

    def perpart(vec):
        return C(vec.reshape(DC, 128).T)

    for i in range(8):
        pvar = "first" if i == 0 else "rest"
        nc1 = get_prog("lab_p1", pvar)
        sh1 = dict(
            wkT=C(g["L1_attn_b_Wk"][i].transpose(1, 0, 2).reshape(D, D)),
            wvT=C(g["L1_attn_b_Wv"][i].transpose(1, 0, 2).reshape(D, D)),
            bkP=perpart(g["L1_attn_b_bk"][i].reshape(D)),
            bvP=perpart(g["L1_attn_b_bv"][i].reshape(D)))
        parts = run_cores(nc1, [dict(xs=shards[c]) for c in range(NCORES)],
                          sh1, out="pacc")
        variant = "first" if i == 0 else ("last" if i == 7 else "mid")
        nc = get_prog("lab_p2", variant)
        shared = dict(
            wqT=C(g["L1_attn_b_Wq"][i].transpose(1, 0, 2).reshape(D, D)),
            wo=g["L1_attn_b_Wo"][i],
            w1=g["L1_attn_b_W1"][i], w2=g["L1_attn_b_W2"][i],
            bqP=perpart(g["L1_attn_b_bq"][i].reshape(D)),
            boP=perpart(g["L1_attn_b_bo"][i]),
            b1P=C(g["L1_attn_b_b1"][i].reshape(NFT, 128).T),
            b2P=perpart(g["L1_attn_b_b2"][i]),
            ln1sP=perpart(g["L1_attn_b_ln1_s"][i]),
            ln1bP=perpart(g["L1_attn_b_ln1_b"][i]),
            ln2sP=perpart(g["L1_attn_b_ln2_s"][i]),
            ln2bP=perpart(g["L1_attn_b_ln2_b"][i]))
        per_core = []
        for c in range(NCORES):
            grp = (c // 4) * 4
            pstack = C(np.stack([parts[j] for j in range(grp, grp + 4)]))
            m = dict(xs=shards[c], pacc4=pstack)
            if variant == "last":
                m["xtop"] = xtop[c]
            per_core.append(m)
        shards = run_cores(nc, per_core, shared)

    nc = get_prog("moe")
    shards = run_cores(nc, [dict(xs=shards[c]) for c in range(NCORES)], sh_moe)

    out = unshard_xT(shards)
    return np.ascontiguousarray(out.transpose(0, 2, 1))


# version 07 (final)
# speedup vs baseline: 1.5777x; 1.0102x over previous
import numpy as np
import contextlib
import concourse.bass as bass
import concourse.mybir as mybir
from concourse.alu_op_type import AluOpType
from concourse.bass_utils import run_bass_kernel_spmd

F32 = mybir.dt.float32
BF16 = mybir.dt.bfloat16
I32 = mybir.dt.int32
AF = mybir.ActivationFunctionType
AX = mybir.AxisListType
OP = AluOpType

NCORES = 8
B = 2
S = 8192
D = 1024
TPC = 2048
NQT = 16
DC = D // 128

def core_qtiles(core):
    cc = core % 4
    return [4 * k + cc for k in range(8)] + [63 - 4 * k - cc for k in range(8)]


class Em:
    ENG = ("tensor", "vector", "scalar", "gpsimd", "sync")

    def __init__(self, name):
        self.nc = bass.Bass("TRN2", target_bir_lowering=False)
        self.stack = contextlib.ExitStack()
        self.ops = []
        self.counts = {}
        self.last = None
        self.n_sb = 0

    def inp(self, name, shape, dtype):
        return self.nc.dram_tensor(name, list(shape), dtype, kind="ExternalInput").ap()

    def out(self, name, shape, dtype):
        return self.nc.dram_tensor(name, list(shape), dtype, kind="ExternalOutput").ap()

    def dram(self, name, shape, dtype):
        return self.nc.dram_tensor(name, list(shape), dtype, kind="Internal").ap()

    def sb(self, shape, dtype, name=None):
        self.n_sb += 1
        return self.stack.enter_context(
            self.nc.sbuf_tensor(name or f"sb{self.n_sb}", list(shape), dtype))

    def ps(self, shape, dtype=F32, name=None):
        self.n_sb += 1
        return self.stack.enter_context(
            self.nc.psum_tensor(name or f"ps{self.n_sb}", list(shape), dtype))

    def sbc(self, name, shape, dtype):
        if not hasattr(self, "_cache"):
            self._cache = {}
        if name not in self._cache:
            self._cache[name] = self.sb(shape, dtype, name=name)
        return self._cache[name]

    def _rec(self, engine, kind, fn, deps):
        key = engine + ("_dma" if kind == "dma" else "")
        inc = 16 if kind == "dma" else 1
        self.counts[key] = self.counts.get(key, 0) + inc
        if deps is None:
            deps = [self.last] if self.last is not None else []
        deps = [d for d in deps if d is not None]
        op = dict(engine=engine, kind=kind, fn=fn, key=key,
                  cum=self.counts[key], deps=deps)
        self.ops.append(op)
        self.last = op
        return op

    def dma(self, dst, src, deps=None, eng="sync"):
        return self._rec(eng, "dma", lambda e: e.dma_start(out=dst, in_=src), deps)

    def mm(self, out, lhsT, rhs, start=True, stop=True, deps=None):
        return self._rec("tensor", "op",
                         lambda e: e.matmul(out=out, lhsT=lhsT, rhs=rhs, start=start,
                                            stop=stop, skip_group_check=True), deps)

    def transpose(self, out, in_, ident, deps=None):
        return self._rec("tensor", "op",
                         lambda e: e.transpose(out, in_, ident), deps)

    def act(self, out, in_, func, scale=1.0, bias=None, deps=None, accum_out=None):
        kw = {}
        if accum_out is not None:
            kw["accum_out"] = accum_out
        if bias is not None:
            assert not isinstance(bias, float), "float act bias needs const AP; use an AP"
            kw["bias"] = bias
        return self._rec("scalar", "op",
                         lambda e: e.activation(out=out, in_=in_, func=func,
                                                scale=scale, **kw), deps)

    def tt(self, out, a, b, op, deps=None, eng="vector"):
        return self._rec(eng, "op",
                         lambda e: e.tensor_tensor(out=out, in0=a, in1=b, op=op), deps)

    def ts(self, out, in0, s1, op0, s2=None, op1=None, deps=None, eng="vector"):
        if op1 is None:
            return self._rec(eng, "op",
                             lambda e: e.tensor_scalar(out=out, in0=in0, scalar1=s1,
                                                       scalar2=None, op0=op0), deps)
        return self._rec(eng, "op",
                         lambda e: e.tensor_scalar(out=out, in0=in0, scalar1=s1,
                                                   scalar2=s2, op0=op0, op1=op1), deps)

    def stt(self, out, in0, scalar, in1, op0, op1, deps=None):
        return self._rec("vector", "op",
                         lambda e: e.scalar_tensor_tensor(out=out, in0=in0, scalar=scalar,
                                                          in1=in1, op0=op0, op1=op1), deps)

    def red(self, out, in_, op, deps=None, negate=None):
        return self._rec("vector", "op",
                         lambda e: e.tensor_reduce(out=out, in_=in_, axis=AX.X, op=op,
                                                   negate=negate), deps)

    def recip(self, out, in_, deps=None):
        return self._rec("vector", "op", lambda e: e.reciprocal(out=out, in_=in_), deps)

    def copy(self, out, in_, deps=None, eng="vector"):
        return self._rec(eng, "op", lambda e: e.tensor_copy(out=out, in_=in_), deps)

    def memset(self, ap, v, deps=None, eng="vector"):
        return self._rec(eng, "op", lambda e: e.memset(ap, float(v)), deps)

    def coll(self, kind, op, ins, outs, replica_groups, deps=None):
        return self._rec("gpsimd", "dma",
                         lambda e: e.collective_compute(kind, op,
                                                        replica_groups=replica_groups,
                                                        ins=ins, outs=outs), deps)

    def emit(self):
        nc = self.nc
        sems = {}
        for k in self.counts:
            sems[k] = self.stack.enter_context(nc.semaphore(f"sem_{k}"))
        byeng = {e: [o for o in self.ops if o["engine"] == e] for e in self.ENG}
        finals = dict(self.counts)

        def run_stream(eng_obj, eng_name):
            water = {}
            for op in byeng[eng_name]:
                for d in op["deps"]:
                    if (d["key"] == op["key"] and eng_name == "tensor"
                            and d["kind"] == "op" and op["kind"] == "op"):
                        continue
                    if water.get(d["key"], -1) >= d["cum"]:
                        continue
                    eng_obj.wait_ge(sems[d["key"]], d["cum"])
                    water[d["key"]] = d["cum"]
                inst = op["fn"](eng_obj)
                inst.then_inc(sems[op["key"]], 16 if op["kind"] == "dma" else 1)
            dk = eng_name + "_dma"
            if dk in finals:
                eng_obj.wait_ge(sems[dk], finals[dk])

        with nc.Block() as block:
            @block.tensor
            def _(e):
                run_stream(e, "tensor")

            @block.vector
            def _(e):
                run_stream(e, "vector")

            @block.scalar
            def _(e):
                run_stream(e, "scalar")

            @block.gpsimd
            def _(e):
                run_stream(e, "gpsimd")

            @block.sync
            def _(e):
                run_stream(e, "sync")
        self.stack.close()
        return nc



def ln_rows(em, xcol, nT, ones_f, scratch_sq, psum, psq, eps=1e-5):
    for c in range(DC):
        em.mm(psum[:1, :], ones_f[:, :1], xcol[:, c, :], start=(c == 0), stop=(c == DC - 1))
    em.act(scratch_sq[:, :, :], xcol[:, :, :], AF.Square)
    for c in range(DC):
        em.mm(psq[:1, :], ones_f[:, :1], scratch_sq[:, c, :], start=(c == 0), stop=(c == DC - 1))
    m = em.sbc("lnr_m", [1, nT], F32)
    var = em.sbc("lnr_v", [1, nT], F32)
    rstd = em.sbc("lnr_r", [1, nT], F32)
    negmr = em.sbc("lnr_n", [1, nT], F32)
    em.ts(m[:1, :], psum[:1, :], 1.0 / D, OP.mult)
    m2 = em.sbc("lnr_m2", [1, nT], F32)
    em.tt(m2[:1, :], m[:1, :], m[:1, :], OP.mult)
    em.ts(m2[:1, :], m2[:1, :], float(eps), OP.subtract)
    em.stt(var[:1, :], psq[:1, :], 1.0 / D, m2[:1, :], OP.mult, OP.subtract)
    sq = em.sbc("lnr_sq", [1, nT], F32)
    em.act(sq[:1, :], var[:1, :], AF.Sqrt)
    em.recip(rstd[:1, :], sq[:1, :])
    em.tt(negmr[:1, :], m[:1, :], rstd[:1, :], OP.mult)
    em.ts(negmr[:1, :], negmr[:1, :], -1.0, OP.mult)
    return rstd, negmr


def bcast_row(em, row, nP, nT, ones_f, out_sb, pbank):
    em.mm(pbank[:nP, :nT], ones_f[:1, :nP], row[:1, :nT])
    em.copy(out_sb[:nP, :nT], pbank[:nP, :nT])


def apply_ln(em, xcol, out_bf, rstd_b, negmr_b, nT, scratch, scale_ap=None, bias_ap=None):
    for c in range(DC):
        em.tt(scratch[:, :nT], xcol[:, c, :], rstd_b[:, :nT], OP.mult)
        if scale_ap is None:
            em.tt(out_bf[:, c, :], scratch[:, :nT], negmr_b[:, :nT], OP.add)
        else:
            em.tt(scratch[:, :nT], scratch[:, :nT], negmr_b[:, :nT], OP.add)
            em.ts(out_bf[:, c, :], scratch[:, :nT], scale_ap(c), OP.mult,
                  bias_ap(c), OP.add)


H_FFN = 3072
NHT = H_FFN // 128


def build_ffn():
    em = Em("ffn")
    xs = em.inp("xs", [D, TPC], F32)
    wgT = em.inp("wgT", [D, H_FFN], I32)
    wuT = em.inp("wuT", [D, H_FFN], I32)
    wdT = em.inp("wdT", [H_FFN, D], I32)
    yT = em.out("yT", [D, TPC], F32)

    xs_r = xs.rearrange("(c p) t -> c p t", p=128)
    xs_p = xs.rearrange("(c p) t -> p c t", p=128)
    xs_p = xs.rearrange("(c p) t -> p c t", p=128)
    yT_r = yT.rearrange("(c p) t -> c p t", p=128)
    wgT_p = wgT.rearrange("(c p) h -> p c h", p=128)
    wuT_p = wuT.rearrange("(c p) h -> p c h", p=128)
    wdT_p = wdT.rearrange("(h p) o -> p h o", p=128)

    ones_f = em.sb([128, 128], F32)
    em.memset(ones_f[:, :], 1.0)

    nT = 512
    xcol = em.sb([128, DC, nT], F32)
    xsq = em.sb([128, DC, nT], F32)
    xln = em.sb([128, DC, nT], BF16)
    rb = em.sb([128, nT], F32)
    nb = em.sb([128, nT], F32)
    wg_i = em.sb([128, DC, 128], I32)
    wg_b = em.sb([128, DC, 128], BF16)
    wu_i = em.sb([128, DC, 128], I32)
    wu_b = em.sb([128, DC, 128], BF16)
    wd_i = em.sb([128, NHT, 128], I32)
    wd_b = em.sb([128, NHT, 128], BF16)
    sg = em.sb([128, nT], BF16)
    su = em.sb([128, nT], F32)
    hsb = em.sb([128, NHT, nT], BF16)
    xres = em.sb([128, nT], F32)
    yout = em.sb([128, nT], F32)

    p_s1 = em.ps([1, nT])
    p_s2 = em.ps([1, nT])
    pb = em.ps([128, nT])
    pg = em.ps([128, nT])
    pu = em.ps([128, nT])
    pd = em.ps([128, nT])

    for ti in range(TPC // nT):
        tsl = slice(ti * nT, (ti + 1) * nT)
        em.dma(xcol[:, :, :], xs_p[:, :, tsl])
        rstd, negmr = ln_rows(em, xcol, nT, ones_f, xsq, p_s1, p_s2)
        bcast_row(em, rstd, 128, nT, ones_f, rb, pb)
        bcast_row(em, negmr, 128, nT, ones_f, nb, pb)
        apply_ln(em, xcol, xln, rb, nb, nT, xres)
        for ht in range(NHT):
            hsl = slice(ht * 128, (ht + 1) * 128)
            em.dma(wg_i[:, :, :], wgT_p[:, :, hsl])
            em.dma(wu_i[:, :, :], wuT_p[:, :, hsl])
            em.copy(wg_b[:, :, :], wg_i[:, :, :], eng="gpsimd")
            em.copy(wu_b[:, :, :], wu_i[:, :, :], eng="gpsimd")
            for c in range(DC):
                em.mm(pg[:, :], wg_b[:, c, :], xln[:, c, :], start=(c == 0), stop=(c == DC - 1))
            for c in range(DC):
                em.mm(pu[:, :], wu_b[:, c, :], xln[:, c, :], start=(c == 0), stop=(c == DC - 1))
            em.act(sg[:, :], pg[:, :], AF.Silu, scale=1.0 / 127.0)
            em.ts(su[:, :], pu[:, :], 1.0 / 127.0, OP.mult)
            em.tt(hsb[:, ht, :], sg[:, :], su[:, :], OP.mult)
        for oc in range(DC):
            osl = slice(oc * 128, (oc + 1) * 128)
            em.dma(wd_i[:, :, :], wdT_p[:, :, osl])
            em.copy(wd_b[:, :, :], wd_i[:, :, :], eng="gpsimd")
            for ht in range(NHT):
                em.mm(pd[:, :], wd_b[:, ht, :], hsb[:, ht, :], start=(ht == 0), stop=(ht == NHT - 1))
            em.dma(xres[:, :], xs_r[oc, :, tsl])
            em.stt(yout[:, :], pd[:, :], 1.0 / 127.0, xres[:, :], OP.mult, OP.add)
            em.dma(yT_r[oc, :, tsl], yout[:, :])
    return em.emit()


NE = 8
NH = 16
HD = 64
LAT = 256


def build_moe():
    em = Em("moe")
    xs = em.inp("xs", [D, TPC], F32)
    rw = em.inp("rw", [D, NE], F32)
    win = em.inp("win", [NE, D, D], F32)
    wout = em.inp("wout", [NE, D, D], F32)
    binP = em.inp("binP", [128, DC, NE], F32)
    boutT = em.inp("boutT", [1, NE, D], F32)
    ident = em.inp("ident", [128, 128], F32)
    yT = em.out("yT", [D, TPC], F32)

    xs_r = xs.rearrange("(c p) t -> c p t", p=128)
    xs_p = xs.rearrange("(c p) t -> p c t", p=128)
    xs_p = xs.rearrange("(c p) t -> p c t", p=128)
    yT_r = yT.rearrange("(c p) t -> c p t", p=128)
    rw_p = rw.rearrange("(c p) e -> p c e", p=128)
    win_p = win.rearrange("e (c p) h -> e p c h", p=128)
    wout_p = wout.rearrange("e (c p) o -> e p c o", p=128)

    ones_f = em.sb([128, 128], F32)
    em.memset(ones_f[:, :], 1.0)
    id_sb = em.sb([128, 128], F32)
    em.dma(id_sb[:, :], ident)
    rw_f = em.sb([128, DC, NE], F32)
    em.dma(rw_f[:, :, :], rw_p)
    rw_sb = em.sb([128, DC, NE], BF16)
    em.copy(rw_sb[:, :, :], rw_f[:, :, :])
    binP_sb = em.sb([128, DC, NE], F32)
    em.dma(binP_sb[:, :, :], binP)
    boutT_sb = em.sb([1, NE, D], F32)
    em.dma(boutT_sb[:, :, :], boutT)

    nT = 512
    xcol = em.sb([128, DC, nT], F32)
    xsq = em.sb([128, DC, nT], F32)
    xln = em.sb([128, DC, nT], BF16)
    rb = em.sb([128, nT], F32)
    nb = em.sb([128, nT], F32)
    scr = em.sb([128, nT], F32)
    lg_sb = em.sb([8, nT], F32)
    ltok = em.sb([128, 4, 8], F32)
    ptok = em.sb([128, 4, 8], F32)
    wtok = em.sb([128, 4, 8], F32)
    tmp8 = em.sb([128, 8], F32)
    col1 = em.sb([128, 1], F32)
    wT_sb = em.sb([8, nT], F32)
    we_row = em.sb([1, nT], F32)
    web = em.sb([128, nT], F32)
    wf32 = em.sb([128, DC, D], F32)
    win_b = em.sb([128, DC, D], BF16)
    wout_b = em.sb([128, DC, D], BF16)
    hg = em.sb([128, nT], F32)
    hs = em.sb([128, DC, nT], BF16)
    yacc = em.sb([128, DC, nT], F32)
    xres = em.sb([128, nT], F32)
    yout = em.sb([128, nT], F32)

    p_s1 = em.ps([1, nT])
    p_s2 = em.ps([1, nT])
    pb = em.ps([128, nT])
    pr = em.ps([8, nT])
    ph = em.ps([128, nT])
    py = em.ps([128, nT])

    for ti in range(TPC // nT):
        tsl = slice(ti * nT, (ti + 1) * nT)
        em.dma(xcol[:, :, :], xs_p[:, :, tsl])
        rstd, negmr = ln_rows(em, xcol, nT, ones_f, xsq, p_s1, p_s2)
        bcast_row(em, rstd, 128, nT, ones_f, rb, pb)
        bcast_row(em, negmr, 128, nT, ones_f, nb, pb)
        apply_ln(em, xcol, xln, rb, nb, nT, scr)
        for c in range(DC):
            em.mm(pr[:8, :], rw_sb[:, c, :], xln[:, c, :], start=(c == 0), stop=(c == DC - 1))
        em.copy(lg_sb[:8, :], pr[:8, :])
        for q in range(nT // 128):
            qsl = slice(q * 128, (q + 1) * 128)
            em.transpose(pb[:, 0:8], lg_sb[0:8, qsl], id_sb[0:8, 0:8])
            em.copy(ltok[:, q, :], pb[:, 0:8])
            em.red(col1[:, :], ltok[:, q, :], OP.max)
            em.ts(tmp8[:, :], ltok[:, q, :], col1[:, 0:1], OP.subtract)
            em.act(tmp8[:, :], tmp8[:, :], AF.Exp)
            em.red(col1[:, :], tmp8[:, :], OP.add)
            em.recip(col1[:, :], col1[:, :])
            em.ts(ptok[:, q, :], tmp8[:, :], col1[:, 0:1], OP.mult)
            em.red(col1[:, :], ptok[:, q, :], OP.max)
            em.ts(tmp8[:, :], ptok[:, q, :], col1[:, 0:1], OP.is_ge)
            em.stt(tmp8[:, :], tmp8[:, :], -1e30, ptok[:, q, :], OP.mult, OP.add)
            em.red(col1[:, :], tmp8[:, :], OP.max)
            em.ts(tmp8[:, :], ptok[:, q, :], col1[:, 0:1], OP.is_ge)
            em.tt(wtok[:, q, :], ptok[:, q, :], tmp8[:, :], OP.mult)
            em.transpose(pr[:8, qsl], wtok[:, q, :], id_sb[0:128, 0:128])
        em.copy(wT_sb[:8, :], pr[:8, :])
        for e in range(NE):
            em.mm(p_s1[:1, :], id_sb[0:8, e:e + 1], wT_sb[0:8, :])
            em.copy(we_row[:1, :], p_s1[:1, :])
            em.mm(pb[:, :], ones_f[0:1, 0:128], we_row[:1, :])
            em.copy(web[:, :], pb[:, :])
            em.dma(wf32[:, :, :], win_p[e])
            em.copy(win_b[:, :, :], wf32[:, :, :], eng="gpsimd")
            for hc in range(DC):
                hsl = slice(hc * 128, (hc + 1) * 128)
                for c in range(DC):
                    em.mm(ph[:, :], win_b[:, c, hsl], xln[:, c, :], start=(c == 0), stop=(c == DC - 1))
                em.act(hg[:, :], ph[:, :], AF.Gelu_apprx_tanh, bias=binP_sb[:, hc, e:e + 1])
                em.tt(hs[:, hc, :], hg[:, :], web[:, :], OP.mult)
            em.dma(wf32[:, :, :], wout_p[e])
            em.copy(wout_b[:, :, :], wf32[:, :, :], eng="gpsimd")
            for oc in range(DC):
                osl = slice(oc * 128, (oc + 1) * 128)
                for hc in range(DC):
                    em.mm(py[:, :], wout_b[:, hc, osl], hs[:, hc, :], start=(hc == 0), stop=False)
                em.mm(py[:, :], boutT_sb[0:1, e, osl], we_row[:1, :], start=False, stop=True)
                if e == 0:
                    em.copy(yacc[:, oc, :], py[:, :])
                else:
                    em.tt(yacc[:, oc, :], yacc[:, oc, :], py[:, :], OP.add)
        for oc in range(DC):
            em.dma(xres[:, :], xs_r[oc, :, tsl])
            em.tt(yout[:, :], yacc[:, oc, :], xres[:, :], OP.add)
            em.dma(yT_r[oc, :, tsl], yout[:, :])
    return em.emit()


NBT = 64


def build_bank():
    em = Em("bank")
    xs = em.inp("xs", [D, TPC], F32)
    kT = em.inp("kT", [D, S], F32)
    v = em.inp("v", [S, D], F32)
    yT = em.out("yT", [D, TPC], F32)

    xs_r = xs.rearrange("(c p) t -> c p t", p=128)
    xs_p = xs.rearrange("(c p) t -> p c t", p=128)
    yT_r = yT.rearrange("(c p) t -> c p t", p=128)
    kT_r = kT.rearrange("(c p) s -> c p s", p=128)
    v_r = v.rearrange("(st p) o -> st p o", p=128)

    kTb2 = em.dram("kTb2", [NBT, DC, 128, 128], BF16)
    vb = em.dram("vb", [NBT, 128, D], BF16)
    kTb2_rd = kTb2.rearrange("n c p m -> n p c m")

    ones_f = em.sb([128, 128], F32)
    em.memset(ones_f[:, :], 1.0)
    ones_b = em.sb([128, 1], BF16)
    em.memset(ones_b[:, :], 1.0)

    cast_f = em.sb([128, 2048], F32)
    cast_b = em.sb([128, 2048], BF16)
    for c in range(DC):
        for j in range(4):
            ssl = slice(j * 2048, (j + 1) * 2048)
            em.dma(cast_f[:, :], kT_r[c][:, ssl])
            em.copy(cast_b[:, :], cast_f[:, :])
            for q in range(16):
                nt = j * 16 + q
                em.dma(kTb2[nt, c], cast_b[:, q * 128:(q + 1) * 128])
    for st in range(NBT):
        em.dma(cast_f[:, :1024], v_r[st])
        em.copy(cast_b[:, :1024], cast_f[:, :1024])
        em.dma(vb[st], cast_b[:, :1024])

    nT = 512
    xcol = em.sb([128, DC, nT], F32)
    xsq = em.sb([128, DC, nT], F32)
    rb = em.sb([128, nT], F32)
    nb = em.sb([128, nT], F32)
    scr = em.sb([128, nT], F32)
    xlnb = em.sb([128, DC, TPC], BF16)
    p_s1 = em.ps([1, nT])
    p_s2 = em.ps([1, nT])
    pbB = em.ps([128, nT])
    for ti in range(TPC // nT):
        tsl = slice(ti * nT, (ti + 1) * nT)
        em.dma(xcol[:, :, :], xs_p[:, :, tsl])
        rstd, negmr = ln_rows(em, xcol, nT, ones_f, xsq, p_s1, p_s2)
        bcast_row(em, rstd, 128, nT, ones_f, rb, pbB)
        bcast_row(em, negmr, 128, nT, ones_f, nb, pbB)
        apply_ln(em, xcol, xlnb[:, :, tsl], rb, nb, nT, scr)

    nG = 256
    kt_sb = em.sb([128, DC, 128], BF16)
    sc_sb = em.sb([128, NBT, nG], F32)
    sqt = em.sb([128, nG], BF16)
    gt = em.sb([128, nG], BF16)
    vt_sb = em.sb([128, D], BF16)
    srow = em.sb([1, nG], F32)
    cb = em.sb([128, nG], F32)
    g1 = em.sb([128, nG], F32)
    yout = em.sb([128, nG], F32)
    xres = em.sb([128, nG], F32)
    psC = em.ps([128, nG])
    py4 = [em.ps([128, nG]) for _ in range(4)]

    SQRT_N = float(np.sqrt(S))
    for tg in range(TPC // nG):
        gsl = slice(tg * nG, (tg + 1) * nG)
        for nt in range(NBT):
            em.dma(kt_sb[:, :, :], kTb2_rd[nt])
            for c in range(DC):
                em.mm(psC[:, :], kt_sb[:, c, :], xlnb[:, c, gsl], start=(c == 0), stop=(c == DC - 1))
            em.copy(sc_sb[:, nt, :], psC[:, :])
            em.act(sqt[:, :], psC[:, :], AF.Square)
            em.mm(p_s1[:1, :nG], ones_b[:, 0:1], sqt[:, :], start=(nt == 0), stop=(nt == NBT - 1))
        em.copy(srow[:1, :], p_s1[:1, :nG])
        em.act(srow[:1, :], srow[:1, :], AF.Sqrt)
        em.recip(srow[:1, :], srow[:1, :])
        em.ts(srow[:1, :], srow[:1, :], SQRT_N, OP.mult)
        em.mm(pbB[:, :nG], ones_f[0:1, 0:128], srow[:1, :])
        em.copy(cb[:, :], pbB[:, :nG])
        for half in range(2):
            for nt in range(NBT):
                em.dma(vt_sb[:, :], vb[nt])
                em.tt(g1[:, :], sc_sb[:, nt, :], cb[:, :], OP.mult)
                em.act(gt[:, :], g1[:, :], AF.Gelu)
                for j in range(4):
                    oc = half * 4 + j
                    em.mm(py4[j][:, :], vt_sb[:, oc * 128:(oc + 1) * 128], gt[:, :],
                          start=(nt == 0), stop=(nt == NBT - 1))
            for j in range(4):
                oc = half * 4 + j
                em.dma(xres[:, :], xs_r[oc, :, gsl])
                em.tt(yout[:, :], py4[j][:, :], xres[:, :], OP.add)
                em.dma(yT_r[oc, :, gsl], yout[:, :])
    return em.emit()


F_LAB = 4096
NFT = F_LAB // 128



def build_lab_p1(variant):
    em = Em("lab_p1_" + variant)
    xs = em.inp("xs", [D, TPC], F32)
    wkT = em.inp("wkT", [D, D], F32)
    wvT = em.inp("wvT", [D, D], F32)
    bkP = em.inp("bkP", [128, DC], F32)
    bvP = em.inp("bvP", [128, DC], F32)
    pacc = em.out("pacc", [128, 2 * DC], F32)

    xs_p = xs.rearrange("(c p) t -> p c t", p=128)
    wkT_p = wkT.rearrange("(c p) o -> p c o", p=128)
    wvT_p = wvT.rearrange("(c p) o -> p c o", p=128)

    ones_f = em.sb([128, 128], F32)
    em.memset(ones_f[:, :], 1.0)

    nT = 512
    xcol = em.sb([128, DC, nT], F32)
    qf = em.sb([128, DC, nT], F32)
    xb = em.sb([128, DC, nT], BF16)
    rbb = em.sb([128, nT], F32)
    nbb = em.sb([128, nT], F32)
    scr = em.sb([128, nT], F32)
    t1 = em.sb([128, nT], F32)
    e1 = em.sb([128, nT], F32)
    t3 = em.sb([128, nT], F32)
    kf = em.sb([128, nT], F32)
    vv = em.sb([128, nT], F32)
    red1 = em.sb([128, 1], F32)
    accs = em.sb([128, 2 * DC], F32)
    wA_b = em.sb([128, DC, D], BF16)
    wB_b = em.sb([128, DC, D], BF16)

    bk_sb = em.sb([128, DC], F32)
    em.dma(bk_sb[:, :], bkP)
    bv_sb = em.sb([128, DC], F32)
    em.dma(bv_sb[:, :], bvP)

    p_s1 = em.ps([1, nT])
    p_s2 = em.ps([1, nT])
    pbb = em.ps([128, nT])
    psK = em.ps([128, nT])
    psV = em.ps([128, nT])

    def load_w(dst_b, src_p):
        em.dma(xcol[:, :, :], src_p[:, :, 0:512])
        em.copy(dst_b[:, :, 0:512], xcol[:, :, :], eng="gpsimd")
        em.dma(xcol[:, :, :], src_p[:, :, 512:])
        em.copy(dst_b[:, :, 512:], xcol[:, :, :], eng="gpsimd")

    load_w(wA_b, wkT_p)
    load_w(wB_b, wvT_p)

    for si in range(TPC // nT):
        ssl = slice(si * nT, (si + 1) * nT)
        em.dma(xcol[:, :, :], xs_p[:, :, ssl])
        if variant == "first":
            rstd, negmr = ln_rows(em, xcol, nT, ones_f, qf, p_s1, p_s2)
            bcast_row(em, rstd, 128, nT, ones_f, rbb, pbb)
            bcast_row(em, negmr, 128, nT, ones_f, nbb, pbb)
            apply_ln(em, xcol, xb, rbb, nbb, nT, scr)
        else:
            em.copy(xb[:, :, :], xcol[:, :, :])
        for ct in range(DC):
            csl = slice(ct * 128, (ct + 1) * 128)
            for c in range(DC):
                em.mm(psK[:, :], wA_b[:, c, csl], xb[:, c, :], start=(c == 0), stop=(c == DC - 1))
            for c in range(DC):
                em.mm(psV[:, :], wB_b[:, c, csl], xb[:, c, :], start=(c == 0), stop=(c == DC - 1))
            em.ts(t1[:, :], psK[:, :], bk_sb[:, ct:ct + 1], OP.add, 0.0, OP.min)
            em.act(e1[:, :], t1[:, :], AF.Exp)
            em.ts(t3[:, :], psK[:, :], bk_sb[:, ct:ct + 1], OP.add, 0.0, OP.max)
            em.tt(kf[:, :], e1[:, :], t3[:, :], OP.add)
            em.ts(vv[:, :], psV[:, :], bv_sb[:, ct:ct + 1], OP.add)
            em.tt(e1[:, :], kf[:, :], vv[:, :], OP.mult)
            em.red(red1[:, :], e1[:, :], OP.add)
            if si == 0:
                em.copy(accs[:, ct:ct + 1], red1[:, :])
            else:
                em.tt(accs[:, ct:ct + 1], accs[:, ct:ct + 1], red1[:, :], OP.add)
            em.red(red1[:, :], kf[:, :], OP.add)
            if si == 0:
                em.copy(accs[:, DC + ct:DC + ct + 1], red1[:, :])
            else:
                em.tt(accs[:, DC + ct:DC + ct + 1], accs[:, DC + ct:DC + ct + 1], red1[:, :], OP.add)
    em.dma(pacc, accs[:, :])
    return em.emit()


def build_lab_p2(variant):
    em = Em("lab_p2_" + variant)
    xs = em.inp("xs", [D, TPC], F32)
    pacc4 = em.inp("pacc4", [4, 128, 2 * DC], F32)
    wqT = em.inp("wqT", [D, D], F32)
    wo = em.inp("wo", [D, D], F32)
    w1 = em.inp("w1", [D, F_LAB], F32)
    w2 = em.inp("w2", [F_LAB, D], F32)
    bqP = em.inp("bqP", [128, DC], F32)
    boP = em.inp("boP", [128, DC], F32)
    b1P = em.inp("b1P", [128, NFT], F32)
    b2P = em.inp("b2P", [128, DC], F32)
    ln1sP = em.inp("ln1sP", [128, DC], F32)
    ln1bP = em.inp("ln1bP", [128, DC], F32)
    ln2sP = em.inp("ln2sP", [128, DC], F32)
    ln2bP = em.inp("ln2bP", [128, DC], F32)
    if variant == "last":
        xtop = em.inp("xtop", [D, TPC], F32)
        xtop_r = xtop.rearrange("(c p) t -> c p t", p=128)
    yT = em.out("yT", [D, TPC], F32)

    xs_p = xs.rearrange("(c p) t -> p c t", p=128)
    yT_r = yT.rearrange("(c p) t -> c p t", p=128)
    wqT_p = wqT.rearrange("(c p) o -> p c o", p=128)
    wo_p = wo.rearrange("(c p) o -> p c o", p=128)
    w1_p = w1.rearrange("(c p) f -> p c f", p=128)
    w2_p = w2.rearrange("(f p) o -> p f o", p=128)

    ones_f = em.sb([128, 128], F32)
    em.memset(ones_f[:, :], 1.0)

    nT = 512
    xcol = em.sb([128, DC, nT], F32)
    qf = em.sb([128, DC, nT], F32)
    xb = em.sb([128, DC, nT], BF16)
    rbb = em.sb([128, nT], F32)
    nbb = em.sb([128, nT], F32)
    scr = em.sb([128, nT], F32)
    t1 = em.sb([128, nT], F32)
    e1 = em.sb([128, nT], F32)
    t3 = em.sb([128, nT], F32)
    wA_b = em.sb([128, DC, D], BF16)
    wB_b = em.sb([128, DC, D], BF16)

    bq_sb = em.sb([128, DC], F32)
    em.dma(bq_sb[:, :], bqP)
    bo_sb = em.sb([128, DC], F32)
    em.dma(bo_sb[:, :], boP)
    b1_sb = em.sb([128, NFT], F32)
    em.dma(b1_sb[:, :], b1P)
    b2_sb = em.sb([128, DC], F32)
    em.dma(b2_sb[:, :], b2P)
    l1s = em.sb([128, DC], F32)
    em.dma(l1s[:, :], ln1sP)
    l1b = em.sb([128, DC], F32)
    em.dma(l1b[:, :], ln1bP)
    l2s = em.sb([128, DC], F32)
    em.dma(l2s[:, :], ln2sP)
    l2b = em.sb([128, DC], F32)
    em.dma(l2b[:, :], ln2bP)

    accs = em.sb([128, 2 * DC], F32)
    pld = em.sb([128, 2 * DC], F32)
    em.dma(accs[:, :], pacc4[0])
    for j in range(1, 4):
        em.dma(pld[:, :], pacc4[j])
        em.tt(accs[:, :], accs[:, :], pld[:, :], OP.add)

    p_s1 = em.ps([1, nT])
    p_s2 = em.ps([1, nT])
    pbb = em.ps([128, nT])
    psK = em.ps([128, nT])
    psV = em.ps([128, nT])

    def load_w(dst_b, src_p):
        em.dma(xcol[:, :, :], src_p[:, :, 0:512])
        em.copy(dst_b[:, :, 0:512], xcol[:, :, :], eng="gpsimd")
        em.dma(xcol[:, :, :], src_p[:, :, 512:])
        em.copy(dst_b[:, :, 512:], xcol[:, :, :], eng="gpsimd")

    load_w(wA_b, wqT_p)
    load_w(wB_b, wo_p)
    attb = em.sb([128, DC, nT], BF16)
    r1 = em.sb([128, DC, nT], F32)
    x1f = em.sb([128, DC, nT], F32)
    x1b = em.sb([128, DC, nT], BF16)
    wstr = em.sb([128, DC, 128], F32)
    wstr_b = em.sb([128, DC, 128], BF16)
    w2str_b = em.sb([128, NFT, 128], BF16)
    h1b = em.sb([128, NFT, nT], BF16)
    yo = em.sb([128, nT], F32)

    for ti in range(TPC // nT):
        tsl = slice(ti * nT, (ti + 1) * nT)
        em.dma(xcol[:, :, :], xs_p[:, :, tsl])
        if variant == "first":
            rstd, negmr = ln_rows(em, xcol, nT, ones_f, qf, p_s1, p_s2)
            bcast_row(em, rstd, 128, nT, ones_f, rbb, pbb)
            bcast_row(em, negmr, 128, nT, ones_f, nbb, pbb)
            for c in range(DC):
                em.tt(scr[:, :], xcol[:, c, :], rbb[:, :], OP.mult)
                em.tt(xcol[:, c, :], scr[:, :], nbb[:, :], OP.add)
        em.copy(xb[:, :, :], xcol[:, :, :])
        for ct in range(DC):
            csl = slice(ct * 128, (ct + 1) * 128)
            for c in range(DC):
                em.mm(psK[:, :], wA_b[:, c, csl], xb[:, c, :], start=(c == 0), stop=(c == DC - 1))
            em.ts(t1[:, :], psK[:, :], bq_sb[:, ct:ct + 1], OP.add, 0.0, OP.min)
            em.act(e1[:, :], t1[:, :], AF.Exp)
            em.ts(t3[:, :], psK[:, :], bq_sb[:, ct:ct + 1], OP.add, 0.0, OP.max)
            em.tt(qf[:, ct, :], e1[:, :], t3[:, :], OP.add)
        for ct in range(DC):
            em.ts(t1[:, :], qf[:, ct, :], accs[:, DC + ct:DC + ct + 1], OP.mult, 1e-6, OP.add)
            em.recip(t1[:, :], t1[:, :])
            em.ts(t3[:, :], qf[:, ct, :], accs[:, ct:ct + 1], OP.mult)
            em.tt(attb[:, ct, :], t3[:, :], t1[:, :], OP.mult)
        for oc in range(DC):
            osl = slice(oc * 128, (oc + 1) * 128)
            for ct in range(DC):
                em.mm(psV[:, :], wB_b[:, ct, osl], attb[:, ct, :], start=(ct == 0), stop=(ct == DC - 1))
            em.act(e1[:, :], psV[:, :], AF.Gelu_apprx_tanh, bias=bo_sb[:, oc:oc + 1])
            em.tt(r1[:, oc, :], e1[:, :], xcol[:, oc, :], OP.add)
        rstd, negmr = ln_rows(em, r1, nT, ones_f, qf, p_s1, p_s2, eps=1e-6)
        bcast_row(em, rstd, 128, nT, ones_f, rbb, pbb)
        bcast_row(em, negmr, 128, nT, ones_f, nbb, pbb)
        for c in range(DC):
            em.tt(scr[:, :], r1[:, c, :], rbb[:, :], OP.mult)
            em.tt(scr[:, :], scr[:, :], nbb[:, :], OP.add)
            em.ts(x1f[:, c, :], scr[:, :], l1s[:, c:c + 1], OP.mult, l1b[:, c:c + 1], OP.add)
        em.copy(x1b[:, :, :], x1f[:, :, :])
        for ft in range(NFT):
            fsl = slice(ft * 128, (ft + 1) * 128)
            em.dma(wstr[:, :, :], w1_p[:, :, fsl])
            em.copy(wstr_b[:, :, :], wstr[:, :, :], eng="gpsimd")
            for c in range(DC):
                em.mm(psK[:, :], wstr_b[:, c, :], x1b[:, c, :], start=(c == 0), stop=(c == DC - 1))
            em.act(h1b[:, ft, :], psK[:, :], AF.Gelu_apprx_tanh, bias=b1_sb[:, ft:ft + 1])
        for oc in range(DC):
            osl = slice(oc * 128, (oc + 1) * 128)
            for grp in range(4):
                em.dma(wstr[:, :, :], w2_p[:, grp * DC:(grp + 1) * DC, osl])
                em.copy(w2str_b[:, grp * DC:(grp + 1) * DC, :], wstr[:, :, :], eng="gpsimd")
            for fc in range(NFT):
                em.mm(psV[:, :], w2str_b[:, fc, :], h1b[:, fc, :], start=(fc == 0), stop=(fc == NFT - 1))
            em.act(e1[:, :], psV[:, :], AF.Gelu_apprx_tanh, bias=b2_sb[:, oc:oc + 1])
            em.tt(r1[:, oc, :], e1[:, :], x1f[:, oc, :], OP.add)
        rstd, negmr = ln_rows(em, r1, nT, ones_f, qf, p_s1, p_s2, eps=1e-6)
        bcast_row(em, rstd, 128, nT, ones_f, rbb, pbb)
        bcast_row(em, negmr, 128, nT, ones_f, nbb, pbb)
        for c in range(DC):
            em.tt(scr[:, :], r1[:, c, :], rbb[:, :], OP.mult)
            em.tt(scr[:, :], scr[:, :], nbb[:, :], OP.add)
            em.ts(yo[:, :], scr[:, :], l2s[:, c:c + 1], OP.mult, l2b[:, c:c + 1], OP.add)
            if variant == "last":
                em.dma(t1[:, :], xtop_r[c, :, tsl])
                em.tt(yo[:, :], yo[:, :], t1[:, :], OP.add)
            em.dma(yT_r[c, :, tsl], yo[:, :])
    return em.emit()



def build_mla():
    em = Em("mla")
    xc = em.inp("xc", [D, S], F32)
    xs = em.inp("xs", [D, TPC], F32)
    wq = em.inp("wq", [D, D], F32)
    wkb = em.inp("wkb", [D, LAT], F32)
    R8T = em.inp("R8T", [D, D], F32)
    RT = em.inp("RT", [D, D], F32)
    wkaT = em.inp("wkaT", [D, LAT], F32)
    wvaT = em.inp("wvaT", [D, LAT], F32)
    wvbT = em.inp("wvbT", [LAT, D], F32)
    woT = em.inp("woT", [D, D], F32)
    yT = em.out("yT", [D, TPC], F32)

    xc_r = xc.rearrange("(c p) s -> c p s", p=128)
    xc_p = xc.rearrange("(c p) s -> p c s", p=128)
    xs_r = xs.rearrange("(c p) t -> c p t", p=128)
    xs_p = xs.rearrange("(c p) t -> p c t", p=128)
    yT_r = yT.rearrange("(c p) t -> c p t", p=128)
    wq_r = wq.rearrange("(c p) d -> c p d", p=128)
    wkb_r = wkb.rearrange("(c p) l -> c p l", p=128)
    R8T_r = R8T.rearrange("(c p) o -> c p o", p=128)
    RT_r = RT.rearrange("(c p) o -> c p o", p=128)
    wkaT_r = wkaT.rearrange("(c p) l -> c p l", p=128)
    wvaT_r = wvaT.rearrange("(c p) l -> c p l", p=128)
    wvbT_r = wvbT.rearrange("(c p) o -> c p o", p=128)
    woT_r = woT.rearrange("(c p) o -> c p o", p=128)

    wqT_d = em.dram("wqT_d", [D, D], BF16)
    qT_d = em.dram("qT_d", [NH, HD, TPC], BF16)
    attT_d = em.dram("attT_d", [D, TPC], BF16)
    wqT_dr = wqT_d.rearrange("(c p) o -> c p o", p=128)
    wqT_dp = wqT_d.rearrange("(c p) o -> p c o", p=128)

    ones_f = em.sb([128, 128], F32)
    em.memset(ones_f[:, :], 1.0)

    wtile = em.sb([128, 128], F32)
    rtile = em.sb([128, 512], F32)
    qev = em.sb([128, 512], BF16)
    wkbT_b = em.sb([128, 2, D], BF16)
    pK = em.ps([128, 512])
    for dt_ in range(DC):
        for cg in range(2):
            for c in range(DC):
                em.dma(wtile[:, :], wq_r[c][:, dt_ * 128:(dt_ + 1) * 128])
                em.dma(rtile[:, :], R8T_r[c][:, cg * 512:(cg + 1) * 512])
                em.mm(pK[:, :], wtile[:, :], rtile[:, :], start=(c == 0), stop=(c == DC - 1))
            em.copy(qev[:, :], pK[:, :])
            em.dma(wqT_dr[dt_][:, cg * 512:(cg + 1) * 512], qev[:, :])
    for lt in range(2):
        for cg in range(2):
            for c in range(DC):
                em.dma(wtile[:, :], wkb_r[c][:, lt * 128:(lt + 1) * 128])
                em.dma(rtile[:, :], RT_r[c][:, cg * 512:(cg + 1) * 512])
                em.mm(pK[:, :], wtile[:, :], rtile[:, :], start=(c == 0), stop=(c == DC - 1))
            em.copy(wkbT_b[:, lt, cg * 512:(cg + 1) * 512], pK[:, :])
    wka_b = em.sb([128, DC, LAT], BF16)
    wva_b = em.sb([128, DC, LAT], BF16)
    wvbT_b = em.sb([128, 2, D], BF16)
    for c in range(DC):
        em.dma(rtile[:, :LAT], wkaT_r[c])
        em.copy(wka_b[:, c, :], rtile[:, :LAT], eng="gpsimd")
        em.dma(rtile[:, :LAT], wvaT_r[c])
        em.copy(wva_b[:, c, :], rtile[:, :LAT], eng="gpsimd")
    for lt in range(2):
        em.dma(rtile[:, :], wvbT_r[lt][:, 0:512])
        em.copy(wvbT_b[:, lt, 0:512], rtile[:, :], eng="gpsimd")
        em.dma(rtile[:, :], wvbT_r[lt][:, 512:])
        em.copy(wvbT_b[:, lt, 512:], rtile[:, :], eng="gpsimd")

    nT = 512
    xcol = em.sb([128, DC, nT], F32)
    xsq = em.sb([128, DC, nT], F32)
    xln = em.sb([128, DC, nT], BF16)
    rb = em.sb([128, nT], F32)
    nb = em.sb([128, nT], F32)
    scr = em.sb([128, nT], F32)
    klatT = em.sb([128, 2, S], BF16)
    vlatT = em.sb([128, 2, S], BF16)
    p_s1 = em.ps([1, nT])
    p_s2 = em.ps([1, nT])
    pb = em.ps([128, nT])
    for si in range(S // nT):
        ssl = slice(si * nT, (si + 1) * nT)
        em.dma(xcol[:, :, :], xc_p[:, :, ssl])
        rstd, negmr = ln_rows(em, xcol, nT, ones_f, xsq, p_s1, p_s2)
        bcast_row(em, rstd, 128, nT, ones_f, rb, pb)
        bcast_row(em, negmr, 128, nT, ones_f, nb, pb)
        apply_ln(em, xcol, xln, rb, nb, nT, scr)
        for lt in range(2):
            lsl = slice(lt * 128, (lt + 1) * 128)
            for c in range(DC):
                em.mm(pK[:, :], wka_b[:, c, lsl], xln[:, c, :], start=(c == 0), stop=(c == DC - 1))
            em.copy(klatT[:, lt, ssl], pK[:, :])
            for c in range(DC):
                em.mm(pK[:, :], wva_b[:, c, lsl], xln[:, c, :], start=(c == 0), stop=(c == DC - 1))
            em.copy(vlatT[:, lt, ssl], pK[:, :])
    qwt_b = em.sb([128, DC, 128], BF16)
    for ti in range(TPC // nT):
        tsl = slice(ti * nT, (ti + 1) * nT)
        em.dma(xcol[:, :, :], xs_p[:, :, tsl])
        rstd, negmr = ln_rows(em, xcol, nT, ones_f, xsq, p_s1, p_s2)
        bcast_row(em, rstd, 128, nT, ones_f, rb, pb)
        bcast_row(em, negmr, 128, nT, ones_f, nb, pb)
        apply_ln(em, xcol, xln, rb, nb, nT, scr)
        for oc in range(DC):
            osl = slice(oc * 128, (oc + 1) * 128)
            em.dma(qwt_b[:, :, :], wqT_dp[:, :, osl])
            for c in range(DC):
                em.mm(pK[:, :], qwt_b[:, c, :], xln[:, c, :], start=(c == 0), stop=(c == DC - 1))
            em.copy(qev[:, :], pK[:, :])
            em.dma(qT_d[2 * oc][:, tsl], qev[0:64, :])
            em.dma(qT_d[2 * oc + 1][:, tsl], qev[64:128, :])

    qh_sb = em.sb([64, TPC], BF16)
    KhT = em.sb([64, 16, nT], BF16)
    Vp = em.sb([128, 64, 132], BF16)
    pT = em.sb([128, nT], BF16)
    lrow = em.sb([1, nT], F32)
    rb64 = em.sb([64, nT], F32)
    attev = em.sb([64, nT], BF16)
    psS = em.ps([128, nT])
    psO = em.ps([128, nT])

    em.memset(Vp[:, :, 64:65], 1.0)
    em.memset(Vp[:, :, 130:131], 1.0)
    for pair in range(NH // 2):
        for st in range(64):
            for lt in range(2):
                em.mm(pK[:, 0:128], vlatT[:, lt, st * 128:(st + 1) * 128],
                      wvbT_b[:, lt, pair * 128:(pair + 1) * 128],
                      start=(lt == 0), stop=(lt == 1))
            em.copy(Vp[:, st, 0:64], pK[:, 0:64])
            em.copy(Vp[:, st, 66:130], pK[:, 64:128])
        for hh in range(2):
            h = pair * 2 + hh
            vofs = 0 if hh == 0 else 66
            em.dma(qh_sb[:, :], qT_d[h])
            for sj in range(S // nT):
                ssl = slice(sj * nT, (sj + 1) * nT)
                for lt in range(2):
                    em.mm(psS[0:64, :], wkbT_b[:, lt, h * 64:(h + 1) * 64],
                          klatT[:, lt, ssl], start=(lt == 0), stop=(lt == 1))
                em.copy(KhT[0:64, sj, :], psS[0:64, :])
            for tg in range(TPC // nT):
                gsl = slice(tg * nT, (tg + 1) * nT)
                for st in range(64):
                    em.mm(psS[:, :], KhT[0:64, st // 4, (st % 4) * 128:(st % 4) * 128 + 128],
                          qh_sb[0:64, gsl])
                    em.act(pT[:, :], psS[:, :], AF.Exp)
                    em.mm(psO[0:65, :], Vp[:, st, vofs:vofs + 65], pT[:, :],
                          start=(st == 0), stop=(st == 63))
                em.copy(lrow[:1, :], psO[64:65, :])
                em.recip(lrow[:1, :], lrow[:1, :])
                em.mm(pb[0:64, :], ones_f[0:1, 0:64], lrow[:1, :])
                em.copy(rb64[:, :], pb[0:64, :])
                em.tt(attev[:, :], psO[0:64, :], rb64[:, :], OP.mult)
                em.dma(attT_d[h * 64:(h + 1) * 64, gsl], attev[:, :])

    attld = em.sb([128, nT], BF16)
    yout = em.sb([128, nT], F32)
    xres = em.sb([128, nT], F32)
    wot_b = em.sb([128, DC, 128], BF16)
    for tg in range(TPC // nT):
        gsl = slice(tg * nT, (tg + 1) * nT)
        for oc in range(DC):
            osl = slice(oc * 128, (oc + 1) * 128)
            for cc in range(DC):
                em.dma(rtile[:, :128], woT_r[cc][:, osl])
                em.copy(wot_b[:, cc, :], rtile[:, :128])
                em.dma(attld[:, :], attT_d[cc * 128:(cc + 1) * 128, gsl])
                em.mm(pK[:, :], wot_b[:, cc, :], attld[:, :], start=(cc == 0), stop=(cc == DC - 1))
            em.dma(xres[:, :], xs_r[oc, :, gsl])
            em.tt(yout[:, :], pK[:, :], xres[:, :], OP.add)
            em.dma(yT_r[oc, :, gsl], yout[:, :])
    return em.emit()



def mha_group_nk(g):
    if g < 4:
        return 8 * g + 8
    return 64 - 8 * (g - 4)


def build_mha():
    em = Em("mha")
    xc = em.inp("xc", [D, S], F32)
    xs = em.inp("xs", [D, TPC], F32)
    wqkv = em.inp("wqkv", [D, 3 * D], F32)
    bqkvP = em.inp("bqkvP", [128, 24], F32)
    bqkvF = em.inp("bqkvF", [1, 3 * D], F32)
    wproj = em.inp("wproj", [D, D], F32)
    bpP = em.inp("bpP", [128, DC], F32)
    masks = em.inp("masks", [128, 64, 256], BF16)
    yT = em.out("yT", [D, TPC], F32)

    xc_r = xc.rearrange("(c p) s -> c p s", p=128)
    xc_p = xc.rearrange("(c p) s -> p c s", p=128)
    xs_r = xs.rearrange("(c p) t -> c p t", p=128)
    xs_p = xs.rearrange("(c p) t -> p c t", p=128)
    yT_r = yT.rearrange("(c p) t -> c p t", p=128)
    wqkv_r = wqkv.rearrange("(c p) o -> c p o", p=128)
    wproj_r = wproj.rearrange("(c p) o -> c p o", p=128)

    KT_d = em.dram("KT_d", [D, S], BF16)
    V_d = em.dram("V_d", [64, 128, D], BF16)
    qT_d = em.dram("qT_d", [NH, HD, TPC], BF16)
    attT_d = em.dram("attT_d", [D, TPC], BF16)
    KT_dr = KT_d.rearrange("(c p) s -> c p s", p=128)

    ones_f = em.sb([128, 128], F32)
    em.memset(ones_f[:, :], 1.0)

    wld = em.sb([128, DC, 512], F32)
    wbig_b = em.sb([128, DC, 2 * D], BF16)
    bqP_sb = em.sb([128, 24], F32)
    em.dma(bqP_sb[:, :], bqkvP)
    bqF_sb = em.sb([1, 3 * D], F32)
    em.dma(bqF_sb[:, :], bqkvF)
    bp_sb = em.sb([128, DC], F32)
    em.dma(bp_sb[:, :], bpP)
    msk_sb = em.sb([128, 64, 256], BF16)
    em.dma(msk_sb[:, :, :], masks)
    bcV = em.sb([128, D], F32)
    pb = em.ps([128, 512])
    for vg in range(2):
        em.mm(pb[:, :], ones_f[0:1, 0:128], bqF_sb[0:1, 2048 + vg * 512:2048 + (vg + 1) * 512])
        em.copy(bcV[:, vg * 512:(vg + 1) * 512], pb[:, :])

    def load_wbig(col0, ncols):
        for seg in range(ncols // 512):
            ssl_d = slice(col0 + seg * 512, col0 + (seg + 1) * 512)
            for c in range(DC):
                em.dma(wld[:, c, :], wqkv_r[c][:, ssl_d])
            em.copy(wbig_b[:, :, seg * 512:(seg + 1) * 512], wld[:, :, :], eng="gpsimd")

    nT = 512
    xcol = em.sb([128, DC, nT], F32)
    xsq = em.sb([128, DC, nT], F32)
    xln = em.sb([128, DC, nT], BF16)
    rb = em.sb([128, nT], F32)
    nb = em.sb([128, nT], F32)
    scr = em.sb([128, nT], F32)
    kvout = em.sb([128, nT], BF16)
    p_s1 = em.ps([1, nT])
    p_s2 = em.ps([1, nT])
    pK = em.ps([128, nT])
    pV = em.ps([128, nT])

    load_wbig(D, 2 * D)
    for si in range(S // nT):
        ssl = slice(si * nT, (si + 1) * nT)
        em.dma(xcol[:, :, :], xc_p[:, :, ssl])
        rstd, negmr = ln_rows(em, xcol, nT, ones_f, xsq, p_s1, p_s2)
        bcast_row(em, rstd, 128, nT, ones_f, rb, pb)
        bcast_row(em, negmr, 128, nT, ones_f, nb, pb)
        apply_ln(em, xcol, xln, rb, nb, nT, scr)
        for ct in range(DC):
            csl = slice(ct * 128, (ct + 1) * 128)
            for c in range(DC):
                em.mm(pK[:, :], wbig_b[:, c, csl], xln[:, c, :], start=(c == 0), stop=(c == DC - 1))
            em.ts(kvout[:, :], pK[:, :], bqP_sb[:, 8 + ct:9 + ct], OP.add)
            em.dma(KT_dr[ct][:, ssl], kvout[:, :])
        for sub in range(4):
            st = si * 4 + sub
            for vg in range(2):
                vsl = slice(D + vg * 512, D + (vg + 1) * 512)
                for c in range(DC):
                    em.mm(pV[:, :], xln[:, c, sub * 128:(sub + 1) * 128],
                          wbig_b[:, c, vsl], start=(c == 0), stop=(c == DC - 1))
                em.tt(kvout[:, :], pV[:, :], bcV[:, vg * 512:(vg + 1) * 512], OP.add)
                em.dma(V_d[st][:, vg * 512:(vg + 1) * 512], kvout[:, :])

    load_wbig(0, D)
    for ti in range(TPC // nT):
        tsl = slice(ti * nT, (ti + 1) * nT)
        em.dma(xcol[:, :, :], xs_p[:, :, tsl])
        rstd, negmr = ln_rows(em, xcol, nT, ones_f, xsq, p_s1, p_s2)
        bcast_row(em, rstd, 128, nT, ones_f, rb, pb)
        bcast_row(em, negmr, 128, nT, ones_f, nb, pb)
        apply_ln(em, xcol, xln, rb, nb, nT, scr)
        for oc in range(DC):
            osl = slice(oc * 128, (oc + 1) * 128)
            for c in range(DC):
                em.mm(pK[:, :], wbig_b[:, c, osl], xln[:, c, :], start=(c == 0), stop=(c == DC - 1))
            em.ts(scr[:, :], pK[:, :], bqP_sb[:, oc:oc + 1], OP.add)
            em.copy(kvout[:, :], scr[:, :])
            em.dma(qT_d[2 * oc][:, tsl], kvout[0:64, :])
            em.dma(qT_d[2 * oc + 1][:, tsl], kvout[64:128, :])

    nG = 256
    qh_sb = em.sb([64, TPC], BF16)
    KhT = em.sb([64, S], BF16)
    Vh = em.sb([128, 64, 65], BF16)
    pT = em.sb([128, nG], BF16)
    lrow = em.sb([1, nG], F32)
    rb64 = em.sb([64, nG], F32)
    attev = em.sb([64, nG], BF16)
    psS = em.ps([128, nG])
    psO = em.ps([128, nG])

    em.memset(Vh[:, :, 64:65], 1.0)
    for h in range(NH):
        em.dma(qh_sb[:, :], qT_d[h])
        em.dma(KhT[:, :], KT_d[h * 64:(h + 1) * 64, :])
        em.dma(Vh[:, :, 0:64], V_d.rearrange("st p c -> p st c")[:, :, h * 64:(h + 1) * 64])
        for g in range(8):
            gsl = slice(g * nG, (g + 1) * nG)
            NK = mha_group_nk(g)
            for st in range(NK):
                em.mm(psS[:, :], KhT[0:64, st * 128:(st + 1) * 128], qh_sb[0:64, gsl])
                j = st - (NK - 8)
                if j >= 0:
                    em.tt(psS[:, :], psS[:, :], msk_sb[:, g * 8 + j, :], OP.add)
                em.act(pT[:, :], psS[:, :], AF.Exp, scale=0.125)
                em.mm(psO[0:65, :], Vh[:, st, 0:65], pT[:, :], start=(st == 0), stop=(st == NK - 1))
            em.copy(lrow[:1, :], psO[64:65, :])
            em.recip(lrow[:1, :], lrow[:1, :])
            em.mm(pb[0:64, :nG], ones_f[0:1, 0:64], lrow[:1, :])
            em.copy(rb64[:, :], pb[0:64, :nG])
            em.tt(attev[:, :], psO[0:64, :], rb64[:, :], OP.mult)
            em.dma(attT_d[h * 64:(h + 1) * 64, gsl], attev[:, :])

    load_wbig_proj = True
    for seg in range(2):
        ssl_d = slice(seg * 512, (seg + 1) * 512)
        for c in range(DC):
            em.dma(wld[:, c, :], wproj_r[c][:, ssl_d])
        em.copy(wbig_b[:, :, seg * 512:(seg + 1) * 512], wld[:, :, :], eng="gpsimd")
    attld = em.sb([128, nT], BF16)
    yout = em.sb([128, nT], F32)
    xres = em.sb([128, nT], F32)
    for tg in range(TPC // nT):
        gsl = slice(tg * nT, (tg + 1) * nT)
        for oc in range(DC):
            osl = slice(oc * 128, (oc + 1) * 128)
            for cc in range(DC):
                em.dma(attld[:, :], attT_d[cc * 128:(cc + 1) * 128, gsl])
                em.mm(pK[:, :], wbig_b[:, cc, osl], attld[:, :], start=(cc == 0), stop=(cc == DC - 1))
            em.ts(scr[:, :], pK[:, :], bp_sb[:, oc:oc + 1], OP.add)
            em.dma(xres[:, :], xs_r[oc, :, gsl])
            em.tt(yout[:, :], scr[:, :], xres[:, :], OP.add)
            em.dma(yT_r[oc, :, gsl], yout[:, :])
    return em.emit()


_progs = {}


def get_prog(name, *args):
    key = (name,) + args
    if key not in _progs:
        _progs[key] = {"ffn": build_ffn, "moe": build_moe, "bank": build_bank,
                       "lab_p1": build_lab_p1, "lab_p2": build_lab_p2,
                       "mla": build_mla, "mha": build_mha}[name](*args)
    return _progs[key]


def shard_xT(x_T):
    shards = []
    for core in range(NCORES):
        b = core // 4
        cols = [x_T[b][:, g * 128:(g + 1) * 128] for g in core_qtiles(core)]
        shards.append(np.ascontiguousarray(np.concatenate(cols, axis=1)))
    return shards


def unshard_xT(shards):
    out = np.zeros((B, D, S), np.float32)
    for core in range(NCORES):
        b = core // 4
        sh = np.asarray(shards[core])
        for j, g in enumerate(core_qtiles(core)):
            out[b][:, g * 128:(g + 1) * 128] = sh[:, j * 128:(j + 1) * 128]
    return out


def run_cores(nc, per_core, shared, out="yT"):
    in_maps = []
    for c in range(NCORES):
        m = dict(shared)
        m.update(per_core[c])
        in_maps.append(m)
    res = run_bass_kernel_spmd(nc, in_maps, core_ids=list(range(NCORES)))
    return [np.asarray(res.results[c][out]) for c in range(NCORES)]


def rope_R():
    R = np.eye(D, dtype=np.float64)
    for h in range(NH):
        for j in range(8):
            th = h * (10000.0 ** (-(2.0 * j) / HD))
            c, s = np.cos(th), np.sin(th)
            r0 = h * HD + 2 * j
            r1 = r0 + 1
            R[r0, r0] = c
            R[r0, r1] = -s
            R[r1, r0] = s
            R[r1, r1] = c
    return R


def mha_masks(core):
    import ml_dtypes
    qt = core_qtiles(core)
    M = np.zeros((128, 64, 256), np.float32)
    for g in range(8):
        NK = mha_group_nk(g)
        tglob = [qt[2 * g], qt[2 * g + 1]]
        qpos = np.concatenate([np.arange(t * 128, (t + 1) * 128) for t in tglob])
        for j in range(8):
            st = NK - 8 + j
            kpos = np.arange(st * 128, (st + 1) * 128)
            allowed = kpos[:, None] <= qpos[None, :]
            M[:, g * 8 + j, :] = np.where(allowed, 0.0, np.float32(-1e30))
    return M.astype(ml_dtypes.bfloat16)


def kernel(**inputs):
    g = {k: np.asarray(v) for k, v in inputs.items()}
    x = g["L0_attn_a_x"]
    xT = np.ascontiguousarray(x.transpose(0, 2, 1)).astype(np.float32, copy=False)
    shards = shard_xT(xT)

    def ctx(shards):
        xc = unshard_xT(shards)
        return [np.ascontiguousarray(xc[c // 4]) for c in range(NCORES)]

    C = np.ascontiguousarray

    R = rope_R()
    nc = get_prog("mla")
    shared = dict(
        wq=g["L0_attn_a_wq"], wkb=g["L0_attn_a_wk_b"],
        R8T=C((R / 8.0).T.astype(np.float32)), RT=C(R.T.astype(np.float32)),
        wkaT=C(g["L0_attn_a_wk_a"].T), wvaT=C(g["L0_attn_a_wv_a"].T),
        wvbT=C(g["L0_attn_a_wv_b"].T), woT=C(g["L0_attn_a_wo"].T))
    xcs = ctx(shards)
    shards = run_cores(nc, [dict(xc=xcs[c], xs=shards[c]) for c in range(NCORES)], shared)

    nc = get_prog("ffn")
    sh_ffn = dict(wgT=C(g["L0_ffn_w_gate"].T), wuT=C(g["L0_ffn_w_up"].T),
                  wdT=C(g["L0_ffn_w_down"].T))
    shards = run_cores(nc, [dict(xs=shards[c]) for c in range(NCORES)], sh_ffn)

    nc = get_prog("mha")
    bq = g["L0_attn_b_b_qkv"]
    shared = dict(wqkv=g["L0_attn_b_w_qkv"],
                  bqkvP=C(bq.reshape(24, 128).T),
                  bqkvF=C(bq.reshape(1, 3 * D)),
                  wproj=g["L0_attn_b_w_proj"],
                  bpP=C(g["L0_attn_b_b_proj"].reshape(DC, 128).T))
    xcs = ctx(shards)
    shards = run_cores(nc, [dict(xc=xcs[c], xs=shards[c], masks=mha_masks(c))
                            for c in range(NCORES)], shared)

    nc = get_prog("ffn")
    shards = run_cores(nc, [dict(xs=shards[c]) for c in range(NCORES)], sh_ffn)

    nc = get_prog("bank")
    shared = dict(kT=C(g["L1_attn_a_k"].T), v=g["L1_attn_a_v"])
    shards = run_cores(nc, [dict(xs=shards[c]) for c in range(NCORES)], shared)

    nc = get_prog("moe")
    sh_moe = dict(rw=g["L1_ffn_router_w"], win=g["L1_ffn_w_in"], wout=g["L1_ffn_w_out"],
                  binP=C(g["L1_ffn_b_in"].T.reshape(DC, 128, NE).transpose(1, 0, 2)),
                  boutT=C(g["L1_ffn_b_out"].reshape(1, NE, D)),
                  ident=np.eye(128, dtype=np.float32))
    shards = run_cores(nc, [dict(xs=shards[c]) for c in range(NCORES)], sh_moe)

    xtop = shards

    def perpart(vec):
        return C(vec.reshape(DC, 128).T)

    for i in range(8):
        pvar = "first" if i == 0 else "rest"
        nc1 = get_prog("lab_p1", pvar)
        sh1 = dict(
            wkT=C(g["L1_attn_b_Wk"][i].transpose(1, 0, 2).reshape(D, D)),
            wvT=C(g["L1_attn_b_Wv"][i].transpose(1, 0, 2).reshape(D, D)),
            bkP=perpart(g["L1_attn_b_bk"][i].reshape(D)),
            bvP=perpart(g["L1_attn_b_bv"][i].reshape(D)))
        parts = run_cores(nc1, [dict(xs=shards[c]) for c in range(NCORES)],
                          sh1, out="pacc")
        variant = "first" if i == 0 else ("last" if i == 7 else "mid")
        nc = get_prog("lab_p2", variant)
        shared = dict(
            wqT=C(g["L1_attn_b_Wq"][i].transpose(1, 0, 2).reshape(D, D)),
            wo=g["L1_attn_b_Wo"][i],
            w1=g["L1_attn_b_W1"][i], w2=g["L1_attn_b_W2"][i],
            bqP=perpart(g["L1_attn_b_bq"][i].reshape(D)),
            boP=perpart(g["L1_attn_b_bo"][i]),
            b1P=C(g["L1_attn_b_b1"][i].reshape(NFT, 128).T),
            b2P=perpart(g["L1_attn_b_b2"][i]),
            ln1sP=perpart(g["L1_attn_b_ln1_s"][i]),
            ln1bP=perpart(g["L1_attn_b_ln1_b"][i]),
            ln2sP=perpart(g["L1_attn_b_ln2_s"][i]),
            ln2bP=perpart(g["L1_attn_b_ln2_b"][i]))
        per_core = []
        for c in range(NCORES):
            grp = (c // 4) * 4
            pstack = C(np.stack([parts[j] for j in range(grp, grp + 4)]))
            m = dict(xs=shards[c], pacc4=pstack)
            if variant == "last":
                m["xtop"] = xtop[c]
            per_core.append(m)
        shards = run_cores(nc, per_core, shared)

    nc = get_prog("moe")
    shards = run_cores(nc, [dict(xs=shards[c]) for c in range(NCORES)], sh_moe)

    out = unshard_xT(shards)
    return np.ascontiguousarray(out.transpose(0, 2, 1))
